# version 02
# baseline (speedup 1.0000x reference)
import math
import numpy as np
import concourse.bass as bass
import concourse.mybir as mybir
from concourse.alu_op_type import AluOpType as OP
from concourse.bass_utils import run_bass_kernel_spmd

F32 = mybir.dt.float32
BF16 = mybir.dt.bfloat16
ACT = mybir.ActivationFunctionType
AX = mybir.AxisListType

E = 2048
T = 2048
OT = 1024
NCH = 16
PAIRS = 16
LAMBDA_INIT = 0.8 - 0.6 * math.exp(-0.3 * 12)
SCALING = 64.0 ** -0.5
I = 1408
ICH = 11
NE = 8

OWN_EVEN = [0, 3, 4, 7, 8, 11, 12, 15]
OWN_ODD = [1, 2, 5, 6, 9, 10, 13, 14]


def perm_for(core):
    own = OWN_EVEN if core % 2 == 0 else OWN_ODD
    other = [c for c in range(16) if c not in own]
    return own + other, own, other


def s_chunks(slot):
    own = list(range(0, 4 * slot + 4))
    other = list(range(8, 8 + 4 * slot + 4))
    return own, other


class Builder:
    def __init__(self, debug=(), stop_after=None):
        self.stop_after = stop_after
        self.debug = set(debug)
        nc = bass.Bass("TRN2", target_bir_lowering=False)
        self.nc = nc
        dr = lambda n, s, d, k="ExternalInput": nc.dram_tensor(n, s, d, kind=k).ap()
        self.xfm = dr("xfm", [E, T], F32)
        self.wq = dr("wq", [E, E], F32)
        self.wk = dr("wk", [E, E], F32)
        self.wv = dr("wv", [E, E], F32)
        self.wo = dr("wo", [E, E], F32)
        self.ws1 = dr("ws1", [E, I], F32)
        self.ws2 = dr("ws2", [I, E], F32)
        self.w1 = dr("w1", [NE, E, I], F32)
        self.w2 = dr("w2", [NE, I, E], F32)
        self.wrt = dr("wrt", [128, 128], F32)
        self.lq = dr("lq", [1, 128], F32)
        self.lk = dr("lk", [1, 128], F32)
        self.cones128 = dr("cones128", [128, 1], F32)
        self.cones1x128 = dr("cones1x128", [1, 128], F32)
        self.cones1x128f = dr("cones1x128f", [1, 128], F32)
        self.crms1x128 = dr("crms1x128", [1, 128], F32)
        self.ctri = dr("ctri", [128, 128], F32)
        self.conehot = dr("conehot", [8, 8 * 128], F32)
        self.cident = dr("cident", [128, 128], F32)
        self.cmrep = dr("cmrep", [2, 128, 512], F32)
        self.out = dr("out", [E, OT], F32, "ExternalOutput")
        self.x2scr = dr("x2scr", [E, OT], F32, "Internal")

        self.sbX = nc.alloc_sbuf_tensor("sbX", [128, NCH * T], BF16)
        self.sbV = nc.alloc_sbuf_tensor("sbV", [128, NCH * T], BF16)
        self.sbW = nc.alloc_sbuf_tensor("sbW", [128, NCH * T], BF16)
        self.sbS = nc.alloc_sbuf_tensor("sbS", [128, 6272], BF16)
        self.sbF = nc.alloc_sbuf_tensor("sbF", [128, 64], F32)
        self.ones128 = nc.alloc_sbuf_tensor("ones128", [128, 1], BF16)
        self.ones1x128 = nc.alloc_sbuf_tensor("ones1x128", [1, 128], BF16)
        self.ones1x128f = nc.alloc_sbuf_tensor("ones1x128f", [1, 128], F32)
        self.rms1x128 = nc.alloc_sbuf_tensor("rms1x128", [1, 128], BF16)

        self.psA = nc.alloc_psum_tensor("psA", [128, 1024], F32)
        self.psB = nc.alloc_psum_tensor("psB", [128, 1024], F32)
        self.psU1 = nc.alloc_psum_tensor("psU1", [128, 512], F32)
        self.psU2 = nc.alloc_psum_tensor("psU2", [128, 512], F32)
        self.psL1 = nc.alloc_psum_tensor("psL1", [128, 512], F32)
        self.psL2 = nc.alloc_psum_tensor("psL2", [128, 512], F32)

        self.sem = {}
        self.cnt = {}
        for name in ("g", "s", "c", "t", "v", "a"):
            self.sem[name] = nc.alloc_semaphore("sem_" + name)
            self.cnt[name] = 0
        self.taps = {}

    def inc(self, instr, sem, n=1):
        instr.then_inc(self.sem[sem], n)
        self.cnt[sem] += n
        return self.cnt[sem]

    def snap(self):
        return dict(self.cnt)

    def check(self, c0, **expected):
        for k, d in expected.items():
            got = self.cnt[k] - c0[k]
            assert got == d, f"emission count mismatch sem {k}: got {got} expected {d}"

    def barrier(self, fence=True):
        b = self
        if fence:
            with self.nc.Block() as blk:
                @blk.sync
                def _(s):
                    s.wait_ge(b.sem["s"], b.cnt["s"])
                    s.wait_ge(b.sem["c"], b.cnt["c"])
                @blk.gpsimd
                def _(g):
                    g.wait_ge(b.sem["g"], b.cnt["g"])
        self.nc.all_engine_barrier()

    def want(self, name):
        return name in self.debug

    def tap(self, name, shape, dtype=F32):
        ap = self.nc.dram_tensor("tap_" + name, shape, dtype, kind="ExternalOutput").ap()
        self.taps[name] = ap
        return ap

    def store_tap(self, name, sb_ap, shape, dtype=F32):
        b = self
        nc = self.nc
        ap = self.tap(name, shape, dtype)
        with nc.Block() as blk:
            @blk.sync
            def _(s):
                n = b.inc(s.dma_start(ap, sb_ap), "s", 16)
                s.wait_ge(b.sem["s"], n)
        self.barrier()

    def build(self):
        steps = [
            ("consts", self.phase_consts, None),
            ("ln1", self.phase_ln1, ("xhat", lambda: self.store_tap("xhat", self.sbX[:], [128, NCH * T], BF16))),
            ("vproj", self.phase_vproj, ("V", lambda: self.store_tap("V", self.sbV[:], [128, NCH * T], BF16))),
            ("kproj", self.phase_kproj, ("K", lambda: self.store_tap("K", self.sbW[:], [128, NCH * T], BF16))),
            ("qproj", self.phase_qproj, ("Q", lambda: self.store_tap("Q", self.sbX[:], [128, NCH * T], BF16))),
            ("bounds", self.phase_bounds, ("bias", lambda: self.store_tap("bias", self.sbF[:, 0:64], [128, 64], F32))),
            ("attn", self.phase_attn, ("O", lambda: self.store_tap("O", self.sbX[:], [128, NCH * T], BF16))),
            ("outproj", self.phase_outproj, ("x2", lambda: self.store_tap("x2", self.x2scr, [E, OT], F32))),
            ("ln2", self.phase_ln2, ("xhat2", lambda: (self.store_tap("xhat2",
                self.sbW[:].rearrange("p (c t) -> p c t", c=NCH)[:, :, 0:1024], [128, NCH, OT], BF16),
                self.store_tap("xlo2",
                self.sbW[:].rearrange("p (c t) -> p c t", c=NCH)[:, :, 1024:2048], [128, NCH, OT], BF16)))),
            ("router", self.phase_router, ("comb", lambda: (self.store_tap("comb", self.sbS[0:8, 2432:2432 + 1024], [8, 1024], BF16),
                                                             self.store_tap("zrouter", self.sbS[:, 4608:4736], [128, 128], BF16)))),
            ("ffn", self.phase_ffn, None),
            ("final", self.phase_final, None),
        ]
        for name, fn, tapinfo in steps:
            fn()
            if tapinfo is not None and self.want(tapinfo[0]):
                tapinfo[1]()
            if self.stop_after == name:
                break
        return self.nc

    def phase_consts(self):
        b = self
        nc = self.nc
        with nc.Block() as blk:
            @blk.gpsimd
            def _(g):
                b.inc(g.dma_start(b.sbX[:].rearrange("p (c t) -> p c t", c=NCH),
                                  b.xfm.rearrange("(c p) t -> p c t", p=128)), "g", 16)
                b.inc(g.dma_start(b.ones128[:], b.cones128), "g", 16)
                b.inc(g.dma_start(b.ones1x128[:], b.cones1x128), "g", 16)
                b.inc(g.dma_start(b.ones1x128f[:], b.cones1x128f), "g", 16)
                b.inc(g.dma_start(b.rms1x128[:], b.crms1x128), "g", 16)
                b.g_ln1 = b.cnt["g"]
                b.inc(g.dma_start(b.sbW[:].rearrange("p (c n) -> p c n", c=NCH),
                                  b.wv.rearrange("(c p) n -> p c n", p=128)), "g", 16)

            @blk.sync
            def _(s):
                lqk = b.sbS[0:1, 0:512].bitcast(F32)
                b.inc(s.dma_start(lqk[:, 0:128], b.lq), "s", 16)
                b.inc(s.dma_start(lqk[:, 128:256], b.lk), "s", 16)

            @blk.vector
            def _(v):
                v.wait_ge(b.sem["s"], b.cnt["s"])
                v.memset(b.sbF[0:1, 5:6], 1e-5)
                lqk = b.sbS[0:1, 0:512].bitcast(F32)
                v.tensor_tensor(lqk[:, 0:128], lqk[:, 0:128], lqk[:, 128:256], OP.mult)
                v.reduce_sum(b.sbF[0:1, 0:1], lqk[:, 0:64], axis=AX.X)
                b.inc(v.reduce_sum(b.sbF[0:1, 1:2], lqk[:, 64:128], axis=AX.X), "v")

            @blk.vector
            def _(v):
                b.inc(v.tensor_scalar(b.sbS[0:1, 600:602], b.sbF[0:1, 0:2], 1.0, None, OP.mult), "v")

            @blk.tensor
            def _(t):
                t.wait_ge(b.sem["v"], b.cnt["v"])
                t.wait_ge(b.sem["g"], b.cnt["g"])
                b.inc(t.matmul(b.psU1[:, 0:2], b.ones1x128[:], b.sbS[0:1, 600:602], start=True, stop=True), "t")

            @blk.vector
            def _(v):
                v.wait_ge(b.sem["t"], b.cnt["t"])
                b.inc(v.tensor_copy(b.sbF[:, 2:4], b.psU1[:, 0:2]), "v")

            @blk.scalar
            def _(sc):
                sc.wait_ge(b.sem["v"], b.cnt["v"])
                b.inc(sc.activation(b.sbF[:, 2:4], b.sbF[:, 2:4], ACT.Exp), "a")

            @blk.vector
            def _(v):
                v.wait_ge(b.sem["a"], b.cnt["a"])
                b.inc(v.tensor_scalar(b.sbF[:, 4:5], b.sbF[:, 2:3], b.sbF[:, 3:4],
                                      float(LAMBDA_INIT), OP.subtract, OP.add), "v")
        self.barrier(fence=False)

    def phase_ln1(self):
        b = self
        nc = self.nc
        X = b.sbX[:].rearrange("p (c t) -> p c t", c=NCH)
        sx = [b.psA[0:1, 0:512], b.psA[0:1, 512:1024], b.psB[0:1, 0:512], b.psB[0:1, 512:1024]]
        sxx = [b.psU1[0:1, :], b.psU2[0:1, :], b.psL1[0:1, :], b.psL2[0:1, :]]
        TMP = [b.sbS[:, 0:2048], b.sbS[:, 2048:4096]]
        c0 = self.snap()
        with nc.Block() as blk:
            @blk.vector
            def _(v):
                v.wait_ge(b.sem["g"], b.g_ln1)
                for c in range(NCH):
                    if c >= 2:
                        v.wait_ge(b.sem["t"], c0["t"] + (c - 2) + 1)
                    b.inc(v.tensor_tensor(TMP[c % 2], X[:, c, :], X[:, c, :], OP.mult), "v")

            @blk.tensor
            def _(t):
                t.wait_ge(b.sem["g"], b.g_ln1)
                for c in range(NCH):
                    for j in range(4):
                        t.matmul(sx[j], b.ones128[:], X[:, c, 512 * j:512 * (j + 1)],
                                 start=(c == 0), stop=(c == NCH - 1))
                    t.wait_ge(b.sem["v"], c0["v"] + c + 1)
                    for j in range(4):
                        ins = t.matmul(sxx[j], b.ones128[:], TMP[c % 2][:, 512 * j:512 * (j + 1)],
                                       start=(c == 0), stop=(c == NCH - 1))
                    b.inc(ins, "t")

            @blk.vector
            def _(v):
                v.wait_ge(b.sem["t"], c0["t"] + NCH)
                m = b.sbS[0:1, 4096:6144]
                var = b.sbS[0:1, 0:2048]
                for j in range(4):
                    v.tensor_scalar(m[:, 512 * j:512 * (j + 1)], sx[j], 1.0 / E, None, OP.mult)
                    v.tensor_scalar(var[:, 512 * j:512 * (j + 1)], sxx[j], 1.0 / E, None, OP.mult)
                msq = b.sbS[0:1, 2048:4096]
                v.tensor_tensor(msq, m, m, OP.mult)
                b.inc(v.tensor_tensor(var, var, msq, OP.subtract), "v")

            @blk.scalar
            def _(sc):
                var = b.sbS[0:1, 0:2048]
                sc.wait_ge(b.sem["v"], c0["v"] + NCH + 1)
                sc.activation(var, var, ACT.Ln, bias=b.sbF[0:1, 5:6], scale=1.0)
                b.inc(sc.activation(var, var, ACT.Exp, scale=-0.5), "a")

            @blk.tensor
            def _(t):
                t.wait_ge(b.sem["a"], c0["a"] + 1)
                m = b.sbS[0:1, 4096:6144]
                r = b.sbS[0:1, 0:2048]
                mout = [b.psA[:, 0:512], b.psA[:, 512:1024], b.psB[:, 0:512], b.psB[:, 512:1024]]
                rout = [b.psU1[:], b.psU2[:], b.psL1[:], b.psL2[:]]
                for j in range(4):
                    t.matmul(mout[j], b.ones1x128[:], m[:, 512 * j:512 * (j + 1)], start=True, stop=True)
                for j in range(4):
                    ins = t.matmul(rout[j], b.ones1x128[:], r[:, 512 * j:512 * (j + 1)], start=True, stop=True)
                b.inc(ins, "t")

            @blk.vector
            def _(v):
                v.wait_ge(b.sem["t"], c0["t"] + NCH + 1)
                mb = b.sbS[:, 4096:6144]
                rb = b.sbS[:, 2048:4096]
                v.tensor_copy(mb[:, 0:512], b.psA[:, 0:512])
                v.tensor_copy(mb[:, 512:1024], b.psA[:, 512:1024])
                v.tensor_copy(mb[:, 1024:1536], b.psB[:, 0:512])
                v.tensor_copy(mb[:, 1536:2048], b.psB[:, 512:1024])
                v.tensor_copy(rb[:, 0:512], b.psU1[:])
                v.tensor_copy(rb[:, 512:1024], b.psU2[:])
                v.tensor_copy(rb[:, 1024:1536], b.psL1[:])
                v.tensor_copy(rb[:, 1536:2048], b.psL2[:])
                for c in range(NCH):
                    v.tensor_tensor(X[:, c, :], X[:, c, :], mb, OP.subtract)
                    ins = v.tensor_tensor(X[:, c, :], X[:, c, :], rb, OP.mult)
                b.inc(ins, "v")
        self.check(c0, v=NCH + 2, t=NCH + 1, a=1)
        self.barrier()

    def phase_vproj(self):
        b = self
        nc = self.nc
        X = b.sbX[:].rearrange("p (c t) -> p c t", c=NCH)
        W = b.sbW[:].rearrange("p (c n) -> p c n", c=NCH)
        V = b.sbV[:].rearrange("p (c n) -> p c n", c=NCH)
        ps = [b.psU1, b.psU2, b.psL1, b.psL2]
        c0 = self.snap()
        with nc.Block() as blk:
            @blk.tensor
            def _(t):
                for n in range(64):
                    dvg, kc = divmod(n, 16)
                    if n >= 4:
                        t.wait_ge(b.sem["v"], c0["v"] + (n - 4) + 1)
                    for f in range(NCH):
                        ins = t.matmul(ps[n % 4][:], X[:, f, kc * 128:(kc + 1) * 128],
                                       W[:, f, dvg * 512:(dvg + 1) * 512],
                                       start=(f == 0), stop=(f == NCH - 1))
                    b.inc(ins, "t")

            @blk.vector
            def _(v):
                for n in range(64):
                    dvg, kc = divmod(n, 16)
                    v.wait_ge(b.sem["t"], c0["t"] + n + 1)
                    b.inc(v.tensor_copy(V[:, kc, dvg * 512:(dvg + 1) * 512], ps[n % 4][:]), "v")
        self.check(c0, t=64, v=64)
        self.barrier()

    def _proj_stream(self, wdram, nout, ngrp, out_ap_fn, scale=None):
        b = self
        nc = self.nc
        X = b.sbX[:].rearrange("p (c t) -> p c t", c=NCH)
        ring = [b.sbS[:, 0:2048].rearrange("p (c n) -> p c n", c=NCH),
                b.sbS[:, 2048:4096].rearrange("p (c n) -> p c n", c=NCH)]
        ps = [b.psU1, b.psU2, b.psL1, b.psL2]
        c0 = self.snap()
        with nc.Block() as blk:
            @blk.gpsimd
            def _(g):
                for d in range(nout):
                    if d >= 2:
                        g.wait_ge(b.sem["t"], c0["t"] + (d - 1) * ngrp)
                    b.inc(g.dma_start(ring[d % 2][:], wdram(d)), "g", 16)

            @blk.tensor
            def _(t):
                for d in range(nout):
                    t.wait_ge(b.sem["g"], c0["g"] + (d + 1) * 16)
                    for tg in range(ngrp):
                        n = d * ngrp + tg
                        if n >= 4:
                            t.wait_ge(b.sem["v"], c0["v"] + (n - 4) + 1)
                        for f in range(NCH):
                            ins = t.matmul(ps[n % 4][:], ring[d % 2][:, f, :],
                                           X[:, f, tg * 512:(tg + 1) * 512],
                                           start=(f == 0), stop=(f == NCH - 1))
                        b.inc(ins, "t")

            @blk.vector
            def _(v):
                for d in range(nout):
                    for tg in range(ngrp):
                        n = d * ngrp + tg
                        v.wait_ge(b.sem["t"], c0["t"] + n + 1)
                        if scale is None:
                            ins = v.tensor_copy(out_ap_fn(d, tg), ps[n % 4][:])
                        else:
                            ins = v.tensor_scalar(out_ap_fn(d, tg), ps[n % 4][:], scale, None, OP.mult)
                        b.inc(ins, "v")
        self.check(c0, t=nout * ngrp, v=nout * ngrp, g=16 * nout)
        self.barrier()

    def phase_kproj(self):
        b = self
        K = b.sbW[:].rearrange("p (c t) -> p c t", c=NCH)
        self._proj_stream(
            lambda d: b.wk.rearrange("(c p) n -> p c n", p=128)[:, :, d * 128:(d + 1) * 128],
            16, 4, lambda d, tg: K[:, d, tg * 512:(tg + 1) * 512])

    def phase_qproj(self):
        b = self
        X = b.sbX[:].rearrange("p (c t) -> p c t", c=NCH)
        self._proj_stream(
            lambda d: b.wq.rearrange("(c p) n -> p c n", p=128)[:, :, d * 128:(d + 1) * 128],
            16, 2, lambda d, tg: X[:, d, 1024 + tg * 512:1024 + (tg + 1) * 512],
            scale=float(SCALING))

    def phase_bounds(self):
        b = self
        nc = self.nc
        X = b.sbX[:].rearrange("p (c t) -> p c t", c=NCH)
        K = b.sbW[:].rearrange("p (c t) -> p c t", c=NCH)
        Q1 = b.sbS[:, 0:1024]
        K1 = b.sbS[:, 1024:3072]
        c0 = self.snap()
        with nc.Block() as blk:
            @blk.gpsimd
            def _(g):
                b.inc(g.dma_start(b.sbS[:, 5120:5248], b.ctri), "g", 16)
                b.inc(g.dma_start(b.sbS[:, 5248:6272].rearrange("p (s n) -> p s n", s=2),
                                  b.cmrep.rearrange("s p n -> p s n")), "g", 16)

            @blk.vector
            def _(v):
                for pair in range(PAIRS):
                    if pair >= 1:
                        v.wait_ge(b.sem["t"], c0["t"] + pair)
                    v.tensor_tensor(Q1, X[:, pair, 1024:2048], X[:, pair, 1024:2048], OP.mult)
                    v.tensor_tensor(K1[:, 0:1024], K[:, pair, 0:1024], K[:, pair, 0:1024], OP.mult)
                    b.inc(v.tensor_tensor(K1[:, 1024:2048], K[:, pair, 1024:2048], K[:, pair, 1024:2048], OP.mult), "v")
                    v.wait_ge(b.sem["t"], c0["t"] + pair + 1)
                    SF = b.sbS[0:1, 3072:5120].bitcast(F32)
                    v.reduce_max(SF[:, 0 + pair:1 + pair], b.psB[0:1, 0:1024], axis=AX.X)
                    v.reduce_max(SF[:, 16 + pair:17 + pair], b.psU1[0:1, :], axis=AX.X)
                    v.reduce_max(SF[:, 32 + pair:33 + pair], b.psU2[0:1, :], axis=AX.X)
                    b.inc(v.reduce_max(SF[:, 48 + pair:49 + pair], b.psA[0:1, 0:1024], axis=AX.X), "v")

            @blk.tensor
            def _(t):
                for pair in range(PAIRS):
                    t.wait_ge(b.sem["v"], c0["v"] + pair * 2 + 1)
                    t.matmul(b.psA[0:1, 0:512], b.ones128[:], Q1[:, 0:512], start=True, stop=True)
                    t.matmul(b.psA[0:1, 512:1024], b.ones128[:], Q1[:, 512:1024], start=True, stop=True)
                    t.matmul(b.psB[0:1, 0:512], b.ones128[:], K1[:, 0:512], start=True, stop=True)
                    t.matmul(b.psB[0:1, 512:1024], b.ones128[:], K1[:, 512:1024], start=True, stop=True)
                    t.matmul(b.psU1[0:1, :], b.ones128[:], K1[:, 1024:1536], start=True, stop=True)
                    ins = t.matmul(b.psU2[0:1, :], b.ones128[:], K1[:, 1536:2048], start=True, stop=True)
                    b.inc(ins, "t")

            @blk.vector
            def _(v):
                SF = b.sbS[0:1, 3072:5120].bitcast(F32)
                v.wait_ge(b.sem["v"], c0["v"] + 2 * PAIRS)
                v.tensor_tensor(SF[:, 0:16], SF[:, 0:16], SF[:, 16:32], OP.max)
                v.tensor_tensor(SF[:, 0:16], SF[:, 0:16], SF[:, 32:48], OP.max)
                b.inc(v.tensor_tensor(b.sbF[0:1, 32:48], SF[:, 0:16], SF[:, 48:64], OP.mult), "v")
        self.check(c0, v=2 * PAIRS + 1, t=PAIRS, g=32)
        self.barrier()
        c1 = self.snap()
        with nc.Block() as blk:
            @blk.scalar
            def _(sc):
                sc.activation(b.sbF[0:1, 32:48], b.sbF[0:1, 32:48], ACT.Ln)
                b.inc(sc.activation(b.sbF[0:1, 32:48], b.sbF[0:1, 32:48], ACT.Exp, scale=0.5), "a")

            @blk.vector
            def _(v):
                v.wait_ge(b.sem["a"], c1["a"] + 1)
                b.inc(v.tensor_scalar(b.sbS[0:1, 4480:4496], b.sbF[0:1, 32:48], -1.0, None, OP.mult), "v")

            @blk.tensor
            def _(t):
                t.wait_ge(b.sem["v"], c1["v"] + 1)
                b.inc(t.matmul(b.psL1[:, 0:16], b.ones1x128[:], b.sbS[0:1, 4480:4496], start=True, stop=True), "t")

            @blk.vector
            def _(v):
                v.wait_ge(b.sem["t"], c1["t"] + 1)
                b.inc(v.tensor_copy(b.sbF[:, 8:24], b.psL1[:, 0:16]), "v")
        self.barrier()

    def phase_attn(self):
        b = self
        for pair in range(PAIRS):
            for slot in range(2):
                self.attn_iter(pair, slot)
        self.barrier()

    def attn_iter(self, pair, slot):
        b = self
        nc = self.nc
        dbgP = (pair == 0 and slot == 0 and self.want("P00"))
        own, other = s_chunks(slot)
        chunks = own + other
        n_ch = len(chunks)
        q0 = slot * 512
        X = b.sbX[:].rearrange("p (c t) -> p c t", c=NCH)
        Khi = b.sbW[64:128, :].rearrange("p (c t) -> p c t", c=NCH)
        Klo = b.sbW[0:64, :].rearrange("p (c t) -> p c t", c=NCH)
        V = b.sbV[:].rearrange("p (c n) -> p c n", c=NCH)
        Qlo = b.sbX[0:64, :].rearrange("p (c t) -> p c t", c=NCH)
        Qhi = b.sbX[64:128, :].rearrange("p (c t) -> p c t", c=NCH)
        P12 = [b.sbS[:, 0:1024], b.sbS[:, 1024:2048]]
        RB1 = b.sbS[:, 2048:2560]
        RB2 = b.sbS[:, 2560:3072]
        OSQ = b.sbS[:, 0:512]
        REC = b.sbS[0:1, 4096:5120]
        RSTD = b.sbS[0:1, 2048:2560]
        TRI = b.sbS[:, 5120:5248]
        BIAS = b.sbF[:, 8:24]
        lam_ap = b.sbF[0:1, 4:5]
        SB = [b.psA, b.psB]
        Oap = X[:, pair, q0:q0 + 512]

        seq = [("S", 0), ("S", 1)]
        for ci in range(n_ch):
            if ci + 2 < n_ch:
                seq.append(("S", ci + 2))
            seq.append(("L", ci))
        t_pos = {}
        for idx, item in enumerate(seq):
            t_pos[item] = idx + 1
        n_t_loop = len(seq)
        c0 = self.snap()
        if dbgP:
            tapP = self.tap("P00", [8, 128, 1024], BF16)
            tapL = self.tap("L00", [1, 1024], F32)
            tapRB = self.tap("RB00", [128, 1024], BF16)

        with nc.Block() as blk:
            if dbgP:
                @blk.sync
                def _(s):
                    for ci in range(n_ch):
                        s.wait_ge(b.sem["v"], c0["v"] + ci + 1)
                        b.inc(s.dma_start(tapP[ci], P12[ci % 2]), "s", 16)

            @blk.tensor
            def _(t):
                for item in seq:
                    kind, ci = item
                    kc = chunks[ci]
                    if kind == "S" and dbgP:
                        if ci >= 2:
                            t.wait_ge(b.sem["s"], c0["s"] + (ci - 2 + 1) * 16)
                    if kind == "S":
                        S = SB[ci % 2]
                        if ci >= 2:
                            t.wait_ge(b.sem["a"], c0["a"] + (ci - 2) + 1)
                        t.matmul(S[:, 0:512], Klo[:, pair, kc * 128:(kc + 1) * 128],
                                 Qlo[:, pair, 1024 + q0:1024 + q0 + 512],
                                 start=True, stop=True, tile_position=(0, 0))
                        ins = t.matmul(S[:, 512:1024], Khi[:, pair, kc * 128:(kc + 1) * 128],
                                       Qhi[:, pair, 1024 + q0:1024 + q0 + 512],
                                       start=True, stop=True, tile_position=(64, 0))
                        b.inc(ins, "t")
                    else:
                        t.wait_ge(b.sem["v"], c0["v"] + ci + 1)
                        P = P12[ci % 2]
                        first = ci == 0
                        last = ci == n_ch - 1
                        t.matmul(b.psL1[0:1, :], b.ones128[:], P[:, 0:512], start=first, stop=last)
                        t.matmul(b.psL2[0:1, :], b.ones128[:], P[:, 512:1024], start=first, stop=last)
                        t.matmul(b.psU1[:], V[:, kc, pair * 128:(pair + 1) * 128], P[:, 0:512],
                                 start=first, stop=last)
                        ins = t.matmul(b.psU2[:], V[:, kc, pair * 128:(pair + 1) * 128], P[:, 512:1024],
                                       start=first, stop=last)
                        b.inc(ins, "t")
                if dbgP:
                    t.wait_ge(b.sem["s"], c0["s"] + n_ch * 16)
                t.wait_ge(b.sem["a"], c0["a"] + n_ch + 1)
                t.matmul(b.psL1[:], b.ones1x128[:], REC[:, 0:512], start=True, stop=True)
                b.inc(t.matmul(b.psL2[:], b.ones1x128[:], REC[:, 512:1024], start=True, stop=True), "t")
                t.wait_ge(b.sem["v"], c0["v"] + n_ch + 3)
                b.inc(t.matmul(b.psU1[0:1, :], b.ones128[:], OSQ, start=True, stop=True), "t")
                t.wait_ge(b.sem["a"], c0["a"] + n_ch + 2)
                b.inc(t.matmul(b.psU2[:], b.rms1x128[:], RSTD[:, 0:512], start=True, stop=True), "t")

            @blk.scalar
            def _(sc):
                for ci in range(n_ch):
                    sc.wait_ge(b.sem["t"], c0["t"] + t_pos[("S", ci)])
                    if ci >= 2:
                        sc.wait_ge(b.sem["t"], c0["t"] + t_pos[("L", ci - 2)])
                    b.inc(sc.activation(P12[ci % 2], SB[ci % 2][:], ACT.Exp,
                                        bias=BIAS[:, pair:pair + 1], scale=1.0), "a")
                sc.wait_ge(b.sem["t"], c0["t"] + n_t_loop)
                sc.activation(b.psA[0:1, 0:512], b.psL1[0:1, :], ACT.Ln)
                sc.activation(b.psA[0:1, 512:1024], b.psL2[0:1, :], ACT.Ln)
                b.inc(sc.activation(REC, b.psA[0:1, 0:1024], ACT.Exp, scale=-1.0), "a")
                sc.wait_ge(b.sem["t"], c0["t"] + n_t_loop + 2)
                sc.activation(RSTD[:, 0:512], b.psU1[0:1, :], ACT.Ln, bias=b.sbF[0:1, 5:6], scale=1.0 / 128.0)
                b.inc(sc.activation(RSTD[:, 0:512], RSTD[:, 0:512], ACT.Exp, scale=-0.5), "a")

            @blk.vector
            def _(v):
                for ci in range(n_ch):
                    kc = chunks[ci]
                    P = P12[ci % 2]
                    v.wait_ge(b.sem["a"], c0["a"] + ci + 1)
                    if ci < len(own):
                        jq = kc - 4 * slot
                        if 0 <= jq <= 3:
                            for half in range(2):
                                base = half * 512
                                if jq > 0:
                                    v.memset(P[:, base:base + jq * 128], 0.0)
                                v.tensor_tensor(P[:, base + jq * 128:base + (jq + 1) * 128],
                                                P[:, base + jq * 128:base + (jq + 1) * 128],
                                                TRI, OP.mult)
                    else:
                        p = kc - 8
                        for half in range(2):
                            base = half * 512
                            v.scalar_tensor_tensor(P[:, base:base + 512], b.sbS[:, 5248 + slot * 512:5248 + (slot + 1) * 512],
                                                   float(p) + 0.5, P[:, base:base + 512],
                                                   OP.is_gt, OP.mult)
                    b.inc(v.tensor_copy(b.sbF[0:1, 47:48], b.sbF[0:1, 47:48]), "v")
                v.wait_ge(b.sem["t"], c0["t"] + n_t_loop + 1)
                v.tensor_copy(RB1, b.psL1[:])
                b.inc(v.tensor_scalar(RB2, b.psL2[:], b.sbF[:, 4:5], 1.0, OP.mult, OP.mult), "v")
                T1 = b.sbS[:, 0:512]
                T2 = b.sbS[:, 512:1024]
                v.tensor_tensor(T1, b.psU1[:], RB1, OP.mult)
                v.tensor_tensor(T2, b.psU2[:], RB2, OP.mult)
                b.inc(v.tensor_tensor(Oap, T1, T2, OP.subtract), "v")
                b.inc(v.tensor_tensor(OSQ, Oap, Oap, OP.mult), "v")
                v.wait_ge(b.sem["t"], c0["t"] + n_t_loop + 3)
                b.inc(v.tensor_tensor(Oap, Oap, b.psU2[:], OP.mult), "v")
        self.check(c0, t=n_t_loop + 3, a=n_ch + 2, v=n_ch + 4)

    def phase_outproj(self):
        b = self
        nc = self.nc
        X = b.sbX[:].rearrange("p (c t) -> p c t", c=NCH)
        ring = [b.sbS[:, 0:2048].rearrange("p (c n) -> p c n", c=NCH),
                b.sbS[:, 2048:4096].rearrange("p (c n) -> p c n", c=NCH)]
        sbv_f32 = b.sbV[:].bitcast(F32)
        xres = [sbv_f32[:, 0:1024], sbv_f32[:, 1024:2048]]
        x2strip = [sbv_f32[:, 2048:3072], sbv_f32[:, 3072:4096]]
        x2b = [b.sbV[:, 8192:9216], b.sbV[:, 9216:10240]]
        x2sq = [b.sbV[:, 10240:11264], b.sbV[:, 11264:12288]]
        sx = [b.psU1[0:1, :], b.psU2[0:1, :]]
        sxx = [b.psL1[0:1, :], b.psL2[0:1, :]]
        c0 = self.snap()
        with nc.Block() as blk:
            def y_done(j):
                return 2 if j == 0 else (4 if j == 1 else 3 * j)

            def st_done(j):
                return 3 * j + 7 if j <= 13 else (47 if j == 14 else 48)

            @blk.gpsimd
            def _(g):
                for fo in range(16):
                    if fo >= 2:
                        g.wait_ge(b.sem["t"], c0["t"] + y_done(fo - 2))
                    b.inc(g.dma_start(ring[fo % 2][:],
                                      b.wo.rearrange("(c p) n -> p c n", p=128)[:, :, fo * 128:(fo + 1) * 128]),
                          "g", 16)

            @blk.sync
            def _(s):
                for fo in range(16):
                    if fo >= 2:
                        s.wait_ge(b.sem["v"], c0["v"] + (fo - 2) * 3 + 1)
                    b.inc(s.dma_start(xres[fo % 2], b.xfm[fo * 128:(fo + 1) * 128, 0:1024]), "s", 16)

            def stats_group(t, j):
                t.wait_ge(b.sem["v"], c0["v"] + j * 3 + 3)
                t.matmul(sx[0], b.ones128[:], x2b[j % 2][:, 0:512], start=(j == 0), stop=(j == 15))
                t.matmul(sx[1], b.ones128[:], x2b[j % 2][:, 512:1024], start=(j == 0), stop=(j == 15))
                t.matmul(sxx[0], b.ones128[:], x2sq[j % 2][:, 0:512], start=(j == 0), stop=(j == 15))
                ins = t.matmul(sxx[1], b.ones128[:], x2sq[j % 2][:, 512:1024], start=(j == 0), stop=(j == 15))
                b.inc(ins, "t")

            @blk.tensor
            def _(t):
                for fo in range(16):
                    t.wait_ge(b.sem["g"], c0["g"] + (fo + 1) * 16)
                    if fo >= 2:
                        t.wait_ge(b.sem["v"], c0["v"] + (fo - 2) * 3 + 1)
                    PS = [b.psA, b.psB][fo % 2]
                    for th in range(2):
                        for pair in range(PAIRS):
                            ins = t.matmul(PS[:, th * 512:(th + 1) * 512], ring[fo % 2][:, pair, :],
                                           X[:, pair, th * 512:(th + 1) * 512],
                                           start=(pair == 0), stop=(pair == PAIRS - 1))
                        b.inc(ins, "t")
                    if fo >= 2:
                        stats_group(t, fo - 2)
                stats_group(t, 14)
                stats_group(t, 15)

            @blk.vector
            def _(v):
                for fo in range(16):
                    PS = [b.psA, b.psB][fo % 2]
                    v.wait_ge(b.sem["t"], c0["t"] + y_done(fo))
                    v.wait_ge(b.sem["s"], c0["s"] + (fo + 1) * 16)
                    if fo >= 2:
                        v.wait_ge(b.sem["c"], c0["c"] + (fo - 2 + 1) * 16)
                        v.wait_ge(b.sem["t"], c0["t"] + st_done(fo - 2))
                    v.tensor_tensor(x2strip[fo % 2], PS[:], xres[fo % 2], OP.add)
                    b.inc(v.tensor_copy(x2b[fo % 2], x2strip[fo % 2]), "v")
                    b.inc(v.tensor_copy(b.sbF[0:1, 47:48], b.sbF[0:1, 47:48]), "v")
                    b.inc(v.tensor_tensor(x2sq[fo % 2], x2b[fo % 2], x2b[fo % 2], OP.mult), "v")

            @blk.scalar
            def _(sc):
                for fo in range(16):
                    sc.wait_ge(b.sem["v"], c0["v"] + fo * 3 + 2)
                    b.inc(sc.dma_start(b.x2scr[fo * 128:(fo + 1) * 128, :], x2strip[fo % 2]), "c", 16)
                sc.wait_ge(b.sem["c"], c0["c"] + 256)
        self.check(c0, t=48, v=48, c=256, g=256, s=256)
        self.barrier()

    def phase_ln2(self):
        b = self
        nc = self.nc
        XH2 = b.sbW[:].rearrange("p (c t) -> p c t", c=NCH)[:, :, 0:1024]
        XL2 = b.sbW[:].rearrange("p (c t) -> p c t", c=NCH)[:, :, 1024:2048]
        sx = [b.psU1[0:1, :], b.psU2[0:1, :]]
        sxx = [b.psL1[0:1, :], b.psL2[0:1, :]]
        m = b.sbS[0:1, 4096:5120]
        var = b.sbS[0:1, 5120:6144]
        sbv_f32 = b.sbV[:].bitcast(F32)
        strips = [sbv_f32[:, 0:1024], sbv_f32[:, 1024:2048]]
        mbf = sbv_f32[:, 2048:3072]
        rbf = sbv_f32[:, 3072:4096]
        hif = b.sbS[:, 0:2048].bitcast(F32)
        c0 = self.snap()
        with nc.Block() as blk:
            @blk.vector
            def _(v):
                for j in range(2):
                    v.tensor_scalar(m[:, 512 * j:512 * (j + 1)], sx[j], 1.0 / E, None, OP.mult)
                    v.tensor_scalar(var[:, 512 * j:512 * (j + 1)], sxx[j], 1.0 / E, None, OP.mult)
                msq = b.sbS[0:1, 2048:3072]
                v.tensor_tensor(msq, m, m, OP.mult)
                b.inc(v.tensor_tensor(var, var, msq, OP.subtract), "v")

            @blk.scalar
            def _(sc):
                sc.wait_ge(b.sem["v"], c0["v"] + 1)
                sc.activation(var, var, ACT.Ln, bias=b.sbF[0:1, 5:6], scale=1.0)
                b.inc(sc.activation(var, var, ACT.Exp, scale=-0.5), "a")

            @blk.tensor
            def _(t):
                t.wait_ge(b.sem["a"], c0["a"] + 1)
                t.matmul(b.psA[:, 0:512], b.ones1x128[:], m[:, 0:512], start=True, stop=True)
                t.matmul(b.psA[:, 512:1024], b.ones1x128[:], m[:, 512:1024], start=True, stop=True)
                t.matmul(b.psB[:, 0:512], b.ones1x128[:], var[:, 0:512], start=True, stop=True)
                b.inc(t.matmul(b.psB[:, 512:1024], b.ones1x128[:], var[:, 512:1024], start=True, stop=True), "t")

            @blk.vector
            def _(v):
                v.wait_ge(b.sem["t"], c0["t"] + 1)
                v.tensor_copy(mbf, b.psA[:, 0:1024])
                b.inc(v.tensor_copy(rbf, b.psB[:, 0:1024]), "v")

            @blk.sync
            def _(s):
                for fo in range(16):
                    if fo >= 2:
                        s.wait_ge(b.sem["v"], c0["v"] + 2 + (fo - 2) + 1)
                    b.inc(s.dma_start(strips[fo % 2], b.x2scr[fo * 128:(fo + 1) * 128, :]), "s", 16)

            @blk.vector
            def _(v):
                for fo in range(16):
                    v.wait_ge(b.sem["s"], c0["s"] + (fo + 1) * 16)
                    st = strips[fo % 2]
                    v.tensor_tensor(st, st, mbf, OP.subtract)
                    v.tensor_tensor(st, st, rbf, OP.mult)
                    v.tensor_copy(XH2[:, fo, :], st)
                    v.tensor_copy(hif, XH2[:, fo, :])
                    b.inc(v.tensor_tensor(XL2[:, fo, :], st, hif, OP.subtract), "v")
        self.check(c0, v=18, t=1, a=1, s=256)
        self.barrier()

    def phase_router(self):
        b = self
        nc = self.nc
        XH2 = b.sbW[:].rearrange("p (c t) -> p c t", c=NCH)[:, :, 0:1024]
        XL2 = b.sbW[:].rearrange("p (c t) -> p c t", c=NCH)[:, :, 1024:2048]
        WRT = b.sbS[:, 2048:2048 + 16 * 8].rearrange("p (c n) -> p c n", c=NCH)
        WRTL = b.sbS[:, 2176:2176 + 16 * 8].rearrange("p (c n) -> p c n", c=NCH)
        WRTF = b.sbS[:, 5120:5376].bitcast(F32)
        WRTHF = b.sbS[:, 5376:5632].bitcast(F32)
        IDN = b.sbS[:, 4480:4608]
        CFM_OFF = 2432
        ZA = b.sbS[:, 4608:4736].bitcast(F32)
        GA = b.sbS[:, 4736:4864].bitcast(F32)
        Z2A = b.sbS[:, 4864:4992].bitcast(F32)
        CMA = b.sbS[:, 4992:5120].bitcast(F32)
        NMX = b.sbF[:, 8:16]
        V1 = b.sbF[:, 16:24]
        V2 = b.sbF[:, 24:32]
        S12 = b.sbF[:, 32:40]
        ZC = b.sbF[:, 44:45]
        OC = b.sbF[:, 45:46]
        ZTM = b.sbS[:, 2304:2368]
        crep = b.sbV[:, 12288:12288 + 8 * 1024].rearrange("p (e t) -> p e t", e=8)
        c0 = self.snap()

        def zs(tc):
            return ZA[:, tc * 8:(tc + 1) * 8]

        with nc.Block() as blk:
            @blk.gpsimd
            def _(g):
                b.inc(g.dma_start(WRTF[:, 0:128], b.wrt), "g", 16)
                b.inc(g.dma_start(b.sbS[0:8, 3456:4480], b.conehot), "g", 16)
                b.inc(g.dma_start(IDN, b.cident), "g", 16)

            @blk.tensor
            def _(t):
                t.wait_ge(b.sem["g"], c0["g"] + 48)
                t.wait_ge(b.sem["v"], c0["v"] + 1)
                for tc in range(8):
                    if tc >= 2:
                        t.wait_ge(b.sem["v"], c0["v"] + 1 + (tc - 2) + 1)
                    PL = [b.psL1, b.psL2][tc % 2]
                    for f in range(NCH):
                        t.matmul(PL[:, 0:8], XH2[:, f, tc * 128:(tc + 1) * 128],
                                 WRT[:, f, :], start=(f == 0), stop=False)
                        t.matmul(PL[:, 0:8], XL2[:, f, tc * 128:(tc + 1) * 128],
                                 WRT[:, f, :], start=False, stop=False)
                        ins = t.matmul(PL[:, 0:8], XH2[:, f, tc * 128:(tc + 1) * 128],
                                       WRTL[:, f, :], start=False,
                                       stop=(f == NCH - 1))
                    b.inc(ins, "t")
                t.wait_ge(b.sem["v"], c0["v"] + 11)
                for tc in range(8):
                    if tc >= 2:
                        t.wait_ge(b.sem["v"], c0["v"] + 11 + (tc - 2) + 1)
                    ins = t.matmul([b.psL1, b.psL2][tc % 2][0:8, 0:128],
                                   ZTM[:, tc * 8:(tc + 1) * 8], IDN, start=True, stop=True)
                    b.inc(ins, "t")
                t.wait_ge(b.sem["v"], c0["v"] + 19)
                for e in range(8):
                    if e >= 2:
                        t.wait_ge(b.sem["v"], c0["v"] + 19 + (e - 2) + 1)
                    PR = [b.psA, b.psB][e % 2]
                    t.matmul(PR[:, 0:512], b.sbS[0:8, 3456 + e * 128:3456 + (e + 1) * 128],
                             b.sbS[0:8, CFM_OFF:CFM_OFF + 512], start=True, stop=True)
                    ins = t.matmul(PR[:, 512:1024], b.sbS[0:8, 3456 + e * 128:3456 + (e + 1) * 128],
                                   b.sbS[0:8, CFM_OFF + 512:CFM_OFF + 1024], start=True, stop=True)
                    b.inc(ins, "t")

            @blk.scalar
            def _(sc):
                sc.wait_ge(b.sem["v"], c0["v"] + 10)
                for tc in range(8):
                    ins = sc.activation(zs(tc), zs(tc), ACT.Exp, bias=NMX[:, tc:tc + 1], scale=1.0)
                b.inc(ins, "a")

            @blk.vector
            def _(v):
                v.wait_ge(b.sem["g"], c0["g"] + 16)
                v.tensor_copy(b.sbS[:, 2048:2176], WRTF)
                v.memset(ZC, 0.0)
                v.tensor_copy(WRTHF, b.sbS[:, 2048:2176])
                v.memset(OC, 1.0)
                b.inc(v.tensor_tensor(b.sbS[:, 2176:2304], WRTF, WRTHF, OP.subtract), "v")
                for tc in range(8):
                    v.wait_ge(b.sem["t"], c0["t"] + tc + 1)
                    PL = [b.psL1, b.psL2][tc % 2]
                    b.inc(v.tensor_copy(zs(tc), PL[:, 0:8]), "v")
                for tc in range(8):
                    ins = v.reduce_max(NMX[:, tc:tc + 1], zs(tc), axis=AX.X, negate=True)
                b.inc(ins, "v")
                v.wait_ge(b.sem["a"], c0["a"] + 1)
                for tc in range(8):
                    v.reduce_max(V1[:, tc:tc + 1], zs(tc), axis=AX.X)
                for tc in range(8):
                    v.tensor_scalar(GA[:, tc * 8:(tc + 1) * 8], zs(tc), V1[:, tc:tc + 1], ZC, OP.subtract, OP.add)
                for tc in range(8):
                    v.tensor_scalar(GA[:, tc * 8:(tc + 1) * 8], GA[:, tc * 8:(tc + 1) * 8], 0.0, None, OP.is_ge)
                for tc in range(8):
                    v.scalar_tensor_tensor(Z2A[:, tc * 8:(tc + 1) * 8], GA[:, tc * 8:(tc + 1) * 8], -10.0,
                                           zs(tc), OP.mult, OP.add)
                for tc in range(8):
                    v.reduce_max(V2[:, tc:tc + 1], Z2A[:, tc * 8:(tc + 1) * 8], axis=AX.X)
                for tc in range(8):
                    v.tensor_scalar(S12[:, tc:tc + 1], V1[:, tc:tc + 1], V2[:, tc:tc + 1], ZC, OP.add, OP.add)
                for tc in range(8):
                    v.tensor_scalar(GA[:, tc * 8:(tc + 1) * 8], zs(tc), V2[:, tc:tc + 1], ZC, OP.subtract, OP.add)
                for tc in range(8):
                    v.tensor_scalar(GA[:, tc * 8:(tc + 1) * 8], GA[:, tc * 8:(tc + 1) * 8], 0.0, None, OP.is_ge)
                v.reciprocal(S12, S12)
                for tc in range(8):
                    v.tensor_tensor(CMA[:, tc * 8:(tc + 1) * 8], zs(tc), GA[:, tc * 8:(tc + 1) * 8], OP.mult)
                for tc in range(8):
                    v.tensor_scalar(CMA[:, tc * 8:(tc + 1) * 8], CMA[:, tc * 8:(tc + 1) * 8],
                                    S12[:, tc:tc + 1], OC, OP.mult, OP.mult)
                b.inc(v.tensor_copy(ZTM, CMA), "v")
                for tc in range(8):
                    v.wait_ge(b.sem["t"], c0["t"] + 8 + tc + 1)
                    b.inc(v.tensor_copy(b.sbS[0:8, CFM_OFF + tc * 128:CFM_OFF + (tc + 1) * 128],
                                        [b.psL1, b.psL2][tc % 2][0:8, 0:128]), "v")
                for e in range(8):
                    PR = [b.psA, b.psB][e % 2]
                    v.wait_ge(b.sem["t"], c0["t"] + 16 + e + 1)
                    v.tensor_copy(crep[:, e, 0:512], PR[:, 0:512])
                    b.inc(v.tensor_copy(crep[:, e, 512:1024], PR[:, 512:1024]), "v")
        self.check(c0, t=24, a=1, v=27, g=48)
        self.barrier()

    def phase_ffn(self):
        b = self
        nc = self.nc
        XH2 = b.sbW[:].rearrange("p (c t) -> p c t", c=NCH)[:, :, 0:1024]
        y2 = b.sbX[:].bitcast(F32).rearrange("p (c t) -> p c t", c=NCH)[:, :, 0:1024]
        h = b.sbV[:, 21504:32768].rearrange("p (c t) -> p c t", c=ICH)
        crep = b.sbV[:, 12288:12288 + 8 * 1024].rearrange("p (e t) -> p e t", e=8)
        w1r = [b.sbV[:, 0:2048].rearrange("p (c n) -> p c n", c=NCH),
               b.sbV[:, 2048:4096].rearrange("p (c n) -> p c n", c=NCH)]
        w2r = [b.sbV[:, 4096:4096 + ICH * 128].rearrange("p (c n) -> p c n", c=ICH),
               b.sbV[:, 5504:5504 + ICH * 128].rearrange("p (c n) -> p c n", c=ICH)]
        hps = [b.psU1, b.psU2, b.psL1, b.psL2]
        yps = [b.psA, b.psB]
        ee = list(range(8)) + [8]

        def w1_dram(e, i):
            src = b.ws1 if e == 8 else b.w1[e]
            return src.rearrange("(c p) n -> p c n", p=128)[:, :, i * 128:(i + 1) * 128]

        def w2_dram(e, fo):
            src = b.ws2 if e == 8 else b.w2[e]
            return src.rearrange("(c p) n -> p c n", p=128)[:, :, fo * 128:(fo + 1) * 128]

        TPE = 2 * ICH + 16
        def vec_before(ei):
            return sum((ICH if ee[k] < 8 else 0) + 16 for k in range(ei))
        c0 = self.snap()

        with nc.Block() as blk:
            @blk.gpsimd
            def _(g):
                nw1 = 0
                nw2 = 0
                for ei in range(9):
                    for i in range(ICH):
                        if nw1 >= 2:
                            pei, pi = divmod(nw1 - 2, ICH)
                            g.wait_ge(b.sem["t"], c0["t"] + pei * TPE + (pi + 1) * 2)
                        b.inc(g.dma_start(w1r[nw1 % 2][:], w1_dram(ee[ei], i)), "g", 16)
                        nw1 += 1
                    for fo in range(16):
                        if nw2 >= 2:
                            pei, pfo = divmod(nw2 - 2, 16)
                            g.wait_ge(b.sem["t"], c0["t"] + pei * TPE + 2 * ICH + pfo + 1)
                        b.inc(g.dma_start(w2r[nw2 % 2][:], w2_dram(ee[ei], fo)), "g", 16)
                        nw2 += 1

            @blk.tensor
            def _(t):
                for ei in range(9):
                    e = ee[ei]
                    for i in range(ICH):
                        t.wait_ge(b.sem["g"], c0["g"] + (ei * (ICH + 16) + i + 1) * 16)
                        for th in range(2):
                            n = (ei * ICH + i) * 2 + th
                            if n >= 4:
                                t.wait_ge(b.sem["a"], c0["a"] + (n - 4) + 1)
                            for f in range(NCH):
                                ins = t.matmul(hps[n % 4][:], w1r[(ei * ICH + i) % 2][:, f, :],
                                               XH2[:, f, th * 512:(th + 1) * 512],
                                               start=(f == 0), stop=(f == NCH - 1))
                            b.inc(ins, "t")
                    for fo in range(16):
                        t.wait_ge(b.sem["g"], c0["g"] + (ei * (ICH + 16) + ICH + fo + 1) * 16)
                        t.wait_ge(b.sem["a"], c0["a"] + (ei + 1) * 2 * ICH)
                        if e < 8:
                            t.wait_ge(b.sem["v"], c0["v"] + vec_before(ei) + ICH)
                        ny = ei * 16 + fo
                        if ny >= 2:
                            pei, pfo = divmod(ny - 2, 16)
                            t.wait_ge(b.sem["v"], c0["v"] + vec_before(pei) +
                                      (ICH if ee[pei] < 8 else 0) + pfo + 1)
                        PS = yps[ny % 2]
                        for th in range(2):
                            for i in range(ICH):
                                ins = t.matmul(PS[:, th * 512:(th + 1) * 512],
                                               w2r[ny % 2][:, i, :],
                                               h[:, i, th * 512:(th + 1) * 512],
                                               start=(i == 0), stop=(i == ICH - 1))
                        b.inc(ins, "t")

            @blk.scalar
            def _(sc):
                for ei in range(9):
                    for i in range(ICH):
                        for th in range(2):
                            n = (ei * ICH + i) * 2 + th
                            sc.wait_ge(b.sem["t"], c0["t"] + ei * TPE + i * 2 + th + 1)
                            b.inc(sc.activation(h[:, i, th * 512:(th + 1) * 512],
                                                hps[n % 4][:], ACT.Silu), "a")

            @blk.vector
            def _(v):
                for ei in range(9):
                    e = ee[ei]
                    if e < 8:
                        for i in range(ICH):
                            v.wait_ge(b.sem["a"], c0["a"] + ei * 2 * ICH + (i + 1) * 2)
                            b.inc(v.tensor_tensor(h[:, i, :], h[:, i, :], crep[:, e, :], OP.mult), "v")
                    for fo in range(16):
                        v.wait_ge(b.sem["t"], c0["t"] + ei * TPE + 2 * ICH + fo + 1)
                        PS = yps[(ei * 16 + fo) % 2]
                        if ei == 0:
                            ins = v.tensor_copy(y2[:, fo, :], PS[:])
                        else:
                            ins = v.tensor_tensor(y2[:, fo, :], y2[:, fo, :], PS[:], OP.add)
                        b.inc(ins, "v")
        self.check(c0, t=9 * TPE, a=9 * 2 * ICH, v=vec_before(9), g=16 * 9 * (ICH + 16))
        self.barrier()

    def phase_final(self):
        b = self
        nc = self.nc
        y2 = b.sbX[:].bitcast(F32).rearrange("p (c t) -> p c t", c=NCH)[:, :, 0:1024]
        sbv_f32 = b.sbV[:].bitcast(F32)
        xres = [sbv_f32[:, 0:1024], sbv_f32[:, 1024:2048]]
        c0 = self.snap()
        with nc.Block() as blk:
            @blk.sync
            def _(s):
                for fo in range(16):
                    if fo >= 2:
                        s.wait_ge(b.sem["v"], c0["v"] + (fo - 2) + 1)
                    b.inc(s.dma_start(xres[fo % 2], b.x2scr[fo * 128:(fo + 1) * 128, :]), "s", 16)

            @blk.vector
            def _(v):
                for fo in range(16):
                    v.wait_ge(b.sem["s"], c0["s"] + (fo + 1) * 16)
                    if fo >= 2:
                        v.wait_ge(b.sem["c"], c0["c"] + (fo - 2 + 1) * 16)
                    b.inc(v.tensor_tensor(xres[fo % 2], y2[:, fo, :], xres[fo % 2], OP.add), "v")

            @blk.scalar
            def _(sc):
                for fo in range(16):
                    sc.wait_ge(b.sem["v"], c0["v"] + fo + 1)
                    b.inc(sc.dma_start(b.out[fo * 128:(fo + 1) * 128, :], xres[fo % 2]), "c", 16)
                sc.wait_ge(b.sem["c"], c0["c"] + 256)
        self.check(c0, s=256, v=16, c=256)


def make_host_inputs(inputs, core):
    x = np.asarray(inputs["L0_attn_x"], np.float32)
    perm, own, other = perm_for(core)
    xt = np.ascontiguousarray(x[core // 2].T).reshape(E, 16, 128)
    xfm = np.ascontiguousarray(xt[:, perm, :].reshape(E, T)).astype(np.float32)
    mvals = [sum(1 for c in other if c <= own[qi]) for qi in range(8)]
    mrep = np.zeros((2, 128, 512), np.float32)
    for s in range(2):
        for qsub in range(4):
            mrep[s, :, qsub * 128:(qsub + 1) * 128] = mvals[4 * s + qsub]
    tri = (np.arange(128)[:, None] <= np.arange(128)[None, :]).astype(np.float32)
    onehot = np.zeros((8, 8 * 128), np.float32)
    for e in range(8):
        onehot[e, e * 128:(e + 1) * 128] = 1.0
    return dict(
        xfm=xfm,
        wq=np.ascontiguousarray(inputs["L0_attn_Wq"], np.float32),
        wk=np.ascontiguousarray(inputs["L0_attn_Wk"], np.float32),
        wv=np.ascontiguousarray(inputs["L0_attn_Wv"], np.float32),
        wo=np.ascontiguousarray(inputs["L0_attn_Wo"], np.float32),
        ws1=np.ascontiguousarray(inputs["L0_ffn_w_shared1"], np.float32),
        ws2=np.ascontiguousarray(inputs["L0_ffn_w_shared2"], np.float32),
        w1=np.ascontiguousarray(inputs["L0_ffn_w1"], np.float32),
        w2=np.ascontiguousarray(inputs["L0_ffn_w2"], np.float32),
        wrt=np.ascontiguousarray(
            np.asarray(inputs["L0_ffn_w_router"], np.float32).reshape(16, 128, 8)
            .transpose(1, 0, 2).reshape(128, 128)),
        lq=np.concatenate([np.asarray(inputs["L0_attn_lq1"], np.float32),
                           np.asarray(inputs["L0_attn_lq2"], np.float32)])[None, :],
        lk=np.concatenate([np.asarray(inputs["L0_attn_lk1"], np.float32),
                           np.asarray(inputs["L0_attn_lk2"], np.float32)])[None, :],
        cones128=np.ones((128, 1), np.float32),
        cones1x128=np.ones((1, 128), np.float32),
        cones1x128f=np.ones((1, 128), np.float32),
        crms1x128=np.full((1, 128), 1.0 - LAMBDA_INIT, np.float32),
        ctri=tri,
        conehot=onehot,
        cident=np.eye(128, dtype=np.float32),
        cmrep=mrep,
    )


def assemble_output(results):
    full = np.zeros((4, 2048, 2048), np.float32)
    for core in range(8):
        perm, own, other = perm_for(core)
        o = np.asarray(results[core]["out"])
        for j, c in enumerate(own):
            full[core // 2, c * 128:(c + 1) * 128, :] = o[:, j * 128:(j + 1) * 128].T
    return full


def kernel(**inputs):
    bld = Builder()
    nc = bld.build()
    in_maps = [make_host_inputs(inputs, core) for core in range(8)]
    res = run_bass_kernel_spmd(nc, in_maps, core_ids=list(range(8)))
    return assemble_output(res.results)


# version 03
# speedup vs baseline: 1.0280x; 1.0280x over previous
import math
import numpy as np
import concourse.bass as bass
import concourse.mybir as mybir
from concourse.alu_op_type import AluOpType as OP
from concourse.bass_utils import run_bass_kernel_spmd

F32 = mybir.dt.float32
BF16 = mybir.dt.bfloat16
ACT = mybir.ActivationFunctionType
AX = mybir.AxisListType

E = 2048
T = 2048
OT = 1024
NCH = 16
PAIRS = 16
LAMBDA_INIT = 0.8 - 0.6 * math.exp(-0.3 * 12)
SCALING = 64.0 ** -0.5
I = 1408
ICH = 11
NE = 8

OWN_EVEN = [0, 3, 4, 7, 8, 11, 12, 15]
OWN_ODD = [1, 2, 5, 6, 9, 10, 13, 14]


def perm_for(core):
    own = OWN_EVEN if core % 2 == 0 else OWN_ODD
    other = [c for c in range(16) if c not in own]
    return own + other, own, other


def s_chunks(slot):
    own = list(range(0, 4 * slot + 4))
    other = list(range(8, 8 + 4 * slot + 4))
    return own, other


class Builder:
    def __init__(self, debug=(), stop_after=None):
        self.stop_after = stop_after
        self.debug = set(debug)
        nc = bass.Bass("TRN2", target_bir_lowering=False)
        self.nc = nc
        dr = lambda n, s, d, k="ExternalInput": nc.dram_tensor(n, s, d, kind=k).ap()
        self.xfm = dr("xfm", [E, T], F32)
        self.wq = dr("wq", [E, E], F32)
        self.wk = dr("wk", [E, E], F32)
        self.wv = dr("wv", [E, E], F32)
        self.wo = dr("wo", [E, E], F32)
        self.ws1 = dr("ws1", [E, I], F32)
        self.ws2 = dr("ws2", [I, E], F32)
        self.w1 = dr("w1", [NE, E, I], F32)
        self.w2 = dr("w2", [NE, I, E], F32)
        self.wrt = dr("wrt", [128, 128], F32)
        self.lq = dr("lq", [1, 128], F32)
        self.lk = dr("lk", [1, 128], F32)
        self.cones128 = dr("cones128", [128, 1], F32)
        self.cones1x128 = dr("cones1x128", [1, 128], F32)
        self.cones1x128f = dr("cones1x128f", [1, 128], F32)
        self.crms1x128 = dr("crms1x128", [1, 128], F32)
        self.ctri = dr("ctri", [128, 128], F32)
        self.conehot = dr("conehot", [8, 8 * 128], F32)
        self.cident = dr("cident", [128, 128], F32)
        self.cmrep = dr("cmrep", [2, 128, 512], F32)
        self.out = dr("out", [E, OT], F32, "ExternalOutput")
        self.x2scr = dr("x2scr", [E, OT], F32, "Internal")

        self.sbX = nc.alloc_sbuf_tensor("sbX", [128, NCH * T], BF16)
        self.sbV = nc.alloc_sbuf_tensor("sbV", [128, NCH * T], BF16)
        self.sbW = nc.alloc_sbuf_tensor("sbW", [128, NCH * T], BF16)
        self.sbS = nc.alloc_sbuf_tensor("sbS", [128, 6272], BF16)
        self.sbF = nc.alloc_sbuf_tensor("sbF", [128, 64], F32)
        self.ones128 = nc.alloc_sbuf_tensor("ones128", [128, 1], BF16)
        self.ones1x128 = nc.alloc_sbuf_tensor("ones1x128", [1, 128], BF16)
        self.ones1x128f = nc.alloc_sbuf_tensor("ones1x128f", [1, 128], F32)
        self.rms1x128 = nc.alloc_sbuf_tensor("rms1x128", [1, 128], BF16)

        self.psA = nc.alloc_psum_tensor("psA", [128, 1024], F32)
        self.psB = nc.alloc_psum_tensor("psB", [128, 1024], F32)
        self.psU1 = nc.alloc_psum_tensor("psU1", [128, 512], F32)
        self.psU2 = nc.alloc_psum_tensor("psU2", [128, 512], F32)
        self.psL1 = nc.alloc_psum_tensor("psL1", [128, 512], F32)
        self.psL2 = nc.alloc_psum_tensor("psL2", [128, 512], F32)

        self.sem = {}
        self.cnt = {}
        for name in ("g", "s", "c", "t", "v", "a"):
            self.sem[name] = nc.alloc_semaphore("sem_" + name)
            self.cnt[name] = 0
        self.taps = {}

    def inc(self, instr, sem, n=1):
        instr.then_inc(self.sem[sem], n)
        self.cnt[sem] += n
        return self.cnt[sem]

    def snap(self):
        return dict(self.cnt)

    def check(self, c0, **expected):
        for k, d in expected.items():
            got = self.cnt[k] - c0[k]
            assert got == d, f"emission count mismatch sem {k}: got {got} expected {d}"

    def barrier(self, fence=True):
        b = self
        if fence:
            with self.nc.Block() as blk:
                @blk.sync
                def _(s):
                    s.wait_ge(b.sem["s"], b.cnt["s"])
                    s.wait_ge(b.sem["c"], b.cnt["c"])
                @blk.gpsimd
                def _(g):
                    g.wait_ge(b.sem["g"], b.cnt["g"])
        self.nc.all_engine_barrier()

    def want(self, name):
        return name in self.debug

    def tap(self, name, shape, dtype=F32):
        ap = self.nc.dram_tensor("tap_" + name, shape, dtype, kind="ExternalOutput").ap()
        self.taps[name] = ap
        return ap

    def store_tap(self, name, sb_ap, shape, dtype=F32):
        b = self
        nc = self.nc
        ap = self.tap(name, shape, dtype)
        with nc.Block() as blk:
            @blk.sync
            def _(s):
                n = b.inc(s.dma_start(ap, sb_ap), "s", 16)
                s.wait_ge(b.sem["s"], n)
        self.barrier()

    def build(self):
        steps = [
            ("consts", self.phase_consts, None),
            ("ln1", self.phase_ln1, ("xhat", lambda: self.store_tap("xhat", self.sbX[:], [128, NCH * T], BF16))),
            ("vproj", self.phase_vproj, ("V", lambda: self.store_tap("V", self.sbV[:], [128, NCH * T], BF16))),
            ("kproj", self.phase_kproj, ("K", lambda: self.store_tap("K", self.sbW[:], [128, NCH * T], BF16))),
            ("qproj", self.phase_qproj, ("Q", lambda: self.store_tap("Q", self.sbX[:], [128, NCH * T], BF16))),
            ("bounds", self.phase_bounds, ("bias", lambda: self.store_tap("bias", self.sbF[:, 0:64], [128, 64], F32))),
            ("attn", self.phase_attn, ("O", lambda: self.store_tap("O", self.sbX[:], [128, NCH * T], BF16))),
            ("outproj", self.phase_outproj, ("x2", lambda: self.store_tap("x2", self.x2scr, [E, OT], F32))),
            ("ln2", self.phase_ln2, ("xhat2", lambda: (self.store_tap("xhat2",
                self.sbW[:].rearrange("p (c t) -> p c t", c=NCH)[:, :, 0:1024], [128, NCH, OT], BF16),
                self.store_tap("xlo2",
                self.sbW[:].rearrange("p (c t) -> p c t", c=NCH)[:, :, 1024:2048], [128, NCH, OT], BF16)))),
            ("router", self.phase_router, ("comb", lambda: (self.store_tap("comb", self.sbS[0:8, 2432:2432 + 1024], [8, 1024], BF16),
                                                             self.store_tap("zrouter", self.sbS[:, 4608:4736], [128, 128], BF16)))),
            ("ffn", self.phase_ffn, None),
            ("final", self.phase_final, None),
        ]
        for name, fn, tapinfo in steps:
            fn()
            if tapinfo is not None and self.want(tapinfo[0]):
                tapinfo[1]()
            if self.stop_after == name:
                break
        return self.nc

    def phase_consts(self):
        b = self
        nc = self.nc
        with nc.Block() as blk:
            @blk.gpsimd
            def _(g):
                b.g_x = []
                for xs in range(4):
                    b.inc(g.dma_start(b.sbX[:].rearrange("p (c t) -> p c t", c=NCH)[:, xs * 4:(xs + 1) * 4, :],
                                      b.xfm.rearrange("(c p) t -> p c t", p=128)[:, xs * 4:(xs + 1) * 4, :]),
                          "g", 16)
                    b.g_x.append(b.cnt["g"])
                b.inc(g.dma_start(b.ones128[:], b.cones128), "g", 16)
                b.inc(g.dma_start(b.ones1x128[:], b.cones1x128), "g", 16)
                b.inc(g.dma_start(b.ones1x128f[:], b.cones1x128f), "g", 16)
                b.inc(g.dma_start(b.rms1x128[:], b.crms1x128), "g", 16)
                b.g_ln1 = b.cnt["g"]
                b.g_wv = []
                for dvg in range(4):
                    b.inc(g.dma_start(b.sbW[:].rearrange("p (c n) -> p c n", c=NCH)[:, :, dvg * 512:(dvg + 1) * 512],
                                      b.wv.rearrange("(c p) n -> p c n", p=128)[:, :, dvg * 512:(dvg + 1) * 512]),
                          "g", 16)
                    b.g_wv.append(b.cnt["g"])

            @blk.sync
            def _(s):
                lqk = b.sbS[0:1, 0:512].bitcast(F32)
                b.inc(s.dma_start(lqk[:, 0:128], b.lq), "s", 16)
                b.inc(s.dma_start(lqk[:, 128:256], b.lk), "s", 16)

            @blk.vector
            def _(v):
                v.wait_ge(b.sem["s"], b.cnt["s"])
                v.memset(b.sbF[0:1, 5:6], 1e-5)
                lqk = b.sbS[0:1, 0:512].bitcast(F32)
                v.tensor_tensor(lqk[:, 0:128], lqk[:, 0:128], lqk[:, 128:256], OP.mult)
                v.reduce_sum(b.sbF[0:1, 0:1], lqk[:, 0:64], axis=AX.X)
                b.inc(v.reduce_sum(b.sbF[0:1, 1:2], lqk[:, 64:128], axis=AX.X), "v")

            @blk.vector
            def _(v):
                b.inc(v.tensor_scalar(b.sbS[0:1, 600:602], b.sbF[0:1, 0:2], 1.0, None, OP.mult), "v")

            @blk.tensor
            def _(t):
                t.wait_ge(b.sem["v"], b.cnt["v"])
                t.wait_ge(b.sem["g"], b.cnt["g"])
                b.inc(t.matmul(b.psU1[:, 0:2], b.ones1x128[:], b.sbS[0:1, 600:602], start=True, stop=True), "t")

            @blk.vector
            def _(v):
                v.wait_ge(b.sem["t"], b.cnt["t"])
                b.inc(v.tensor_copy(b.sbF[:, 2:4], b.psU1[:, 0:2]), "v")

            @blk.scalar
            def _(sc):
                sc.wait_ge(b.sem["v"], b.cnt["v"])
                b.inc(sc.activation(b.sbF[:, 2:4], b.sbF[:, 2:4], ACT.Exp), "a")

            @blk.vector
            def _(v):
                v.wait_ge(b.sem["a"], b.cnt["a"])
                b.inc(v.tensor_scalar(b.sbF[:, 4:5], b.sbF[:, 2:3], b.sbF[:, 3:4],
                                      float(LAMBDA_INIT), OP.subtract, OP.add), "v")
        self.barrier(fence=False)

    def phase_ln1(self):
        b = self
        nc = self.nc
        X = b.sbX[:].rearrange("p (c t) -> p c t", c=NCH)
        sx = [b.psA[0:1, 0:512], b.psA[0:1, 512:1024], b.psB[0:1, 0:512], b.psB[0:1, 512:1024]]
        sxx = [b.psU1[0:1, :], b.psU2[0:1, :], b.psL1[0:1, :], b.psL2[0:1, :]]
        TMP = [b.sbS[:, 0:2048], b.sbS[:, 2048:4096]]
        c0 = self.snap()
        with nc.Block() as blk:
            @blk.vector
            def _(v):
                for c in range(NCH):
                    if c % 4 == 0:
                        v.wait_ge(b.sem["g"], b.g_x[c // 4])
                    if c >= 2:
                        v.wait_ge(b.sem["t"], c0["t"] + (c - 2) + 1)
                    b.inc(v.tensor_tensor(TMP[c % 2], X[:, c, :], X[:, c, :], OP.mult), "v")

            @blk.tensor
            def _(t):
                t.wait_ge(b.sem["g"], b.g_ln1)
                for c in range(NCH):
                    for j in range(4):
                        t.matmul(sx[j], b.ones128[:], X[:, c, 512 * j:512 * (j + 1)],
                                 start=(c == 0), stop=(c == NCH - 1))
                    t.wait_ge(b.sem["v"], c0["v"] + c + 1)
                    for j in range(4):
                        ins = t.matmul(sxx[j], b.ones128[:], TMP[c % 2][:, 512 * j:512 * (j + 1)],
                                       start=(c == 0), stop=(c == NCH - 1))
                    b.inc(ins, "t")

            @blk.vector
            def _(v):
                v.wait_ge(b.sem["t"], c0["t"] + NCH)
                m = b.sbS[0:1, 4096:6144]
                var = b.sbS[0:1, 0:2048]
                for j in range(4):
                    v.tensor_scalar(m[:, 512 * j:512 * (j + 1)], sx[j], 1.0 / E, None, OP.mult)
                    v.tensor_scalar(var[:, 512 * j:512 * (j + 1)], sxx[j], 1.0 / E, None, OP.mult)
                msq = b.sbS[0:1, 2048:4096]
                v.tensor_tensor(msq, m, m, OP.mult)
                b.inc(v.tensor_tensor(var, var, msq, OP.subtract), "v")

            @blk.scalar
            def _(sc):
                var = b.sbS[0:1, 0:2048]
                sc.wait_ge(b.sem["v"], c0["v"] + NCH + 1)
                sc.activation(var, var, ACT.Ln, bias=b.sbF[0:1, 5:6], scale=1.0)
                b.inc(sc.activation(var, var, ACT.Exp, scale=-0.5), "a")

            @blk.tensor
            def _(t):
                t.wait_ge(b.sem["a"], c0["a"] + 1)
                m = b.sbS[0:1, 4096:6144]
                r = b.sbS[0:1, 0:2048]
                mout = [b.psA[:, 0:512], b.psA[:, 512:1024], b.psB[:, 0:512], b.psB[:, 512:1024]]
                rout = [b.psU1[:], b.psU2[:], b.psL1[:], b.psL2[:]]
                for j in range(4):
                    t.matmul(mout[j], b.ones1x128[:], m[:, 512 * j:512 * (j + 1)], start=True, stop=True)
                for j in range(4):
                    ins = t.matmul(rout[j], b.ones1x128[:], r[:, 512 * j:512 * (j + 1)], start=True, stop=True)
                b.inc(ins, "t")

            @blk.vector
            def _(v):
                v.wait_ge(b.sem["t"], c0["t"] + NCH + 1)
                mb = b.sbS[:, 4096:6144]
                rb = b.sbS[:, 2048:4096]
                v.tensor_copy(mb[:, 0:512], b.psA[:, 0:512])
                v.tensor_copy(mb[:, 512:1024], b.psA[:, 512:1024])
                v.tensor_copy(mb[:, 1024:1536], b.psB[:, 0:512])
                v.tensor_copy(mb[:, 1536:2048], b.psB[:, 512:1024])
                v.tensor_copy(rb[:, 0:512], b.psU1[:])
                v.tensor_copy(rb[:, 512:1024], b.psU2[:])
                v.tensor_copy(rb[:, 1024:1536], b.psL1[:])
                v.tensor_copy(rb[:, 1536:2048], b.psL2[:])
                for c in range(NCH):
                    v.tensor_tensor(X[:, c, :], X[:, c, :], mb, OP.subtract)
                    ins = v.tensor_tensor(X[:, c, :], X[:, c, :], rb, OP.mult)
                b.inc(ins, "v")
        self.check(c0, v=NCH + 2, t=NCH + 1, a=1)
        self.barrier()

    def phase_vproj(self):
        b = self
        nc = self.nc
        X = b.sbX[:].rearrange("p (c t) -> p c t", c=NCH)
        W = b.sbW[:].rearrange("p (c n) -> p c n", c=NCH)
        V = b.sbV[:].rearrange("p (c n) -> p c n", c=NCH)
        ps = [b.psU1, b.psU2, b.psL1, b.psL2]
        c0 = self.snap()
        with nc.Block() as blk:
            @blk.tensor
            def _(t):
                for n in range(64):
                    dvg, kc = divmod(n, 16)
                    if kc == 0:
                        t.wait_ge(b.sem["g"], b.g_wv[dvg])
                    if n >= 4:
                        t.wait_ge(b.sem["v"], c0["v"] + (n - 4) + 1)
                    for f in range(NCH):
                        ins = t.matmul(ps[n % 4][:], X[:, f, kc * 128:(kc + 1) * 128],
                                       W[:, f, dvg * 512:(dvg + 1) * 512],
                                       start=(f == 0), stop=(f == NCH - 1))
                    b.inc(ins, "t")

            @blk.vector
            def _(v):
                for n in range(64):
                    dvg, kc = divmod(n, 16)
                    v.wait_ge(b.sem["t"], c0["t"] + n + 1)
                    b.inc(v.tensor_copy(V[:, kc, dvg * 512:(dvg + 1) * 512], ps[n % 4][:]), "v")
        self.check(c0, t=64, v=64)
        self.barrier()

    def _proj_stream(self, wdram, nout, ngrp, out_ap_fn, scale=None):
        b = self
        nc = self.nc
        X = b.sbX[:].rearrange("p (c t) -> p c t", c=NCH)
        ring = [b.sbS[:, 0:2048].rearrange("p (c n) -> p c n", c=NCH),
                b.sbS[:, 2048:4096].rearrange("p (c n) -> p c n", c=NCH)]
        ps = [b.psU1, b.psU2, b.psL1, b.psL2]
        c0 = self.snap()
        with nc.Block() as blk:
            @blk.gpsimd
            def _(g):
                for d in range(nout):
                    if d >= 2:
                        g.wait_ge(b.sem["t"], c0["t"] + (d - 1) * ngrp)
                    b.inc(g.dma_start(ring[d % 2][:], wdram(d)), "g", 16)

            @blk.tensor
            def _(t):
                for d in range(nout):
                    t.wait_ge(b.sem["g"], c0["g"] + (d + 1) * 16)
                    for tg in range(ngrp):
                        n = d * ngrp + tg
                        if n >= 4:
                            t.wait_ge(b.sem["v"], c0["v"] + (n - 4) + 1)
                        for f in range(NCH):
                            ins = t.matmul(ps[n % 4][:], ring[d % 2][:, f, :],
                                           X[:, f, tg * 512:(tg + 1) * 512],
                                           start=(f == 0), stop=(f == NCH - 1))
                        b.inc(ins, "t")

            @blk.vector
            def _(v):
                for d in range(nout):
                    for tg in range(ngrp):
                        n = d * ngrp + tg
                        v.wait_ge(b.sem["t"], c0["t"] + n + 1)
                        if scale is None:
                            ins = v.tensor_copy(out_ap_fn(d, tg), ps[n % 4][:])
                        else:
                            ins = v.tensor_scalar(out_ap_fn(d, tg), ps[n % 4][:], scale, None, OP.mult)
                        b.inc(ins, "v")
        self.check(c0, t=nout * ngrp, v=nout * ngrp, g=16 * nout)
        self.barrier()

    def phase_kproj(self):
        b = self
        K = b.sbW[:].rearrange("p (c t) -> p c t", c=NCH)
        self._proj_stream(
            lambda d: b.wk.rearrange("(c p) n -> p c n", p=128)[:, :, d * 128:(d + 1) * 128],
            16, 4, lambda d, tg: K[:, d, tg * 512:(tg + 1) * 512])

    def phase_qproj(self):
        b = self
        X = b.sbX[:].rearrange("p (c t) -> p c t", c=NCH)
        self._proj_stream(
            lambda d: b.wq.rearrange("(c p) n -> p c n", p=128)[:, :, d * 128:(d + 1) * 128],
            16, 2, lambda d, tg: X[:, d, 1024 + tg * 512:1024 + (tg + 1) * 512],
            scale=float(SCALING))

    def phase_bounds(self):
        b = self
        nc = self.nc
        X = b.sbX[:].rearrange("p (c t) -> p c t", c=NCH)
        K = b.sbW[:].rearrange("p (c t) -> p c t", c=NCH)
        Q1 = b.sbS[:, 0:1024]
        K1 = b.sbS[:, 1024:3072]
        c0 = self.snap()
        with nc.Block() as blk:
            @blk.gpsimd
            def _(g):
                b.inc(g.dma_start(b.sbS[:, 5120:5248], b.ctri), "g", 16)
                b.inc(g.dma_start(b.sbS[:, 5248:6272].rearrange("p (s n) -> p s n", s=2),
                                  b.cmrep.rearrange("s p n -> p s n")), "g", 16)

            @blk.vector
            def _(v):
                for pair in range(PAIRS):
                    if pair >= 1:
                        v.wait_ge(b.sem["t"], c0["t"] + pair)
                    v.tensor_tensor(Q1, X[:, pair, 1024:2048], X[:, pair, 1024:2048], OP.mult)
                    v.tensor_tensor(K1[:, 0:1024], K[:, pair, 0:1024], K[:, pair, 0:1024], OP.mult)
                    b.inc(v.tensor_tensor(K1[:, 1024:2048], K[:, pair, 1024:2048], K[:, pair, 1024:2048], OP.mult), "v")
                    v.wait_ge(b.sem["t"], c0["t"] + pair + 1)
                    SF = b.sbS[0:1, 3072:5120].bitcast(F32)
                    v.reduce_max(SF[:, 0 + pair:1 + pair], b.psB[0:1, 0:1024], axis=AX.X)
                    v.reduce_max(SF[:, 16 + pair:17 + pair], b.psU1[0:1, :], axis=AX.X)
                    v.reduce_max(SF[:, 32 + pair:33 + pair], b.psU2[0:1, :], axis=AX.X)
                    b.inc(v.reduce_max(SF[:, 48 + pair:49 + pair], b.psA[0:1, 0:1024], axis=AX.X), "v")

            @blk.tensor
            def _(t):
                for pair in range(PAIRS):
                    t.wait_ge(b.sem["v"], c0["v"] + pair * 2 + 1)
                    t.matmul(b.psA[0:1, 0:512], b.ones128[:], Q1[:, 0:512], start=True, stop=True)
                    t.matmul(b.psA[0:1, 512:1024], b.ones128[:], Q1[:, 512:1024], start=True, stop=True)
                    t.matmul(b.psB[0:1, 0:512], b.ones128[:], K1[:, 0:512], start=True, stop=True)
                    t.matmul(b.psB[0:1, 512:1024], b.ones128[:], K1[:, 512:1024], start=True, stop=True)
                    t.matmul(b.psU1[0:1, :], b.ones128[:], K1[:, 1024:1536], start=True, stop=True)
                    ins = t.matmul(b.psU2[0:1, :], b.ones128[:], K1[:, 1536:2048], start=True, stop=True)
                    b.inc(ins, "t")

            @blk.vector
            def _(v):
                SF = b.sbS[0:1, 3072:5120].bitcast(F32)
                v.wait_ge(b.sem["v"], c0["v"] + 2 * PAIRS)
                v.tensor_tensor(SF[:, 0:16], SF[:, 0:16], SF[:, 16:32], OP.max)
                v.tensor_tensor(SF[:, 0:16], SF[:, 0:16], SF[:, 32:48], OP.max)
                b.inc(v.tensor_tensor(b.sbF[0:1, 32:48], SF[:, 0:16], SF[:, 48:64], OP.mult), "v")
        self.check(c0, v=2 * PAIRS + 1, t=PAIRS, g=32)
        self.barrier()
        c1 = self.snap()
        with nc.Block() as blk:
            @blk.scalar
            def _(sc):
                sc.activation(b.sbF[0:1, 32:48], b.sbF[0:1, 32:48], ACT.Ln)
                b.inc(sc.activation(b.sbF[0:1, 32:48], b.sbF[0:1, 32:48], ACT.Exp, scale=0.5), "a")

            @blk.vector
            def _(v):
                v.wait_ge(b.sem["a"], c1["a"] + 1)
                b.inc(v.tensor_scalar(b.sbS[0:1, 4480:4496], b.sbF[0:1, 32:48], -1.0, None, OP.mult), "v")

            @blk.tensor
            def _(t):
                t.wait_ge(b.sem["v"], c1["v"] + 1)
                b.inc(t.matmul(b.psL1[:, 0:16], b.ones1x128[:], b.sbS[0:1, 4480:4496], start=True, stop=True), "t")

            @blk.vector
            def _(v):
                v.wait_ge(b.sem["t"], c1["t"] + 1)
                b.inc(v.tensor_copy(b.sbF[:, 8:24], b.psL1[:, 0:16]), "v")
        self.barrier()

    def phase_attn(self):
        b = self
        for pair in range(PAIRS):
            for slot in range(2):
                self.attn_iter(pair, slot)
        self.barrier()

    def attn_iter(self, pair, slot):
        b = self
        nc = self.nc
        dbgP = (pair == 0 and slot == 0 and self.want("P00"))
        own, other = s_chunks(slot)
        chunks = own + other
        n_ch = len(chunks)
        q0 = slot * 512
        X = b.sbX[:].rearrange("p (c t) -> p c t", c=NCH)
        Khi = b.sbW[64:128, :].rearrange("p (c t) -> p c t", c=NCH)
        Klo = b.sbW[0:64, :].rearrange("p (c t) -> p c t", c=NCH)
        V = b.sbV[:].rearrange("p (c n) -> p c n", c=NCH)
        Qlo = b.sbX[0:64, :].rearrange("p (c t) -> p c t", c=NCH)
        Qhi = b.sbX[64:128, :].rearrange("p (c t) -> p c t", c=NCH)
        P12 = [b.sbS[:, 0:1024], b.sbS[:, 1024:2048]]
        RB1 = b.sbS[:, 2048:2560]
        RB2 = b.sbS[:, 2560:3072]
        OSQ = b.sbS[:, 0:512]
        REC = b.sbS[0:1, 4096:5120]
        RSTD = b.sbS[0:1, 2048:2560]
        TRI = b.sbS[:, 5120:5248]
        BIAS = b.sbF[:, 8:24]
        lam_ap = b.sbF[0:1, 4:5]
        SB = [b.psA, b.psB]
        Oap = X[:, pair, q0:q0 + 512]

        seq = [("S", 0), ("S", 1)]
        for ci in range(n_ch):
            if ci + 2 < n_ch:
                seq.append(("S", ci + 2))
            seq.append(("L", ci))
        t_pos = {}
        for idx, item in enumerate(seq):
            t_pos[item] = idx + 1
        n_t_loop = len(seq)
        c0 = self.snap()
        if dbgP:
            tapP = self.tap("P00", [8, 128, 1024], BF16)
            tapL = self.tap("L00", [1, 1024], F32)
            tapRB = self.tap("RB00", [128, 1024], BF16)

        with nc.Block() as blk:
            if dbgP:
                @blk.sync
                def _(s):
                    for ci in range(n_ch):
                        s.wait_ge(b.sem["v"], c0["v"] + ci + 1)
                        b.inc(s.dma_start(tapP[ci], P12[ci % 2]), "s", 16)

            @blk.tensor
            def _(t):
                for item in seq:
                    kind, ci = item
                    kc = chunks[ci]
                    if kind == "S" and dbgP:
                        if ci >= 2:
                            t.wait_ge(b.sem["s"], c0["s"] + (ci - 2 + 1) * 16)
                    if kind == "S":
                        S = SB[ci % 2]
                        if ci >= 2:
                            t.wait_ge(b.sem["a"], c0["a"] + (ci - 2) + 1)
                        t.matmul(S[:, 0:512], Klo[:, pair, kc * 128:(kc + 1) * 128],
                                 Qlo[:, pair, 1024 + q0:1024 + q0 + 512],
                                 start=True, stop=True, tile_position=(0, 0))
                        ins = t.matmul(S[:, 512:1024], Khi[:, pair, kc * 128:(kc + 1) * 128],
                                       Qhi[:, pair, 1024 + q0:1024 + q0 + 512],
                                       start=True, stop=True, tile_position=(64, 0))
                        b.inc(ins, "t")
                    else:
                        t.wait_ge(b.sem["v"], c0["v"] + ci + 1)
                        P = P12[ci % 2]
                        first = ci == 0
                        last = ci == n_ch - 1
                        t.matmul(b.psL1[0:1, :], b.ones128[:], P[:, 0:512], start=first, stop=last)
                        t.matmul(b.psL2[0:1, :], b.ones128[:], P[:, 512:1024], start=first, stop=last)
                        t.matmul(b.psU1[:], V[:, kc, pair * 128:(pair + 1) * 128], P[:, 0:512],
                                 start=first, stop=last)
                        ins = t.matmul(b.psU2[:], V[:, kc, pair * 128:(pair + 1) * 128], P[:, 512:1024],
                                       start=first, stop=last)
                        b.inc(ins, "t")
                if dbgP:
                    t.wait_ge(b.sem["s"], c0["s"] + n_ch * 16)
                t.wait_ge(b.sem["a"], c0["a"] + n_ch + 1)
                t.matmul(b.psL1[:], b.ones1x128[:], REC[:, 0:512], start=True, stop=True)
                b.inc(t.matmul(b.psL2[:], b.ones1x128[:], REC[:, 512:1024], start=True, stop=True), "t")
                t.wait_ge(b.sem["v"], c0["v"] + n_ch + 3)
                b.inc(t.matmul(b.psU1[0:1, :], b.ones128[:], OSQ, start=True, stop=True), "t")
                t.wait_ge(b.sem["a"], c0["a"] + n_ch + 2)
                b.inc(t.matmul(b.psU2[:], b.rms1x128[:], RSTD[:, 0:512], start=True, stop=True), "t")

            @blk.scalar
            def _(sc):
                for ci in range(n_ch):
                    sc.wait_ge(b.sem["t"], c0["t"] + t_pos[("S", ci)])
                    if ci >= 2:
                        sc.wait_ge(b.sem["t"], c0["t"] + t_pos[("L", ci - 2)])
                    b.inc(sc.activation(P12[ci % 2], SB[ci % 2][:], ACT.Exp,
                                        bias=BIAS[:, pair:pair + 1], scale=1.0), "a")
                sc.wait_ge(b.sem["t"], c0["t"] + n_t_loop)
                sc.activation(b.psA[0:1, 0:512], b.psL1[0:1, :], ACT.Ln)
                sc.activation(b.psA[0:1, 512:1024], b.psL2[0:1, :], ACT.Ln)
                b.inc(sc.activation(REC, b.psA[0:1, 0:1024], ACT.Exp, scale=-1.0), "a")
                sc.wait_ge(b.sem["t"], c0["t"] + n_t_loop + 2)
                sc.activation(RSTD[:, 0:512], b.psU1[0:1, :], ACT.Ln, bias=b.sbF[0:1, 5:6], scale=1.0 / 128.0)
                b.inc(sc.activation(RSTD[:, 0:512], RSTD[:, 0:512], ACT.Exp, scale=-0.5), "a")

            @blk.vector
            def _(v):
                for ci in range(n_ch):
                    kc = chunks[ci]
                    P = P12[ci % 2]
                    v.wait_ge(b.sem["a"], c0["a"] + ci + 1)
                    if ci < len(own):
                        jq = kc - 4 * slot
                        if 0 <= jq <= 3:
                            for half in range(2):
                                base = half * 512
                                if jq > 0:
                                    v.memset(P[:, base:base + jq * 128], 0.0)
                                v.tensor_tensor(P[:, base + jq * 128:base + (jq + 1) * 128],
                                                P[:, base + jq * 128:base + (jq + 1) * 128],
                                                TRI, OP.mult)
                    else:
                        p = kc - 8
                        if not (slot == 1 and p < 4):
                            for half in range(2):
                                base = half * 512
                                v.scalar_tensor_tensor(P[:, base:base + 512], b.sbS[:, 5248 + slot * 512:5248 + (slot + 1) * 512],
                                                       float(p) + 0.5, P[:, base:base + 512],
                                                       OP.is_gt, OP.mult)
                    b.inc(v.tensor_copy(b.sbF[0:1, 47:48], b.sbF[0:1, 47:48]), "v")
                v.wait_ge(b.sem["t"], c0["t"] + n_t_loop + 1)
                v.tensor_copy(RB1, b.psL1[:])
                b.inc(v.tensor_scalar(RB2, b.psL2[:], b.sbF[:, 4:5], 1.0, OP.mult, OP.mult), "v")
                T1 = b.sbS[:, 0:512]
                T2 = b.sbS[:, 512:1024]
                v.tensor_tensor(T1, b.psU1[:], RB1, OP.mult)
                v.tensor_tensor(T2, b.psU2[:], RB2, OP.mult)
                b.inc(v.tensor_tensor(Oap, T1, T2, OP.subtract), "v")
                b.inc(v.tensor_tensor(OSQ, Oap, Oap, OP.mult), "v")
                v.wait_ge(b.sem["t"], c0["t"] + n_t_loop + 3)
                b.inc(v.tensor_tensor(Oap, Oap, b.psU2[:], OP.mult), "v")
        self.check(c0, t=n_t_loop + 3, a=n_ch + 2, v=n_ch + 4)

    def phase_outproj(self):
        b = self
        nc = self.nc
        X = b.sbX[:].rearrange("p (c t) -> p c t", c=NCH)
        ring = [b.sbS[:, 0:2048].rearrange("p (c n) -> p c n", c=NCH),
                b.sbS[:, 2048:4096].rearrange("p (c n) -> p c n", c=NCH)]
        sbv_f32 = b.sbV[:].bitcast(F32)
        xres = [sbv_f32[:, 0:1024], sbv_f32[:, 1024:2048]]
        x2strip = [sbv_f32[:, 2048:3072], sbv_f32[:, 3072:4096]]
        x2b = [b.sbV[:, 8192:9216], b.sbV[:, 9216:10240]]
        x2sq = [b.sbV[:, 10240:11264], b.sbV[:, 11264:12288]]
        sx = [b.psU1[0:1, :], b.psU2[0:1, :]]
        sxx = [b.psL1[0:1, :], b.psL2[0:1, :]]
        c0 = self.snap()
        with nc.Block() as blk:
            def y_done(j):
                return 2 if j == 0 else (4 if j == 1 else 3 * j)

            def st_done(j):
                return 3 * j + 7 if j <= 13 else (47 if j == 14 else 48)

            @blk.gpsimd
            def _(g):
                for fo in range(16):
                    if fo >= 2:
                        g.wait_ge(b.sem["t"], c0["t"] + y_done(fo - 2))
                    b.inc(g.dma_start(ring[fo % 2][:],
                                      b.wo.rearrange("(c p) n -> p c n", p=128)[:, :, fo * 128:(fo + 1) * 128]),
                          "g", 16)

            @blk.sync
            def _(s):
                for fo in range(16):
                    if fo >= 2:
                        s.wait_ge(b.sem["v"], c0["v"] + (fo - 2) * 3 + 1)
                    b.inc(s.dma_start(xres[fo % 2], b.xfm[fo * 128:(fo + 1) * 128, 0:1024]), "s", 16)

            def stats_group(t, j):
                t.wait_ge(b.sem["v"], c0["v"] + j * 3 + 3)
                t.matmul(sx[0], b.ones128[:], x2b[j % 2][:, 0:512], start=(j == 0), stop=(j == 15))
                t.matmul(sx[1], b.ones128[:], x2b[j % 2][:, 512:1024], start=(j == 0), stop=(j == 15))
                t.matmul(sxx[0], b.ones128[:], x2sq[j % 2][:, 0:512], start=(j == 0), stop=(j == 15))
                ins = t.matmul(sxx[1], b.ones128[:], x2sq[j % 2][:, 512:1024], start=(j == 0), stop=(j == 15))
                b.inc(ins, "t")

            @blk.tensor
            def _(t):
                for fo in range(16):
                    t.wait_ge(b.sem["g"], c0["g"] + (fo + 1) * 16)
                    if fo >= 2:
                        t.wait_ge(b.sem["v"], c0["v"] + (fo - 2) * 3 + 1)
                    PS = [b.psA, b.psB][fo % 2]
                    for th in range(2):
                        for pair in range(PAIRS):
                            ins = t.matmul(PS[:, th * 512:(th + 1) * 512], ring[fo % 2][:, pair, :],
                                           X[:, pair, th * 512:(th + 1) * 512],
                                           start=(pair == 0), stop=(pair == PAIRS - 1))
                        b.inc(ins, "t")
                    if fo >= 2:
                        stats_group(t, fo - 2)
                stats_group(t, 14)
                stats_group(t, 15)

            @blk.vector
            def _(v):
                for fo in range(16):
                    PS = [b.psA, b.psB][fo % 2]
                    v.wait_ge(b.sem["t"], c0["t"] + y_done(fo))
                    v.wait_ge(b.sem["s"], c0["s"] + (fo + 1) * 16)
                    if fo >= 2:
                        v.wait_ge(b.sem["c"], c0["c"] + (fo - 2 + 1) * 16)
                        v.wait_ge(b.sem["t"], c0["t"] + st_done(fo - 2))
                    v.tensor_tensor(x2strip[fo % 2], PS[:], xres[fo % 2], OP.add)
                    b.inc(v.tensor_copy(x2b[fo % 2], x2strip[fo % 2]), "v")
                    b.inc(v.tensor_copy(b.sbF[0:1, 47:48], b.sbF[0:1, 47:48]), "v")
                    b.inc(v.tensor_tensor(x2sq[fo % 2], x2b[fo % 2], x2b[fo % 2], OP.mult), "v")

            @blk.scalar
            def _(sc):
                for fo in range(16):
                    sc.wait_ge(b.sem["v"], c0["v"] + fo * 3 + 2)
                    b.inc(sc.dma_start(b.x2scr[fo * 128:(fo + 1) * 128, :], x2strip[fo % 2]), "c", 16)
                sc.wait_ge(b.sem["c"], c0["c"] + 256)
        self.check(c0, t=48, v=48, c=256, g=256, s=256)
        self.barrier()

    def phase_ln2(self):
        b = self
        nc = self.nc
        XH2 = b.sbW[:].rearrange("p (c t) -> p c t", c=NCH)[:, :, 0:1024]
        XL2 = b.sbW[:].rearrange("p (c t) -> p c t", c=NCH)[:, :, 1024:2048]
        sx = [b.psU1[0:1, :], b.psU2[0:1, :]]
        sxx = [b.psL1[0:1, :], b.psL2[0:1, :]]
        m = b.sbS[0:1, 4096:5120]
        var = b.sbS[0:1, 5120:6144]
        sbv_f32 = b.sbV[:].bitcast(F32)
        strips = [sbv_f32[:, 0:1024], sbv_f32[:, 1024:2048]]
        mbf = sbv_f32[:, 2048:3072]
        rbf = sbv_f32[:, 3072:4096]
        hif = b.sbS[:, 0:2048].bitcast(F32)
        c0 = self.snap()
        with nc.Block() as blk:
            @blk.vector
            def _(v):
                for j in range(2):
                    v.tensor_scalar(m[:, 512 * j:512 * (j + 1)], sx[j], 1.0 / E, None, OP.mult)
                    v.tensor_scalar(var[:, 512 * j:512 * (j + 1)], sxx[j], 1.0 / E, None, OP.mult)
                msq = b.sbS[0:1, 2048:3072]
                v.tensor_tensor(msq, m, m, OP.mult)
                b.inc(v.tensor_tensor(var, var, msq, OP.subtract), "v")

            @blk.scalar
            def _(sc):
                sc.wait_ge(b.sem["v"], c0["v"] + 1)
                sc.activation(var, var, ACT.Ln, bias=b.sbF[0:1, 5:6], scale=1.0)
                b.inc(sc.activation(var, var, ACT.Exp, scale=-0.5), "a")

            @blk.tensor
            def _(t):
                t.wait_ge(b.sem["a"], c0["a"] + 1)
                t.matmul(b.psA[:, 0:512], b.ones1x128[:], m[:, 0:512], start=True, stop=True)
                t.matmul(b.psA[:, 512:1024], b.ones1x128[:], m[:, 512:1024], start=True, stop=True)
                t.matmul(b.psB[:, 0:512], b.ones1x128[:], var[:, 0:512], start=True, stop=True)
                b.inc(t.matmul(b.psB[:, 512:1024], b.ones1x128[:], var[:, 512:1024], start=True, stop=True), "t")

            @blk.vector
            def _(v):
                v.wait_ge(b.sem["t"], c0["t"] + 1)
                v.tensor_copy(mbf, b.psA[:, 0:1024])
                b.inc(v.tensor_copy(rbf, b.psB[:, 0:1024]), "v")

            @blk.sync
            def _(s):
                for fo in range(16):
                    if fo >= 2:
                        s.wait_ge(b.sem["v"], c0["v"] + 2 + (fo - 2) * 2 + 2)
                    b.inc(s.dma_start(strips[fo % 2], b.x2scr[fo * 128:(fo + 1) * 128, :]), "s", 16)

            @blk.scalar
            def _(sc):
                for fo in range(16):
                    sc.wait_ge(b.sem["v"], c0["v"] + 2 + fo * 2 + 1)
                    sc.activation(XH2[:, fo, :], strips[fo % 2], ACT.Copy)
                    b.inc(sc.activation(hif, XH2[:, fo, :], ACT.Copy), "a")

            @blk.vector
            def _(v):
                for fo in range(16):
                    v.wait_ge(b.sem["s"], c0["s"] + (fo + 1) * 16)
                    st = strips[fo % 2]
                    v.tensor_tensor(st, st, mbf, OP.subtract)
                    b.inc(v.tensor_tensor(st, st, rbf, OP.mult), "v")
                    v.wait_ge(b.sem["a"], c0["a"] + 1 + fo + 1)
                    b.inc(v.tensor_tensor(XL2[:, fo, :], st, hif, OP.subtract), "v")
        self.check(c0, v=2 + 32, t=1, a=1 + 16, s=256)
        self.barrier()

    def phase_router(self):
        b = self
        nc = self.nc
        XH2 = b.sbW[:].rearrange("p (c t) -> p c t", c=NCH)[:, :, 0:1024]
        XL2 = b.sbW[:].rearrange("p (c t) -> p c t", c=NCH)[:, :, 1024:2048]
        WRT = b.sbS[:, 2048:2048 + 16 * 8].rearrange("p (c n) -> p c n", c=NCH)
        WRTL = b.sbS[:, 2176:2176 + 16 * 8].rearrange("p (c n) -> p c n", c=NCH)
        WRTF = b.sbS[:, 5120:5376].bitcast(F32)
        WRTHF = b.sbS[:, 5376:5632].bitcast(F32)
        IDN = b.sbS[:, 4480:4608]
        CFM_OFF = 2432
        ZA = b.sbS[:, 4608:4736].bitcast(F32)
        GA = b.sbS[:, 4736:4864].bitcast(F32)
        Z2A = b.sbS[:, 4864:4992].bitcast(F32)
        CMA = b.sbS[:, 4992:5120].bitcast(F32)
        NMX = b.sbF[:, 8:16]
        V1 = b.sbF[:, 16:24]
        V2 = b.sbF[:, 24:32]
        S12 = b.sbF[:, 32:40]
        ZC = b.sbF[:, 44:45]
        OC = b.sbF[:, 45:46]
        ZTM = b.sbS[:, 2304:2368]
        crep = b.sbV[:, 12288:12288 + 8 * 1024].rearrange("p (e t) -> p e t", e=8)
        c0 = self.snap()

        def zs(tc):
            return ZA[:, tc * 8:(tc + 1) * 8]

        with nc.Block() as blk:
            @blk.gpsimd
            def _(g):
                b.inc(g.dma_start(WRTF[:, 0:128], b.wrt), "g", 16)
                b.inc(g.dma_start(b.sbS[0:8, 3456:4480], b.conehot), "g", 16)
                b.inc(g.dma_start(IDN, b.cident), "g", 16)

            @blk.tensor
            def _(t):
                t.wait_ge(b.sem["g"], c0["g"] + 48)
                t.wait_ge(b.sem["v"], c0["v"] + 1)
                for tc in range(8):
                    if tc >= 2:
                        t.wait_ge(b.sem["v"], c0["v"] + 1 + (tc - 2) + 1)
                    PL = [b.psL1, b.psL2][tc % 2]
                    for f in range(NCH):
                        t.matmul(PL[:, 0:8], XH2[:, f, tc * 128:(tc + 1) * 128],
                                 WRT[:, f, :], start=(f == 0), stop=False)
                        t.matmul(PL[:, 0:8], XL2[:, f, tc * 128:(tc + 1) * 128],
                                 WRT[:, f, :], start=False, stop=False)
                        ins = t.matmul(PL[:, 0:8], XH2[:, f, tc * 128:(tc + 1) * 128],
                                       WRTL[:, f, :], start=False,
                                       stop=(f == NCH - 1))
                    b.inc(ins, "t")
                t.wait_ge(b.sem["v"], c0["v"] + 11)
                for tc in range(8):
                    if tc >= 2:
                        t.wait_ge(b.sem["v"], c0["v"] + 11 + (tc - 2) + 1)
                    ins = t.matmul([b.psL1, b.psL2][tc % 2][0:8, 0:128],
                                   ZTM[:, tc * 8:(tc + 1) * 8], IDN, start=True, stop=True)
                    b.inc(ins, "t")
                t.wait_ge(b.sem["v"], c0["v"] + 19)
                for e in range(8):
                    if e >= 2:
                        t.wait_ge(b.sem["v"], c0["v"] + 19 + (e - 2) + 1)
                    PR = [b.psA, b.psB][e % 2]
                    t.matmul(PR[:, 0:512], b.sbS[0:8, 3456 + e * 128:3456 + (e + 1) * 128],
                             b.sbS[0:8, CFM_OFF:CFM_OFF + 512], start=True, stop=True)
                    ins = t.matmul(PR[:, 512:1024], b.sbS[0:8, 3456 + e * 128:3456 + (e + 1) * 128],
                                   b.sbS[0:8, CFM_OFF + 512:CFM_OFF + 1024], start=True, stop=True)
                    b.inc(ins, "t")

            @blk.scalar
            def _(sc):
                sc.wait_ge(b.sem["v"], c0["v"] + 10)
                for tc in range(8):
                    ins = sc.activation(zs(tc), zs(tc), ACT.Exp, bias=NMX[:, tc:tc + 1], scale=1.0)
                b.inc(ins, "a")

            @blk.vector
            def _(v):
                v.wait_ge(b.sem["g"], c0["g"] + 16)
                v.tensor_copy(b.sbS[:, 2048:2176], WRTF)
                v.memset(ZC, 0.0)
                v.tensor_copy(WRTHF, b.sbS[:, 2048:2176])
                v.memset(OC, 1.0)
                b.inc(v.tensor_tensor(b.sbS[:, 2176:2304], WRTF, WRTHF, OP.subtract), "v")
                for tc in range(8):
                    v.wait_ge(b.sem["t"], c0["t"] + tc + 1)
                    PL = [b.psL1, b.psL2][tc % 2]
                    b.inc(v.tensor_copy(zs(tc), PL[:, 0:8]), "v")
                for tc in range(8):
                    ins = v.reduce_max(NMX[:, tc:tc + 1], zs(tc), axis=AX.X, negate=True)
                b.inc(ins, "v")
                v.wait_ge(b.sem["a"], c0["a"] + 1)
                for tc in range(8):
                    v.reduce_max(V1[:, tc:tc + 1], zs(tc), axis=AX.X)
                for tc in range(8):
                    v.tensor_scalar(GA[:, tc * 8:(tc + 1) * 8], zs(tc), V1[:, tc:tc + 1], ZC, OP.subtract, OP.add)
                for tc in range(8):
                    v.tensor_scalar(GA[:, tc * 8:(tc + 1) * 8], GA[:, tc * 8:(tc + 1) * 8], 0.0, None, OP.is_ge)
                for tc in range(8):
                    v.scalar_tensor_tensor(Z2A[:, tc * 8:(tc + 1) * 8], GA[:, tc * 8:(tc + 1) * 8], -10.0,
                                           zs(tc), OP.mult, OP.add)
                for tc in range(8):
                    v.reduce_max(V2[:, tc:tc + 1], Z2A[:, tc * 8:(tc + 1) * 8], axis=AX.X)
                for tc in range(8):
                    v.tensor_scalar(S12[:, tc:tc + 1], V1[:, tc:tc + 1], V2[:, tc:tc + 1], ZC, OP.add, OP.add)
                for tc in range(8):
                    v.tensor_scalar(GA[:, tc * 8:(tc + 1) * 8], zs(tc), V2[:, tc:tc + 1], ZC, OP.subtract, OP.add)
                for tc in range(8):
                    v.tensor_scalar(GA[:, tc * 8:(tc + 1) * 8], GA[:, tc * 8:(tc + 1) * 8], 0.0, None, OP.is_ge)
                v.reciprocal(S12, S12)
                for tc in range(8):
                    v.tensor_tensor(CMA[:, tc * 8:(tc + 1) * 8], zs(tc), GA[:, tc * 8:(tc + 1) * 8], OP.mult)
                for tc in range(8):
                    v.tensor_scalar(CMA[:, tc * 8:(tc + 1) * 8], CMA[:, tc * 8:(tc + 1) * 8],
                                    S12[:, tc:tc + 1], OC, OP.mult, OP.mult)
                b.inc(v.tensor_copy(ZTM, CMA), "v")
                for tc in range(8):
                    v.wait_ge(b.sem["t"], c0["t"] + 8 + tc + 1)
                    b.inc(v.tensor_copy(b.sbS[0:8, CFM_OFF + tc * 128:CFM_OFF + (tc + 1) * 128],
                                        [b.psL1, b.psL2][tc % 2][0:8, 0:128]), "v")
                for e in range(8):
                    PR = [b.psA, b.psB][e % 2]
                    v.wait_ge(b.sem["t"], c0["t"] + 16 + e + 1)
                    v.tensor_copy(crep[:, e, 0:512], PR[:, 0:512])
                    b.inc(v.tensor_copy(crep[:, e, 512:1024], PR[:, 512:1024]), "v")
        self.check(c0, t=24, a=1, v=27, g=48)
        self.barrier()

    def phase_ffn(self):
        b = self
        nc = self.nc
        XH2 = b.sbW[:].rearrange("p (c t) -> p c t", c=NCH)[:, :, 0:1024]
        y2 = b.sbX[:].bitcast(F32).rearrange("p (c t) -> p c t", c=NCH)[:, :, 0:1024]
        h = b.sbV[:, 21504:32768].rearrange("p (c t) -> p c t", c=ICH)
        crep = b.sbV[:, 12288:12288 + 8 * 1024].rearrange("p (e t) -> p e t", e=8)
        w1r = [b.sbV[:, 0:2048].rearrange("p (c n) -> p c n", c=NCH),
               b.sbV[:, 2048:4096].rearrange("p (c n) -> p c n", c=NCH)]
        w2r = [b.sbV[:, 4096:4096 + ICH * 128].rearrange("p (c n) -> p c n", c=ICH),
               b.sbV[:, 5504:5504 + ICH * 128].rearrange("p (c n) -> p c n", c=ICH)]
        hps = [b.psU1, b.psU2, b.psL1, b.psL2]
        yps = [b.psA, b.psB]
        ee = list(range(8)) + [8]

        def w1_dram(e, i):
            src = b.ws1 if e == 8 else b.w1[e]
            return src.rearrange("(c p) n -> p c n", p=128)[:, :, i * 128:(i + 1) * 128]

        def w2_dram(e, fo):
            src = b.ws2 if e == 8 else b.w2[e]
            return src.rearrange("(c p) n -> p c n", p=128)[:, :, fo * 128:(fo + 1) * 128]

        TPE = 2 * ICH + 16
        def vec_before(ei):
            return sum((ICH if ee[k] < 8 else 0) + 16 for k in range(ei))
        c0 = self.snap()

        with nc.Block() as blk:
            @blk.gpsimd
            def _(g):
                nw1 = 0
                nw2 = 0
                for ei in range(9):
                    for i in range(ICH):
                        if nw1 >= 2:
                            pei, pi = divmod(nw1 - 2, ICH)
                            g.wait_ge(b.sem["t"], c0["t"] + pei * TPE + (pi + 1) * 2)
                        b.inc(g.dma_start(w1r[nw1 % 2][:], w1_dram(ee[ei], i)), "g", 16)
                        nw1 += 1
                    for fo in range(16):
                        if nw2 >= 2:
                            pei, pfo = divmod(nw2 - 2, 16)
                            g.wait_ge(b.sem["t"], c0["t"] + pei * TPE + 2 * ICH + pfo + 1)
                        b.inc(g.dma_start(w2r[nw2 % 2][:], w2_dram(ee[ei], fo)), "g", 16)
                        nw2 += 1

            @blk.tensor
            def _(t):
                for ei in range(9):
                    e = ee[ei]
                    for i in range(ICH):
                        t.wait_ge(b.sem["g"], c0["g"] + (ei * (ICH + 16) + i + 1) * 16)
                        for th in range(2):
                            n = (ei * ICH + i) * 2 + th
                            if n >= 4:
                                t.wait_ge(b.sem["a"], c0["a"] + (n - 4) + 1)
                            for f in range(NCH):
                                ins = t.matmul(hps[n % 4][:], w1r[(ei * ICH + i) % 2][:, f, :],
                                               XH2[:, f, th * 512:(th + 1) * 512],
                                               start=(f == 0), stop=(f == NCH - 1))
                            b.inc(ins, "t")
                    for fo in range(16):
                        t.wait_ge(b.sem["g"], c0["g"] + (ei * (ICH + 16) + ICH + fo + 1) * 16)
                        t.wait_ge(b.sem["a"], c0["a"] + (ei + 1) * 2 * ICH)
                        if e < 8:
                            t.wait_ge(b.sem["v"], c0["v"] + vec_before(ei) + ICH)
                        ny = ei * 16 + fo
                        if ny >= 2:
                            pei, pfo = divmod(ny - 2, 16)
                            t.wait_ge(b.sem["v"], c0["v"] + vec_before(pei) +
                                      (ICH if ee[pei] < 8 else 0) + pfo + 1)
                        PS = yps[ny % 2]
                        for th in range(2):
                            for i in range(ICH):
                                ins = t.matmul(PS[:, th * 512:(th + 1) * 512],
                                               w2r[ny % 2][:, i, :],
                                               h[:, i, th * 512:(th + 1) * 512],
                                               start=(i == 0), stop=(i == ICH - 1))
                        b.inc(ins, "t")

            @blk.scalar
            def _(sc):
                for ei in range(9):
                    for i in range(ICH):
                        for th in range(2):
                            n = (ei * ICH + i) * 2 + th
                            sc.wait_ge(b.sem["t"], c0["t"] + ei * TPE + i * 2 + th + 1)
                            b.inc(sc.activation(h[:, i, th * 512:(th + 1) * 512],
                                                hps[n % 4][:], ACT.Silu), "a")

            @blk.vector
            def _(v):
                for ei in range(9):
                    e = ee[ei]
                    if e < 8:
                        for i in range(ICH):
                            v.wait_ge(b.sem["a"], c0["a"] + ei * 2 * ICH + (i + 1) * 2)
                            b.inc(v.tensor_tensor(h[:, i, :], h[:, i, :], crep[:, e, :], OP.mult), "v")
                    for fo in range(16):
                        v.wait_ge(b.sem["t"], c0["t"] + ei * TPE + 2 * ICH + fo + 1)
                        PS = yps[(ei * 16 + fo) % 2]
                        if ei == 0:
                            ins = v.tensor_copy(y2[:, fo, :], PS[:])
                        else:
                            ins = v.tensor_tensor(y2[:, fo, :], y2[:, fo, :], PS[:], OP.add)
                        b.inc(ins, "v")
        self.check(c0, t=9 * TPE, a=9 * 2 * ICH, v=vec_before(9), g=16 * 9 * (ICH + 16))
        self.barrier()

    def phase_final(self):
        b = self
        nc = self.nc
        y2 = b.sbX[:].bitcast(F32).rearrange("p (c t) -> p c t", c=NCH)[:, :, 0:1024]
        sbv_f32 = b.sbV[:].bitcast(F32)
        xres = [sbv_f32[:, 0:1024], sbv_f32[:, 1024:2048]]
        c0 = self.snap()
        with nc.Block() as blk:
            @blk.sync
            def _(s):
                for fo in range(16):
                    if fo >= 2:
                        s.wait_ge(b.sem["v"], c0["v"] + (fo - 2) + 1)
                        s.wait_ge(b.sem["c"], c0["c"] + (fo - 2 + 1) * 16)
                    b.inc(s.dma_start(xres[fo % 2], b.x2scr[fo * 128:(fo + 1) * 128, :]), "s", 16)

            @blk.vector
            def _(v):
                for fo in range(16):
                    v.wait_ge(b.sem["s"], c0["s"] + (fo + 1) * 16)
                    b.inc(v.tensor_tensor(xres[fo % 2], y2[:, fo, :], xres[fo % 2], OP.add), "v")

            @blk.scalar
            def _(sc):
                for fo in range(16):
                    sc.wait_ge(b.sem["v"], c0["v"] + fo + 1)
                    b.inc(sc.dma_start(b.out[fo * 128:(fo + 1) * 128, :], xres[fo % 2]), "c", 16)
                sc.wait_ge(b.sem["c"], c0["c"] + 256)
        self.check(c0, s=256, v=16, c=256)


def make_host_inputs(inputs, core):
    x = np.asarray(inputs["L0_attn_x"], np.float32)
    perm, own, other = perm_for(core)
    xt = np.ascontiguousarray(x[core // 2].T).reshape(E, 16, 128)
    xfm = np.ascontiguousarray(xt[:, perm, :].reshape(E, T)).astype(np.float32)
    mvals = [sum(1 for c in other if c <= own[qi]) for qi in range(8)]
    mrep = np.zeros((2, 128, 512), np.float32)
    for s in range(2):
        for qsub in range(4):
            mrep[s, :, qsub * 128:(qsub + 1) * 128] = mvals[4 * s + qsub]
    tri = (np.arange(128)[:, None] <= np.arange(128)[None, :]).astype(np.float32)
    onehot = np.zeros((8, 8 * 128), np.float32)
    for e in range(8):
        onehot[e, e * 128:(e + 1) * 128] = 1.0
    return dict(
        xfm=xfm,
        wq=np.ascontiguousarray(inputs["L0_attn_Wq"], np.float32),
        wk=np.ascontiguousarray(inputs["L0_attn_Wk"], np.float32),
        wv=np.ascontiguousarray(inputs["L0_attn_Wv"], np.float32),
        wo=np.ascontiguousarray(inputs["L0_attn_Wo"], np.float32),
        ws1=np.ascontiguousarray(inputs["L0_ffn_w_shared1"], np.float32),
        ws2=np.ascontiguousarray(inputs["L0_ffn_w_shared2"], np.float32),
        w1=np.ascontiguousarray(inputs["L0_ffn_w1"], np.float32),
        w2=np.ascontiguousarray(inputs["L0_ffn_w2"], np.float32),
        wrt=np.ascontiguousarray(
            np.asarray(inputs["L0_ffn_w_router"], np.float32).reshape(16, 128, 8)
            .transpose(1, 0, 2).reshape(128, 128)),
        lq=np.concatenate([np.asarray(inputs["L0_attn_lq1"], np.float32),
                           np.asarray(inputs["L0_attn_lq2"], np.float32)])[None, :],
        lk=np.concatenate([np.asarray(inputs["L0_attn_lk1"], np.float32),
                           np.asarray(inputs["L0_attn_lk2"], np.float32)])[None, :],
        cones128=np.ones((128, 1), np.float32),
        cones1x128=np.ones((1, 128), np.float32),
        cones1x128f=np.ones((1, 128), np.float32),
        crms1x128=np.full((1, 128), 1.0 - LAMBDA_INIT, np.float32),
        ctri=tri,
        conehot=onehot,
        cident=np.eye(128, dtype=np.float32),
        cmrep=mrep,
    )


def assemble_output(results):
    full = np.zeros((4, 2048, 2048), np.float32)
    for core in range(8):
        perm, own, other = perm_for(core)
        o = np.asarray(results[core]["out"])
        for j, c in enumerate(own):
            full[core // 2, c * 128:(c + 1) * 128, :] = o[:, j * 128:(j + 1) * 128].T
    return full


def kernel(**inputs):
    bld = Builder()
    nc = bld.build()
    in_maps = [make_host_inputs(inputs, core) for core in range(8)]
    res = run_bass_kernel_spmd(nc, in_maps, core_ids=list(range(8)))
    return assemble_output(res.results)


# version 06
# speedup vs baseline: 1.0470x; 1.0185x over previous
import math
import numpy as np
import concourse.bass as bass
import concourse.mybir as mybir
from concourse.alu_op_type import AluOpType as OP
from concourse.bass_utils import run_bass_kernel_spmd

F32 = mybir.dt.float32
BF16 = mybir.dt.bfloat16
ACT = mybir.ActivationFunctionType
AX = mybir.AxisListType

E = 2048
T = 2048
OT = 1024
NCH = 16
PAIRS = 16
LAMBDA_INIT = 0.8 - 0.6 * math.exp(-0.3 * 12)
SCALING = 64.0 ** -0.5
I = 1408
ICH = 11
NE = 8

OWN_EVEN = [0, 3, 4, 7, 8, 11, 12, 15]
OWN_ODD = [1, 2, 5, 6, 9, 10, 13, 14]


def perm_for(core):
    own = OWN_EVEN if core % 2 == 0 else OWN_ODD
    other = [c for c in range(16) if c not in own]
    return own + other, own, other


def s_chunks(slot):
    own = list(range(0, 4 * slot + 4))
    other = list(range(8, 8 + 4 * slot + 4))
    return own, other


class Builder:
    def __init__(self, debug=(), stop_after=None):
        self.stop_after = stop_after
        self.debug = set(debug)
        nc = bass.Bass("TRN2", target_bir_lowering=False, detect_race_conditions=False)
        self.nc = nc
        dr = lambda n, s, d, k="ExternalInput": nc.dram_tensor(n, s, d, kind=k).ap()
        self.xfm = dr("xfm", [E, T], F32)
        self.wq = dr("wq", [E, E], F32)
        self.wk = dr("wk", [E, E], F32)
        self.wv = dr("wv", [E, E], F32)
        self.wo = dr("wo", [E, E], F32)
        self.ws1 = dr("ws1", [E, I], F32)
        self.ws2 = dr("ws2", [I, E], F32)
        self.w1 = dr("w1", [NE, E, I], F32)
        self.w2 = dr("w2", [NE, I, E], F32)
        self.wrt = dr("wrt", [128, 128], F32)
        self.lq = dr("lq", [1, 128], F32)
        self.lk = dr("lk", [1, 128], F32)
        self.cones128 = dr("cones128", [128, 1], F32)
        self.cones1x128 = dr("cones1x128", [1, 128], F32)
        self.cones1x128f = dr("cones1x128f", [1, 128], F32)
        self.crms1x128 = dr("crms1x128", [1, 128], F32)
        self.ctri = dr("ctri", [128, 128], F32)
        self.conehot = dr("conehot", [8, 8 * 128], F32)
        self.cident = dr("cident", [128, 128], F32)
        self.cmrep = dr("cmrep", [2, 128, 512], F32)
        self.out = dr("out", [E, OT], F32, "ExternalOutput")
        self.x2scr = dr("x2scr", [E, OT], F32, "Internal")

        self.sbX = nc.alloc_sbuf_tensor("sbX", [128, NCH * T], BF16)
        self.sbV = nc.alloc_sbuf_tensor("sbV", [128, NCH * T], BF16)
        self.sbW = nc.alloc_sbuf_tensor("sbW", [128, NCH * T], BF16)
        self.sbS = nc.alloc_sbuf_tensor("sbS", [128, 6272], BF16)
        self.sbF = nc.alloc_sbuf_tensor("sbF", [128, 64], F32)
        self.ones128 = nc.alloc_sbuf_tensor("ones128", [128, 1], BF16)
        self.ones1x128 = nc.alloc_sbuf_tensor("ones1x128", [1, 128], BF16)
        self.ones1x128f = nc.alloc_sbuf_tensor("ones1x128f", [1, 128], F32)
        self.rms1x128 = nc.alloc_sbuf_tensor("rms1x128", [1, 128], BF16)

        self.psA = nc.alloc_psum_tensor("psA", [128, 1024], F32)
        self.psB = nc.alloc_psum_tensor("psB", [128, 1024], F32)
        self.psU1 = nc.alloc_psum_tensor("psU1", [128, 512], F32)
        self.psU2 = nc.alloc_psum_tensor("psU2", [128, 512], F32)
        self.psL1 = nc.alloc_psum_tensor("psL1", [128, 512], F32)
        self.psL2 = nc.alloc_psum_tensor("psL2", [128, 512], F32)

        self.sem = {}
        self.cnt = {}
        for name in ("g", "s", "c", "t", "v", "a"):
            self.sem[name] = nc.alloc_semaphore("sem_" + name)
            self.cnt[name] = 0
        self.taps = {}

    def inc(self, instr, sem, n=1):
        instr.then_inc(self.sem[sem], n)
        self.cnt[sem] += n
        return self.cnt[sem]

    def snap(self):
        return dict(self.cnt)

    def check(self, c0, **expected):
        for k, d in expected.items():
            got = self.cnt[k] - c0[k]
            assert got == d, f"emission count mismatch sem {k}: got {got} expected {d}"

    def barrier(self, fence=True):
        b = self
        if fence:
            with self.nc.Block() as blk:
                @blk.sync
                def _(s):
                    s.wait_ge(b.sem["s"], b.cnt["s"])
                    s.wait_ge(b.sem["c"], b.cnt["c"])
                @blk.gpsimd
                def _(g):
                    g.wait_ge(b.sem["g"], b.cnt["g"])
        self.nc.all_engine_barrier()

    def want(self, name):
        return name in self.debug

    def tap(self, name, shape, dtype=F32):
        ap = self.nc.dram_tensor("tap_" + name, shape, dtype, kind="ExternalOutput").ap()
        self.taps[name] = ap
        return ap

    def store_tap(self, name, sb_ap, shape, dtype=F32):
        b = self
        nc = self.nc
        ap = self.tap(name, shape, dtype)
        with nc.Block() as blk:
            @blk.sync
            def _(s):
                n = b.inc(s.dma_start(ap, sb_ap), "s", 16)
                s.wait_ge(b.sem["s"], n)
        self.barrier()

    def build(self):
        steps = [
            ("consts", self.phase_consts, None),
            ("ln1", self.phase_ln1, ("xhat", lambda: self.store_tap("xhat", self.sbX[:], [128, NCH * T], BF16))),
            ("vproj", self.phase_vproj, ("V", lambda: self.store_tap("V", self.sbV[:], [128, NCH * T], BF16))),
            ("kproj", self.phase_kproj, ("K", lambda: self.store_tap("K", self.sbW[:], [128, NCH * T], BF16))),
            ("qproj", self.phase_qproj, ("Q", lambda: self.store_tap("Q", self.sbX[:], [128, NCH * T], BF16))),
            ("bounds", self.phase_bounds, ("bias", lambda: self.store_tap("bias", self.sbF[:, 0:64], [128, 64], F32))),
            ("attn", self.phase_attn, ("O", lambda: self.store_tap("O", self.sbX[:], [128, NCH * T], BF16))),
            ("outproj", self.phase_outproj, ("x2", lambda: self.store_tap("x2", self.x2scr, [E, OT], F32))),
            ("ln2", self.phase_ln2, ("xhat2", lambda: (self.store_tap("xhat2",
                self.sbW[:].rearrange("p (c t) -> p c t", c=NCH)[:, :, 0:1024], [128, NCH, OT], BF16),
                self.store_tap("xlo2",
                self.sbW[:].rearrange("p (c t) -> p c t", c=NCH)[:, :, 1024:2048], [128, NCH, OT], BF16)))),
            ("router", self.phase_router, ("comb", lambda: (self.store_tap("comb", self.sbS[0:8, 2432:2432 + 1024], [8, 1024], BF16),
                                                             self.store_tap("zrouter", self.sbS[:, 4608:4736], [128, 128], BF16)))),
            ("ffn", self.phase_ffn, None),
            ("final", self.phase_final, None),
        ]
        for name, fn, tapinfo in steps:
            fn()
            if tapinfo is not None and self.want(tapinfo[0]):
                tapinfo[1]()
            if self.stop_after == name:
                break
        return self.nc

    def phase_consts(self):
        b = self
        nc = self.nc
        with nc.Block() as blk:
            @blk.gpsimd
            def _(g):
                b.g_x = []
                for xs in range(4):
                    b.inc(g.dma_start(b.sbX[:].rearrange("p (c t) -> p c t", c=NCH)[:, xs * 4:(xs + 1) * 4, :],
                                      b.xfm.rearrange("(c p) t -> p c t", p=128)[:, xs * 4:(xs + 1) * 4, :]),
                          "g", 16)
                    b.g_x.append(b.cnt["g"])
                b.inc(g.dma_start(b.ones128[:], b.cones128), "g", 16)
                b.inc(g.dma_start(b.ones1x128[:], b.cones1x128), "g", 16)
                b.inc(g.dma_start(b.ones1x128f[:], b.cones1x128f), "g", 16)
                b.inc(g.dma_start(b.rms1x128[:], b.crms1x128), "g", 16)
                b.g_ln1 = b.cnt["g"]
                b.g_wv = []
                for dvg in range(4):
                    b.inc(g.dma_start(b.sbW[:].rearrange("p (c n) -> p c n", c=NCH)[:, :, dvg * 512:(dvg + 1) * 512],
                                      b.wv.rearrange("(c p) n -> p c n", p=128)[:, :, dvg * 512:(dvg + 1) * 512]),
                          "g", 16)
                    b.g_wv.append(b.cnt["g"])

            @blk.sync
            def _(s):
                lqk = b.sbS[0:1, 0:512].bitcast(F32)
                b.inc(s.dma_start(lqk[:, 0:128], b.lq), "s", 16)
                b.inc(s.dma_start(lqk[:, 128:256], b.lk), "s", 16)

            @blk.vector
            def _(v):
                v.wait_ge(b.sem["s"], b.cnt["s"])
                v.memset(b.sbF[0:1, 5:6], 1e-5)
                lqk = b.sbS[0:1, 0:512].bitcast(F32)
                v.tensor_tensor(lqk[:, 0:128], lqk[:, 0:128], lqk[:, 128:256], OP.mult)
                v.reduce_sum(b.sbF[0:1, 0:1], lqk[:, 0:64], axis=AX.X)
                b.inc(v.reduce_sum(b.sbF[0:1, 1:2], lqk[:, 64:128], axis=AX.X), "v")

            @blk.vector
            def _(v):
                b.inc(v.tensor_scalar(b.sbS[0:1, 600:602], b.sbF[0:1, 0:2], 1.0, None, OP.mult), "v")

            @blk.tensor
            def _(t):
                t.wait_ge(b.sem["v"], b.cnt["v"])
                t.wait_ge(b.sem["g"], b.cnt["g"])
                b.inc(t.matmul(b.psU1[:, 0:2], b.ones1x128[:], b.sbS[0:1, 600:602], start=True, stop=True), "t")

            @blk.vector
            def _(v):
                v.wait_ge(b.sem["t"], b.cnt["t"])
                b.inc(v.tensor_copy(b.sbF[:, 2:4], b.psU1[:, 0:2]), "v")

            @blk.scalar
            def _(sc):
                sc.wait_ge(b.sem["v"], b.cnt["v"])
                b.inc(sc.activation(b.sbF[:, 2:4], b.sbF[:, 2:4], ACT.Exp), "a")

            @blk.vector
            def _(v):
                v.wait_ge(b.sem["a"], b.cnt["a"])
                b.inc(v.tensor_scalar(b.sbF[:, 4:5], b.sbF[:, 2:3], b.sbF[:, 3:4],
                                      float(LAMBDA_INIT), OP.subtract, OP.add), "v")
        self.barrier(fence=False)

    def phase_ln1(self):
        b = self
        nc = self.nc
        X = b.sbX[:].rearrange("p (c t) -> p c t", c=NCH)
        sx = [b.psA[0:1, 0:512], b.psA[0:1, 512:1024], b.psB[0:1, 0:512], b.psB[0:1, 512:1024]]
        sxx = [b.psU1[0:1, :], b.psU2[0:1, :], b.psL1[0:1, :], b.psL2[0:1, :]]
        TMP = [b.sbS[:, 0:2048], b.sbS[:, 2048:4096]]
        c0 = self.snap()
        with nc.Block() as blk:
            @blk.vector
            def _(v):
                for c in range(NCH):
                    if c % 4 == 0:
                        v.wait_ge(b.sem["g"], b.g_x[c // 4])
                    if c >= 2:
                        v.wait_ge(b.sem["t"], c0["t"] + (c - 2) + 1)
                    b.inc(v.tensor_tensor(TMP[c % 2], X[:, c, :], X[:, c, :], OP.mult), "v")

            @blk.tensor
            def _(t):
                t.wait_ge(b.sem["g"], b.g_ln1)
                for c in range(NCH):
                    for j in range(4):
                        t.matmul(sx[j], b.ones128[:], X[:, c, 512 * j:512 * (j + 1)],
                                 start=(c == 0), stop=(c == NCH - 1))
                    t.wait_ge(b.sem["v"], c0["v"] + c + 1)
                    for j in range(4):
                        ins = t.matmul(sxx[j], b.ones128[:], TMP[c % 2][:, 512 * j:512 * (j + 1)],
                                       start=(c == 0), stop=(c == NCH - 1))
                    b.inc(ins, "t")

            @blk.vector
            def _(v):
                v.wait_ge(b.sem["t"], c0["t"] + NCH)
                m = b.sbS[0:1, 4096:6144]
                var = b.sbS[0:1, 0:2048]
                for j in range(4):
                    v.tensor_scalar(m[:, 512 * j:512 * (j + 1)], sx[j], 1.0 / E, None, OP.mult)
                    v.tensor_scalar(var[:, 512 * j:512 * (j + 1)], sxx[j], 1.0 / E, None, OP.mult)
                msq = b.sbS[0:1, 2048:4096]
                v.tensor_tensor(msq, m, m, OP.mult)
                b.inc(v.tensor_tensor(var, var, msq, OP.subtract), "v")

            @blk.scalar
            def _(sc):
                var = b.sbS[0:1, 0:2048]
                sc.wait_ge(b.sem["v"], c0["v"] + NCH + 1)
                sc.activation(var, var, ACT.Ln, bias=b.sbF[0:1, 5:6], scale=1.0)
                b.inc(sc.activation(var, var, ACT.Exp, scale=-0.5), "a")

            @blk.tensor
            def _(t):
                t.wait_ge(b.sem["a"], c0["a"] + 1)
                m = b.sbS[0:1, 4096:6144]
                r = b.sbS[0:1, 0:2048]
                mout = [b.psA[:, 0:512], b.psA[:, 512:1024], b.psB[:, 0:512], b.psB[:, 512:1024]]
                rout = [b.psU1[:], b.psU2[:], b.psL1[:], b.psL2[:]]
                for j in range(4):
                    t.matmul(mout[j], b.ones1x128[:], m[:, 512 * j:512 * (j + 1)], start=True, stop=True)
                for j in range(4):
                    ins = t.matmul(rout[j], b.ones1x128[:], r[:, 512 * j:512 * (j + 1)], start=True, stop=True)
                b.inc(ins, "t")

            @blk.vector
            def _(v):
                v.wait_ge(b.sem["t"], c0["t"] + NCH + 1)
                mb = b.sbS[:, 4096:6144]
                rb = b.sbS[:, 2048:4096]
                v.tensor_copy(mb[:, 0:512], b.psA[:, 0:512])
                v.tensor_copy(mb[:, 512:1024], b.psA[:, 512:1024])
                v.tensor_copy(mb[:, 1024:1536], b.psB[:, 0:512])
                v.tensor_copy(mb[:, 1536:2048], b.psB[:, 512:1024])
                v.tensor_copy(rb[:, 0:512], b.psU1[:])
                v.tensor_copy(rb[:, 512:1024], b.psU2[:])
                v.tensor_copy(rb[:, 1024:1536], b.psL1[:])
                v.tensor_copy(rb[:, 1536:2048], b.psL2[:])
                b.v_ln1apply = []
                for c in range(NCH):
                    v.tensor_tensor(X[:, c, :], X[:, c, :], mb, OP.subtract)
                    b.inc(v.tensor_tensor(X[:, c, :], X[:, c, :], rb, OP.mult), "v")
                    b.v_ln1apply.append(b.cnt["v"])
        self.check(c0, v=NCH + 1 + NCH, t=NCH + 1, a=1)
        if self.want("xhat"):
            self.barrier()

    def phase_vproj(self):
        b = self
        nc = self.nc
        X = b.sbX[:].rearrange("p (c t) -> p c t", c=NCH)
        W = b.sbW[:].rearrange("p (c n) -> p c n", c=NCH)
        V = b.sbV[:].rearrange("p (c n) -> p c n", c=NCH)
        ps = [b.psU1, b.psU2, b.psL1, b.psL2]
        c0 = self.snap()
        with nc.Block() as blk:
            @blk.tensor
            def _(t):
                for n in range(64):
                    dvg, kc = divmod(n, 16)
                    if kc == 0:
                        t.wait_ge(b.sem["g"], b.g_wv[dvg])
                    if n >= 4:
                        t.wait_ge(b.sem["v"], c0["v"] + (n - 4) + 1)
                    for f in range(NCH):
                        if n == 0:
                            t.wait_ge(b.sem["v"], b.v_ln1apply[f])
                        ins = t.matmul(ps[n % 4][:], X[:, f, kc * 128:(kc + 1) * 128],
                                       W[:, f, dvg * 512:(dvg + 1) * 512],
                                       start=(f == 0), stop=(f == NCH - 1))
                    b.inc(ins, "t")

            @blk.vector
            def _(v):
                for n in range(64):
                    dvg, kc = divmod(n, 16)
                    v.wait_ge(b.sem["t"], c0["t"] + n + 1)
                    b.inc(v.tensor_copy(V[:, kc, dvg * 512:(dvg + 1) * 512], ps[n % 4][:]), "v")
        self.check(c0, t=64, v=64)
        self.barrier()

    def _proj_stream(self, wdram, nout, ngrp, out_ap_fn, scale=None):
        b = self
        nc = self.nc
        X = b.sbX[:].rearrange("p (c t) -> p c t", c=NCH)
        ring = [b.sbS[:, 0:2048].rearrange("p (c n) -> p c n", c=NCH),
                b.sbS[:, 2048:4096].rearrange("p (c n) -> p c n", c=NCH)]
        ps = [b.psU1, b.psU2, b.psL1, b.psL2]
        c0 = self.snap()
        with nc.Block() as blk:
            @blk.gpsimd
            def _(g):
                for d in range(nout):
                    if d >= 2:
                        g.wait_ge(b.sem["t"], c0["t"] + (d - 1) * ngrp)
                    b.inc(g.dma_start(ring[d % 2][:], wdram(d)), "g", 16)

            @blk.tensor
            def _(t):
                for d in range(nout):
                    t.wait_ge(b.sem["g"], c0["g"] + (d + 1) * 16)
                    for tg in range(ngrp):
                        n = d * ngrp + tg
                        if n >= 4:
                            t.wait_ge(b.sem["v"], c0["v"] + (n - 4) + 1)
                        for f in range(NCH):
                            ins = t.matmul(ps[n % 4][:], ring[d % 2][:, f, :],
                                           X[:, f, tg * 512:(tg + 1) * 512],
                                           start=(f == 0), stop=(f == NCH - 1))
                        b.inc(ins, "t")

            @blk.vector
            def _(v):
                for d in range(nout):
                    for tg in range(ngrp):
                        n = d * ngrp + tg
                        v.wait_ge(b.sem["t"], c0["t"] + n + 1)
                        if scale is None:
                            ins = v.tensor_copy(out_ap_fn(d, tg), ps[n % 4][:])
                        else:
                            ins = v.tensor_scalar(out_ap_fn(d, tg), ps[n % 4][:], scale, None, OP.mult)
                        b.inc(ins, "v")
        self.check(c0, t=nout * ngrp, v=nout * ngrp, g=16 * nout)
        self.barrier()

    def phase_kproj(self):
        b = self
        K = b.sbW[:].rearrange("p (c t) -> p c t", c=NCH)
        self._proj_stream(
            lambda d: b.wk.rearrange("(c p) n -> p c n", p=128)[:, :, d * 128:(d + 1) * 128],
            16, 4, lambda d, tg: K[:, d, tg * 512:(tg + 1) * 512])

    def phase_qproj(self):
        b = self
        X = b.sbX[:].rearrange("p (c t) -> p c t", c=NCH)
        self._proj_stream(
            lambda d: b.wq.rearrange("(c p) n -> p c n", p=128)[:, :, d * 128:(d + 1) * 128],
            16, 2, lambda d, tg: X[:, d, 1024 + tg * 512:1024 + (tg + 1) * 512],
            scale=float(SCALING))

    def phase_bounds(self):
        b = self
        nc = self.nc
        X = b.sbX[:].rearrange("p (c t) -> p c t", c=NCH)
        K = b.sbW[:].rearrange("p (c t) -> p c t", c=NCH)
        Q1 = b.sbS[:, 0:1024]
        K1 = b.sbS[:, 1024:3072]
        c0 = self.snap()
        with nc.Block() as blk:
            @blk.gpsimd
            def _(g):
                b.inc(g.dma_start(b.sbS[:, 5120:5248], b.ctri), "g", 16)
                b.inc(g.dma_start(b.sbS[:, 5248:6272].rearrange("p (s n) -> p s n", s=2),
                                  b.cmrep.rearrange("s p n -> p s n")), "g", 16)

            @blk.vector
            def _(v):
                for pair in range(PAIRS):
                    if pair >= 1:
                        v.wait_ge(b.sem["t"], c0["t"] + pair)
                    v.tensor_tensor(Q1, X[:, pair, 1024:2048], X[:, pair, 1024:2048], OP.mult)
                    v.tensor_tensor(K1[:, 0:1024], K[:, pair, 0:1024], K[:, pair, 0:1024], OP.mult)
                    b.inc(v.tensor_tensor(K1[:, 1024:2048], K[:, pair, 1024:2048], K[:, pair, 1024:2048], OP.mult), "v")
                    v.wait_ge(b.sem["t"], c0["t"] + pair + 1)
                    SF = b.sbS[0:1, 3072:5120].bitcast(F32)
                    v.reduce_max(SF[:, 0 + pair:1 + pair], b.psB[0:1, 0:1024], axis=AX.X)
                    v.reduce_max(SF[:, 16 + pair:17 + pair], b.psU1[0:1, :], axis=AX.X)
                    v.reduce_max(SF[:, 32 + pair:33 + pair], b.psU2[0:1, :], axis=AX.X)
                    b.inc(v.reduce_max(SF[:, 48 + pair:49 + pair], b.psA[0:1, 0:1024], axis=AX.X), "v")

            @blk.tensor
            def _(t):
                for pair in range(PAIRS):
                    t.wait_ge(b.sem["v"], c0["v"] + pair * 2 + 1)
                    t.matmul(b.psA[0:1, 0:512], b.ones128[:], Q1[:, 0:512], start=True, stop=True)
                    t.matmul(b.psA[0:1, 512:1024], b.ones128[:], Q1[:, 512:1024], start=True, stop=True)
                    t.matmul(b.psB[0:1, 0:512], b.ones128[:], K1[:, 0:512], start=True, stop=True)
                    t.matmul(b.psB[0:1, 512:1024], b.ones128[:], K1[:, 512:1024], start=True, stop=True)
                    t.matmul(b.psU1[0:1, :], b.ones128[:], K1[:, 1024:1536], start=True, stop=True)
                    ins = t.matmul(b.psU2[0:1, :], b.ones128[:], K1[:, 1536:2048], start=True, stop=True)
                    b.inc(ins, "t")

            @blk.vector
            def _(v):
                SF = b.sbS[0:1, 3072:5120].bitcast(F32)
                v.wait_ge(b.sem["v"], c0["v"] + 2 * PAIRS)
                v.tensor_tensor(SF[:, 0:16], SF[:, 0:16], SF[:, 16:32], OP.max)
                v.tensor_tensor(SF[:, 0:16], SF[:, 0:16], SF[:, 32:48], OP.max)
                b.inc(v.tensor_tensor(b.sbF[0:1, 32:48], SF[:, 0:16], SF[:, 48:64], OP.mult), "v")
        self.check(c0, v=2 * PAIRS + 1, t=PAIRS, g=32)
        self.barrier()
        c1 = self.snap()
        with nc.Block() as blk:
            @blk.scalar
            def _(sc):
                sc.activation(b.sbF[0:1, 32:48], b.sbF[0:1, 32:48], ACT.Ln)
                b.inc(sc.activation(b.sbF[0:1, 32:48], b.sbF[0:1, 32:48], ACT.Exp, scale=0.5), "a")

            @blk.vector
            def _(v):
                v.wait_ge(b.sem["a"], c1["a"] + 1)
                b.inc(v.tensor_scalar(b.sbS[0:1, 4480:4496], b.sbF[0:1, 32:48], -1.0, None, OP.mult), "v")

            @blk.tensor
            def _(t):
                t.wait_ge(b.sem["v"], c1["v"] + 1)
                b.inc(t.matmul(b.psL1[:, 0:16], b.ones1x128[:], b.sbS[0:1, 4480:4496], start=True, stop=True), "t")

            @blk.vector
            def _(v):
                v.wait_ge(b.sem["t"], c1["t"] + 1)
                b.inc(v.tensor_copy(b.sbF[:, 8:24], b.psL1[:, 0:16]), "v")
        self.barrier()

    def phase_attn(self):
        b = self
        with self.nc.Block() as blk:
            for pair in range(PAIRS):
                for slot in range(2):
                    self.attn_iter(blk, pair, slot)
        self.barrier()

    def attn_iter(self, blk, pair, slot):
        b = self
        nc = self.nc
        dbgP = (pair == 0 and slot == 0 and self.want("P00"))
        own, other = s_chunks(slot)
        chunks = own + other
        n_ch = len(chunks)
        q0 = slot * 512
        X = b.sbX[:].rearrange("p (c t) -> p c t", c=NCH)
        Khi = b.sbW[64:128, :].rearrange("p (c t) -> p c t", c=NCH)
        Klo = b.sbW[0:64, :].rearrange("p (c t) -> p c t", c=NCH)
        V = b.sbV[:].rearrange("p (c n) -> p c n", c=NCH)
        Qlo = b.sbX[0:64, :].rearrange("p (c t) -> p c t", c=NCH)
        Qhi = b.sbX[64:128, :].rearrange("p (c t) -> p c t", c=NCH)
        P12 = [b.sbS[:, 0:1024], b.sbS[:, 1024:2048], b.sbS[:, 3072:4096]]
        RB1 = b.sbS[:, 2048:2560]
        RB2 = b.sbS[:, 2560:3072]
        OSQ = b.sbS[:, 0:512]
        REC = b.sbS[0:1, 4096:5120]
        RSTD = b.sbS[0:1, 2048:2560]
        TRI = b.sbS[:, 5120:5248]
        BIAS = b.sbF[:, 8:24]
        lam_ap = b.sbF[0:1, 4:5]
        SB = [b.psA, b.psB]
        Oap = X[:, pair, q0:q0 + 512]

        seq = [("S", 0), ("S", 1)]
        for ci in range(n_ch):
            if ci + 2 < n_ch:
                seq.append(("S", ci + 2))
            seq.append(("L", ci))
        t_pos = {}
        for idx, item in enumerate(seq):
            t_pos[item] = idx + 1
        n_t_loop = len(seq)
        c0 = self.snap()
        if dbgP:
            tapP = self.tap("P00", [8, 128, 1024], BF16)
            tapL = self.tap("L00", [1, 1024], F32)
            tapRB = self.tap("RB00", [128, 1024], BF16)

        if True:
            if dbgP:
                @blk.sync
                def _(s):
                    for ci in range(n_ch):
                        s.wait_ge(b.sem["v"], c0["v"] + ci + 1)
                        b.inc(s.dma_start(tapP[ci], P12[ci % 2]), "s", 16)

            @blk.tensor
            def _(t):
                for item in seq:
                    kind, ci = item
                    kc = chunks[ci]
                    if kind == "S" and dbgP:
                        if ci >= 2:
                            t.wait_ge(b.sem["s"], c0["s"] + (ci - 2 + 1) * 16)
                    if kind == "S":
                        S = SB[ci % 2]
                        if ci >= 2:
                            t.wait_ge(b.sem["a"], c0["a"] + (ci - 2) + 1)
                        t.matmul(S[:, 0:512], Klo[:, pair, kc * 128:(kc + 1) * 128],
                                 Qlo[:, pair, 1024 + q0:1024 + q0 + 512],
                                 start=True, stop=True, tile_position=(0, 0))
                        ins = t.matmul(S[:, 512:1024], Khi[:, pair, kc * 128:(kc + 1) * 128],
                                       Qhi[:, pair, 1024 + q0:1024 + q0 + 512],
                                       start=True, stop=True, tile_position=(64, 0))
                        b.inc(ins, "t")
                    else:
                        t.wait_ge(b.sem["v"], c0["v"] + ci + 1)
                        P = P12[ci % 3]
                        first = ci == 0
                        last = ci == n_ch - 1
                        t.matmul(b.psL1[0:1, :], b.ones128[:], P[:, 0:512], start=first, stop=last)
                        t.matmul(b.psL2[0:1, :], b.ones128[:], P[:, 512:1024], start=first, stop=last)
                        t.matmul(b.psU1[:], V[:, kc, pair * 128:(pair + 1) * 128], P[:, 0:512],
                                 start=first, stop=last)
                        ins = t.matmul(b.psU2[:], V[:, kc, pair * 128:(pair + 1) * 128], P[:, 512:1024],
                                       start=first, stop=last)
                        b.inc(ins, "t")
                if dbgP:
                    t.wait_ge(b.sem["s"], c0["s"] + n_ch * 16)
                t.wait_ge(b.sem["a"], c0["a"] + n_ch + 1)
                t.matmul(b.psL1[:], b.ones1x128[:], REC[:, 0:512], start=True, stop=True)
                b.inc(t.matmul(b.psL2[:], b.ones1x128[:], REC[:, 512:1024], start=True, stop=True), "t")
                t.wait_ge(b.sem["v"], c0["v"] + n_ch + 3)
                b.inc(t.matmul(b.psU1[0:1, :], b.ones128[:], OSQ, start=True, stop=True), "t")
                t.wait_ge(b.sem["a"], c0["a"] + n_ch + 2)
                b.inc(t.matmul(b.psU2[:], b.rms1x128[:], RSTD[:, 0:512], start=True, stop=True), "t")

            @blk.scalar
            def _(sc):
                for ci in range(n_ch):
                    sc.wait_ge(b.sem["t"], c0["t"] + t_pos[("S", ci)])
                    if ci >= 3:
                        sc.wait_ge(b.sem["t"], c0["t"] + t_pos[("L", ci - 3)])
                    b.inc(sc.activation(P12[ci % 3], SB[ci % 2][:], ACT.Exp,
                                        bias=BIAS[:, pair:pair + 1], scale=1.0), "a")
                sc.wait_ge(b.sem["t"], c0["t"] + n_t_loop)
                sc.activation(b.psA[0:1, 0:512], b.psL1[0:1, :], ACT.Ln)
                sc.activation(b.psA[0:1, 512:1024], b.psL2[0:1, :], ACT.Ln)
                b.inc(sc.activation(REC, b.psA[0:1, 0:1024], ACT.Exp, scale=-1.0), "a")
                sc.wait_ge(b.sem["t"], c0["t"] + n_t_loop + 2)
                sc.activation(RSTD[:, 0:512], b.psU1[0:1, :], ACT.Ln, bias=b.sbF[0:1, 5:6], scale=1.0 / 128.0)
                b.inc(sc.activation(RSTD[:, 0:512], RSTD[:, 0:512], ACT.Exp, scale=-0.5), "a")

            @blk.vector
            def _(v):
                for ci in range(n_ch):
                    kc = chunks[ci]
                    P = P12[ci % 3]
                    v.wait_ge(b.sem["a"], c0["a"] + ci + 1)
                    if ci < len(own):
                        jq = kc - 4 * slot
                        if 0 <= jq <= 3:
                            for half in range(2):
                                base = half * 512
                                if jq > 0:
                                    v.memset(P[:, base:base + jq * 128], 0.0)
                                v.tensor_tensor(P[:, base + jq * 128:base + (jq + 1) * 128],
                                                P[:, base + jq * 128:base + (jq + 1) * 128],
                                                TRI, OP.mult)
                    else:
                        p = kc - 8
                        if not (slot == 1 and p < 4):
                            for half in range(2):
                                base = half * 512
                                v.scalar_tensor_tensor(P[:, base:base + 512], b.sbS[:, 5248 + slot * 512:5248 + (slot + 1) * 512],
                                                       float(p) + 0.5, P[:, base:base + 512],
                                                       OP.is_gt, OP.mult)
                    b.inc(v.tensor_copy(b.sbF[0:1, 47:48], b.sbF[0:1, 47:48]), "v")
                v.wait_ge(b.sem["t"], c0["t"] + n_t_loop + 1)
                v.tensor_copy(RB1, b.psL1[:])
                b.inc(v.tensor_scalar(RB2, b.psL2[:], b.sbF[:, 4:5], 1.0, OP.mult, OP.mult), "v")
                T1 = b.sbS[:, 0:512]
                T2 = b.sbS[:, 512:1024]
                v.tensor_tensor(T1, b.psU1[:], RB1, OP.mult)
                v.tensor_tensor(T2, b.psU2[:], RB2, OP.mult)
                b.inc(v.tensor_tensor(Oap, T1, T2, OP.subtract), "v")
                b.inc(v.tensor_tensor(OSQ, Oap, Oap, OP.mult), "v")
                v.wait_ge(b.sem["t"], c0["t"] + n_t_loop + 3)
                b.inc(v.tensor_tensor(Oap, Oap, b.psU2[:], OP.mult), "v")
        self.check(c0, t=n_t_loop + 3, a=n_ch + 2, v=n_ch + 4)

    def phase_outproj(self):
        b = self
        nc = self.nc
        X = b.sbX[:].rearrange("p (c t) -> p c t", c=NCH)
        ring = [b.sbS[:, 0:2048].rearrange("p (c n) -> p c n", c=NCH),
                b.sbS[:, 2048:4096].rearrange("p (c n) -> p c n", c=NCH)]
        sbv_f32 = b.sbV[:].bitcast(F32)
        xres = [sbv_f32[:, 0:1024], sbv_f32[:, 1024:2048]]
        x2strip = [sbv_f32[:, 2048:3072], sbv_f32[:, 3072:4096]]
        x2b = [b.sbV[:, 8192:9216], b.sbV[:, 9216:10240]]
        x2sq = [b.sbV[:, 10240:11264], b.sbV[:, 11264:12288]]
        sx = [b.psU1[0:1, :], b.psU2[0:1, :]]
        sxx = [b.psL1[0:1, :], b.psL2[0:1, :]]
        c0 = self.snap()
        with nc.Block() as blk:
            def y_done(j):
                return 2 if j == 0 else (4 if j == 1 else 3 * j)

            def st_done(j):
                return 3 * j + 7 if j <= 13 else (47 if j == 14 else 48)

            @blk.gpsimd
            def _(g):
                for fo in range(16):
                    if fo >= 2:
                        g.wait_ge(b.sem["t"], c0["t"] + y_done(fo - 2))
                    b.inc(g.dma_start(ring[fo % 2][:],
                                      b.wo.rearrange("(c p) n -> p c n", p=128)[:, :, fo * 128:(fo + 1) * 128]),
                          "g", 16)

            @blk.sync
            def _(s):
                for fo in range(16):
                    if fo >= 2:
                        s.wait_ge(b.sem["v"], c0["v"] + (fo - 2) * 3 + 1)
                    b.inc(s.dma_start(xres[fo % 2], b.xfm[fo * 128:(fo + 1) * 128, 0:1024]), "s", 16)

            def stats_group(t, j):
                t.wait_ge(b.sem["v"], c0["v"] + j * 3 + 3)
                t.matmul(sx[0], b.ones128[:], x2b[j % 2][:, 0:512], start=(j == 0), stop=(j == 15))
                t.matmul(sx[1], b.ones128[:], x2b[j % 2][:, 512:1024], start=(j == 0), stop=(j == 15))
                t.matmul(sxx[0], b.ones128[:], x2sq[j % 2][:, 0:512], start=(j == 0), stop=(j == 15))
                ins = t.matmul(sxx[1], b.ones128[:], x2sq[j % 2][:, 512:1024], start=(j == 0), stop=(j == 15))
                b.inc(ins, "t")

            @blk.tensor
            def _(t):
                for fo in range(16):
                    t.wait_ge(b.sem["g"], c0["g"] + (fo + 1) * 16)
                    if fo >= 2:
                        t.wait_ge(b.sem["v"], c0["v"] + (fo - 2) * 3 + 1)
                    PS = [b.psA, b.psB][fo % 2]
                    for th in range(2):
                        for pair in range(PAIRS):
                            ins = t.matmul(PS[:, th * 512:(th + 1) * 512], ring[fo % 2][:, pair, :],
                                           X[:, pair, th * 512:(th + 1) * 512],
                                           start=(pair == 0), stop=(pair == PAIRS - 1))
                        b.inc(ins, "t")
                    if fo >= 2:
                        stats_group(t, fo - 2)
                stats_group(t, 14)
                stats_group(t, 15)

            @blk.vector
            def _(v):
                for fo in range(16):
                    PS = [b.psA, b.psB][fo % 2]
                    v.wait_ge(b.sem["t"], c0["t"] + y_done(fo))
                    v.wait_ge(b.sem["s"], c0["s"] + (fo + 1) * 16)
                    if fo >= 2:
                        v.wait_ge(b.sem["c"], c0["c"] + (fo - 2 + 1) * 16)
                        v.wait_ge(b.sem["t"], c0["t"] + st_done(fo - 2))
                    v.tensor_tensor(x2strip[fo % 2], PS[:], xres[fo % 2], OP.add)
                    b.inc(v.tensor_copy(x2b[fo % 2], x2strip[fo % 2]), "v")
                    b.inc(v.tensor_copy(b.sbF[0:1, 47:48], b.sbF[0:1, 47:48]), "v")
                    b.inc(v.tensor_tensor(x2sq[fo % 2], x2b[fo % 2], x2b[fo % 2], OP.mult), "v")

            @blk.scalar
            def _(sc):
                for fo in range(16):
                    sc.wait_ge(b.sem["v"], c0["v"] + fo * 3 + 2)
                    b.inc(sc.dma_start(b.x2scr[fo * 128:(fo + 1) * 128, :], x2strip[fo % 2]), "c", 16)
                sc.wait_ge(b.sem["c"], c0["c"] + 256)
        self.check(c0, t=48, v=48, c=256, g=256, s=256)
        self.barrier()

    def phase_ln2(self):
        b = self
        nc = self.nc
        XH2 = b.sbW[:].rearrange("p (c t) -> p c t", c=NCH)[:, :, 0:1024]
        XL2 = b.sbW[:].rearrange("p (c t) -> p c t", c=NCH)[:, :, 1024:2048]
        sx = [b.psU1[0:1, :], b.psU2[0:1, :]]
        sxx = [b.psL1[0:1, :], b.psL2[0:1, :]]
        m = b.sbS[0:1, 4096:5120]
        var = b.sbS[0:1, 5120:6144]
        sbv_f32 = b.sbV[:].bitcast(F32)
        strips = [sbv_f32[:, 0:1024], sbv_f32[:, 1024:2048], sbv_f32[:, 2048:3072]]
        mbf = sbv_f32[:, 3072:4096]
        rbf = sbv_f32[:, 4096:5120]
        hifs = [b.sbS[:, 0:2048].bitcast(F32), b.sbS[:, 2048:4096].bitcast(F32)]
        c0 = self.snap()
        with nc.Block() as blk:
            @blk.vector
            def _(v):
                for j in range(2):
                    v.tensor_scalar(m[:, 512 * j:512 * (j + 1)], sx[j], 1.0 / E, None, OP.mult)
                    v.tensor_scalar(var[:, 512 * j:512 * (j + 1)], sxx[j], 1.0 / E, None, OP.mult)
                msq = b.sbS[0:1, 2048:3072]
                v.tensor_tensor(msq, m, m, OP.mult)
                b.inc(v.tensor_tensor(var, var, msq, OP.subtract), "v")

            @blk.scalar
            def _(sc):
                sc.wait_ge(b.sem["v"], c0["v"] + 1)
                sc.activation(var, var, ACT.Ln, bias=b.sbF[0:1, 5:6], scale=1.0)
                b.inc(sc.activation(var, var, ACT.Exp, scale=-0.5), "a")

            @blk.tensor
            def _(t):
                t.wait_ge(b.sem["a"], c0["a"] + 1)
                t.matmul(b.psA[:, 0:512], b.ones1x128[:], m[:, 0:512], start=True, stop=True)
                t.matmul(b.psA[:, 512:1024], b.ones1x128[:], m[:, 512:1024], start=True, stop=True)
                t.matmul(b.psB[:, 0:512], b.ones1x128[:], var[:, 0:512], start=True, stop=True)
                b.inc(t.matmul(b.psB[:, 512:1024], b.ones1x128[:], var[:, 512:1024], start=True, stop=True), "t")

            @blk.vector
            def _(v):
                v.wait_ge(b.sem["t"], c0["t"] + 1)
                v.tensor_copy(mbf, b.psA[:, 0:1024])
                b.inc(v.tensor_copy(rbf, b.psB[:, 0:1024]), "v")

            @blk.sync
            def _(s):
                for fo in range(16):
                    if fo >= 3:
                        s.wait_ge(b.sem["v"], c0["v"] + 2 + (fo - 3) * 2 + 2)
                    b.inc(s.dma_start(strips[fo % 3], b.x2scr[fo * 128:(fo + 1) * 128, :]), "s", 16)

            @blk.scalar
            def _(sc):
                for fo in range(16):
                    sc.wait_ge(b.sem["v"], c0["v"] + 2 + fo * 2 + 1)
                    if fo >= 2:
                        sc.wait_ge(b.sem["v"], c0["v"] + 2 + (fo - 2) * 2 + 2)
                    sc.activation(XH2[:, fo, :], strips[fo % 3], ACT.Copy)
                    b.inc(sc.activation(hifs[fo % 2], XH2[:, fo, :], ACT.Copy), "a")

            @blk.vector
            def _(v):
                for fo in range(16):
                    v.wait_ge(b.sem["s"], c0["s"] + (fo + 1) * 16)
                    st = strips[fo % 3]
                    v.tensor_tensor(st, st, mbf, OP.subtract)
                    b.inc(v.tensor_tensor(st, st, rbf, OP.mult), "v")
                    v.wait_ge(b.sem["a"], c0["a"] + 1 + fo + 1)
                    b.inc(v.tensor_tensor(XL2[:, fo, :], st, hifs[fo % 2], OP.subtract), "v")
        self.check(c0, v=2 + 32, t=1, a=1 + 16, s=256)
        self.barrier()

    def phase_router(self):
        b = self
        nc = self.nc
        XH2 = b.sbW[:].rearrange("p (c t) -> p c t", c=NCH)[:, :, 0:1024]
        XL2 = b.sbW[:].rearrange("p (c t) -> p c t", c=NCH)[:, :, 1024:2048]
        WRT = b.sbS[:, 2048:2048 + 16 * 8].rearrange("p (c n) -> p c n", c=NCH)
        WRTL = b.sbS[:, 2176:2176 + 16 * 8].rearrange("p (c n) -> p c n", c=NCH)
        WRTF = b.sbS[:, 5120:5376].bitcast(F32)
        WRTHF = b.sbS[:, 5376:5632].bitcast(F32)
        IDN = b.sbS[:, 4480:4608]
        CFM_OFF = 2432
        ZA = b.sbS[:, 4608:4736].bitcast(F32)
        GA = b.sbS[:, 4736:4864].bitcast(F32)
        Z2A = b.sbS[:, 4864:4992].bitcast(F32)
        CMA = b.sbS[:, 4992:5120].bitcast(F32)
        NMX = b.sbF[:, 8:16]
        V1 = b.sbF[:, 16:24]
        V2 = b.sbF[:, 24:32]
        S12 = b.sbF[:, 32:40]
        ZC = b.sbF[:, 44:45]
        OC = b.sbF[:, 45:46]
        ZTM = b.sbS[:, 2304:2368]
        crep = b.sbV[:, 12288:12288 + 8 * 1024].rearrange("p (e t) -> p e t", e=8)
        c0 = self.snap()

        def zs(tc):
            return ZA[:, tc * 8:(tc + 1) * 8]

        with nc.Block() as blk:
            @blk.gpsimd
            def _(g):
                b.inc(g.dma_start(WRTF[:, 0:128], b.wrt), "g", 16)
                b.inc(g.dma_start(b.sbS[0:8, 3456:4480], b.conehot), "g", 16)
                b.inc(g.dma_start(IDN, b.cident), "g", 16)

            @blk.tensor
            def _(t):
                t.wait_ge(b.sem["g"], c0["g"] + 48)
                t.wait_ge(b.sem["v"], c0["v"] + 1)
                for tc in range(8):
                    if tc >= 2:
                        t.wait_ge(b.sem["v"], c0["v"] + 1 + (tc - 2) + 1)
                    PL = [b.psL1, b.psL2][tc % 2]
                    for f in range(NCH):
                        t.matmul(PL[:, 0:8], XH2[:, f, tc * 128:(tc + 1) * 128],
                                 WRT[:, f, :], start=(f == 0), stop=False)
                        t.matmul(PL[:, 0:8], XL2[:, f, tc * 128:(tc + 1) * 128],
                                 WRT[:, f, :], start=False, stop=False)
                        ins = t.matmul(PL[:, 0:8], XH2[:, f, tc * 128:(tc + 1) * 128],
                                       WRTL[:, f, :], start=False,
                                       stop=(f == NCH - 1))
                    b.inc(ins, "t")
                t.wait_ge(b.sem["v"], c0["v"] + 11)
                for tc in range(8):
                    if tc >= 2:
                        t.wait_ge(b.sem["v"], c0["v"] + 11 + (tc - 2) + 1)
                    ins = t.matmul([b.psL1, b.psL2][tc % 2][0:8, 0:128],
                                   ZTM[:, tc * 8:(tc + 1) * 8], IDN, start=True, stop=True)
                    b.inc(ins, "t")
                t.wait_ge(b.sem["v"], c0["v"] + 19)
                for e in range(8):
                    if e >= 2:
                        t.wait_ge(b.sem["v"], c0["v"] + 19 + (e - 2) + 1)
                    PR = [b.psA, b.psB][e % 2]
                    t.matmul(PR[:, 0:512], b.sbS[0:8, 3456 + e * 128:3456 + (e + 1) * 128],
                             b.sbS[0:8, CFM_OFF:CFM_OFF + 512], start=True, stop=True)
                    ins = t.matmul(PR[:, 512:1024], b.sbS[0:8, 3456 + e * 128:3456 + (e + 1) * 128],
                                   b.sbS[0:8, CFM_OFF + 512:CFM_OFF + 1024], start=True, stop=True)
                    b.inc(ins, "t")

            @blk.scalar
            def _(sc):
                sc.wait_ge(b.sem["v"], c0["v"] + 10)
                for tc in range(8):
                    ins = sc.activation(zs(tc), zs(tc), ACT.Exp, bias=NMX[:, tc:tc + 1], scale=1.0)
                b.inc(ins, "a")

            @blk.vector
            def _(v):
                v.wait_ge(b.sem["g"], c0["g"] + 16)
                v.tensor_copy(b.sbS[:, 2048:2176], WRTF)
                v.memset(ZC, 0.0)
                v.tensor_copy(WRTHF, b.sbS[:, 2048:2176])
                v.memset(OC, 1.0)
                b.inc(v.tensor_tensor(b.sbS[:, 2176:2304], WRTF, WRTHF, OP.subtract), "v")
                for tc in range(8):
                    v.wait_ge(b.sem["t"], c0["t"] + tc + 1)
                    PL = [b.psL1, b.psL2][tc % 2]
                    b.inc(v.tensor_copy(zs(tc), PL[:, 0:8]), "v")
                for tc in range(8):
                    ins = v.reduce_max(NMX[:, tc:tc + 1], zs(tc), axis=AX.X, negate=True)
                b.inc(ins, "v")
                v.wait_ge(b.sem["a"], c0["a"] + 1)
                for tc in range(8):
                    v.reduce_max(V1[:, tc:tc + 1], zs(tc), axis=AX.X)
                for tc in range(8):
                    v.tensor_scalar(GA[:, tc * 8:(tc + 1) * 8], zs(tc), V1[:, tc:tc + 1], ZC, OP.subtract, OP.add)
                for tc in range(8):
                    v.tensor_scalar(GA[:, tc * 8:(tc + 1) * 8], GA[:, tc * 8:(tc + 1) * 8], 0.0, None, OP.is_ge)
                for tc in range(8):
                    v.scalar_tensor_tensor(Z2A[:, tc * 8:(tc + 1) * 8], GA[:, tc * 8:(tc + 1) * 8], -10.0,
                                           zs(tc), OP.mult, OP.add)
                for tc in range(8):
                    v.reduce_max(V2[:, tc:tc + 1], Z2A[:, tc * 8:(tc + 1) * 8], axis=AX.X)
                for tc in range(8):
                    v.tensor_scalar(S12[:, tc:tc + 1], V1[:, tc:tc + 1], V2[:, tc:tc + 1], ZC, OP.add, OP.add)
                for tc in range(8):
                    v.tensor_scalar(GA[:, tc * 8:(tc + 1) * 8], zs(tc), V2[:, tc:tc + 1], ZC, OP.subtract, OP.add)
                for tc in range(8):
                    v.tensor_scalar(GA[:, tc * 8:(tc + 1) * 8], GA[:, tc * 8:(tc + 1) * 8], 0.0, None, OP.is_ge)
                v.reciprocal(S12, S12)
                for tc in range(8):
                    v.tensor_tensor(CMA[:, tc * 8:(tc + 1) * 8], zs(tc), GA[:, tc * 8:(tc + 1) * 8], OP.mult)
                for tc in range(8):
                    v.tensor_scalar(CMA[:, tc * 8:(tc + 1) * 8], CMA[:, tc * 8:(tc + 1) * 8],
                                    S12[:, tc:tc + 1], OC, OP.mult, OP.mult)
                b.inc(v.tensor_copy(ZTM, CMA), "v")
                for tc in range(8):
                    v.wait_ge(b.sem["t"], c0["t"] + 8 + tc + 1)
                    b.inc(v.tensor_copy(b.sbS[0:8, CFM_OFF + tc * 128:CFM_OFF + (tc + 1) * 128],
                                        [b.psL1, b.psL2][tc % 2][0:8, 0:128]), "v")
                for e in range(8):
                    PR = [b.psA, b.psB][e % 2]
                    v.wait_ge(b.sem["t"], c0["t"] + 16 + e + 1)
                    v.tensor_copy(crep[:, e, 0:512], PR[:, 0:512])
                    b.inc(v.tensor_copy(crep[:, e, 512:1024], PR[:, 512:1024]), "v")
        self.check(c0, t=24, a=1, v=27, g=48)
        self.barrier()

    def phase_ffn(self):
        b = self
        nc = self.nc
        XH2 = b.sbW[:].rearrange("p (c t) -> p c t", c=NCH)[:, :, 0:1024]
        y2 = b.sbX[:].bitcast(F32).rearrange("p (c t) -> p c t", c=NCH)[:, :, 0:1024]
        h = b.sbV[:, 21504:32768].rearrange("p (c t) -> p c t", c=ICH)
        crep = b.sbV[:, 12288:12288 + 8 * 1024].rearrange("p (e t) -> p e t", e=8)
        w1r = [b.sbV[:, 0:2048].rearrange("p (c n) -> p c n", c=NCH),
               b.sbV[:, 2048:4096].rearrange("p (c n) -> p c n", c=NCH)]
        w2r = [b.sbV[:, 4096:4096 + ICH * 128].rearrange("p (c n) -> p c n", c=ICH),
               b.sbV[:, 5504:5504 + ICH * 128].rearrange("p (c n) -> p c n", c=ICH)]
        hps = [b.psU1, b.psU2, b.psL1, b.psL2]
        yps = [b.psA, b.psB]
        ee = list(range(8)) + [8]

        def w1_dram(e, i):
            src = b.ws1 if e == 8 else b.w1[e]
            return src.rearrange("(c p) n -> p c n", p=128)[:, :, i * 128:(i + 1) * 128]

        def w2_dram(e, fo):
            src = b.ws2 if e == 8 else b.w2[e]
            return src.rearrange("(c p) n -> p c n", p=128)[:, :, fo * 128:(fo + 1) * 128]

        TPE = 2 * ICH + 16
        def vec_before(ei):
            return sum((ICH if ee[k] < 8 else 0) + 16 for k in range(ei))
        c0 = self.snap()

        with nc.Block() as blk:
            @blk.gpsimd
            def _(g):
                nw1 = 0
                nw2 = 0
                for ei in range(9):
                    for i in range(ICH):
                        if nw1 >= 2:
                            pei, pi = divmod(nw1 - 2, ICH)
                            g.wait_ge(b.sem["t"], c0["t"] + pei * TPE + (pi + 1) * 2)
                        b.inc(g.dma_start(w1r[nw1 % 2][:], w1_dram(ee[ei], i)), "g", 16)
                        nw1 += 1
                    for fo in range(16):
                        if nw2 >= 2:
                            pei, pfo = divmod(nw2 - 2, 16)
                            g.wait_ge(b.sem["t"], c0["t"] + pei * TPE + 2 * ICH + pfo + 1)
                        b.inc(g.dma_start(w2r[nw2 % 2][:], w2_dram(ee[ei], fo)), "g", 16)
                        nw2 += 1

            @blk.tensor
            def _(t):
                for ei in range(9):
                    e = ee[ei]
                    for i in range(ICH):
                        t.wait_ge(b.sem["g"], c0["g"] + (ei * (ICH + 16) + i + 1) * 16)
                        for th in range(2):
                            n = (ei * ICH + i) * 2 + th
                            if n >= 4:
                                t.wait_ge(b.sem["a"], c0["a"] + (n - 4) + 1)
                            for f in range(NCH):
                                ins = t.matmul(hps[n % 4][:], w1r[(ei * ICH + i) % 2][:, f, :],
                                               XH2[:, f, th * 512:(th + 1) * 512],
                                               start=(f == 0), stop=(f == NCH - 1))
                            b.inc(ins, "t")
                    for fo in range(16):
                        t.wait_ge(b.sem["g"], c0["g"] + (ei * (ICH + 16) + ICH + fo + 1) * 16)
                        t.wait_ge(b.sem["a"], c0["a"] + (ei + 1) * 2 * ICH)
                        if e < 8:
                            t.wait_ge(b.sem["v"], c0["v"] + vec_before(ei) + ICH)
                        ny = ei * 16 + fo
                        if ny >= 2:
                            pei, pfo = divmod(ny - 2, 16)
                            t.wait_ge(b.sem["v"], c0["v"] + vec_before(pei) +
                                      (ICH if ee[pei] < 8 else 0) + pfo + 1)
                        PS = yps[ny % 2]
                        for th in range(2):
                            for i in range(ICH):
                                ins = t.matmul(PS[:, th * 512:(th + 1) * 512],
                                               w2r[ny % 2][:, i, :],
                                               h[:, i, th * 512:(th + 1) * 512],
                                               start=(i == 0), stop=(i == ICH - 1))
                        b.inc(ins, "t")

            @blk.scalar
            def _(sc):
                for ei in range(9):
                    for i in range(ICH):
                        for th in range(2):
                            n = (ei * ICH + i) * 2 + th
                            sc.wait_ge(b.sem["t"], c0["t"] + ei * TPE + i * 2 + th + 1)
                            b.inc(sc.activation(h[:, i, th * 512:(th + 1) * 512],
                                                hps[n % 4][:], ACT.Silu), "a")

            @blk.vector
            def _(v):
                for ei in range(9):
                    e = ee[ei]
                    if e < 8:
                        for i in range(ICH):
                            v.wait_ge(b.sem["a"], c0["a"] + ei * 2 * ICH + (i + 1) * 2)
                            b.inc(v.tensor_tensor(h[:, i, :], h[:, i, :], crep[:, e, :], OP.mult), "v")
                    for fo in range(16):
                        v.wait_ge(b.sem["t"], c0["t"] + ei * TPE + 2 * ICH + fo + 1)
                        PS = yps[(ei * 16 + fo) % 2]
                        if ei == 0:
                            ins = v.tensor_copy(y2[:, fo, :], PS[:])
                        else:
                            ins = v.tensor_tensor(y2[:, fo, :], y2[:, fo, :], PS[:], OP.add)
                        b.inc(ins, "v")
        self.check(c0, t=9 * TPE, a=9 * 2 * ICH, v=vec_before(9), g=16 * 9 * (ICH + 16))
        self.barrier()

    def phase_final(self):
        b = self
        nc = self.nc
        y2 = b.sbX[:].bitcast(F32).rearrange("p (c t) -> p c t", c=NCH)[:, :, 0:1024]
        sbv_f32 = b.sbV[:].bitcast(F32)
        xres = [sbv_f32[:, 0:1024], sbv_f32[:, 1024:2048]]
        c0 = self.snap()
        with nc.Block() as blk:
            @blk.sync
            def _(s):
                for fo in range(16):
                    if fo >= 2:
                        s.wait_ge(b.sem["v"], c0["v"] + (fo - 2) + 1)
                        s.wait_ge(b.sem["c"], c0["c"] + (fo - 2 + 1) * 16)
                    b.inc(s.dma_start(xres[fo % 2], b.x2scr[fo * 128:(fo + 1) * 128, :]), "s", 16)

            @blk.vector
            def _(v):
                for fo in range(16):
                    v.wait_ge(b.sem["s"], c0["s"] + (fo + 1) * 16)
                    b.inc(v.tensor_tensor(xres[fo % 2], y2[:, fo, :], xres[fo % 2], OP.add), "v")

            @blk.scalar
            def _(sc):
                for fo in range(16):
                    sc.wait_ge(b.sem["v"], c0["v"] + fo + 1)
                    b.inc(sc.dma_start(b.out[fo * 128:(fo + 1) * 128, :], xres[fo % 2]), "c", 16)
                sc.wait_ge(b.sem["c"], c0["c"] + 256)
        self.check(c0, s=256, v=16, c=256)


def make_host_inputs(inputs, core):
    x = np.asarray(inputs["L0_attn_x"], np.float32)
    perm, own, other = perm_for(core)
    xt = np.ascontiguousarray(x[core // 2].T).reshape(E, 16, 128)
    xfm = np.ascontiguousarray(xt[:, perm, :].reshape(E, T)).astype(np.float32)
    mvals = [sum(1 for c in other if c <= own[qi]) for qi in range(8)]
    mrep = np.zeros((2, 128, 512), np.float32)
    for s in range(2):
        for qsub in range(4):
            mrep[s, :, qsub * 128:(qsub + 1) * 128] = mvals[4 * s + qsub]
    tri = (np.arange(128)[:, None] <= np.arange(128)[None, :]).astype(np.float32)
    onehot = np.zeros((8, 8 * 128), np.float32)
    for e in range(8):
        onehot[e, e * 128:(e + 1) * 128] = 1.0
    return dict(
        xfm=xfm,
        wq=np.ascontiguousarray(inputs["L0_attn_Wq"], np.float32),
        wk=np.ascontiguousarray(inputs["L0_attn_Wk"], np.float32),
        wv=np.ascontiguousarray(inputs["L0_attn_Wv"], np.float32),
        wo=np.ascontiguousarray(inputs["L0_attn_Wo"], np.float32),
        ws1=np.ascontiguousarray(inputs["L0_ffn_w_shared1"], np.float32),
        ws2=np.ascontiguousarray(inputs["L0_ffn_w_shared2"], np.float32),
        w1=np.ascontiguousarray(inputs["L0_ffn_w1"], np.float32),
        w2=np.ascontiguousarray(inputs["L0_ffn_w2"], np.float32),
        wrt=np.ascontiguousarray(
            np.asarray(inputs["L0_ffn_w_router"], np.float32).reshape(16, 128, 8)
            .transpose(1, 0, 2).reshape(128, 128)),
        lq=np.concatenate([np.asarray(inputs["L0_attn_lq1"], np.float32),
                           np.asarray(inputs["L0_attn_lq2"], np.float32)])[None, :],
        lk=np.concatenate([np.asarray(inputs["L0_attn_lk1"], np.float32),
                           np.asarray(inputs["L0_attn_lk2"], np.float32)])[None, :],
        cones128=np.ones((128, 1), np.float32),
        cones1x128=np.ones((1, 128), np.float32),
        cones1x128f=np.ones((1, 128), np.float32),
        crms1x128=np.full((1, 128), 1.0 - LAMBDA_INIT, np.float32),
        ctri=tri,
        conehot=onehot,
        cident=np.eye(128, dtype=np.float32),
        cmrep=mrep,
    )


def assemble_output(results):
    full = np.zeros((4, 2048, 2048), np.float32)
    for core in range(8):
        perm, own, other = perm_for(core)
        o = np.asarray(results[core]["out"])
        for j, c in enumerate(own):
            full[core // 2, c * 128:(c + 1) * 128, :] = o[:, j * 128:(j + 1) * 128].T
    return full


def kernel(**inputs):
    bld = Builder()
    nc = bld.build()
    in_maps = [make_host_inputs(inputs, core) for core in range(8)]
    res = run_bass_kernel_spmd(nc, in_maps, core_ids=list(range(8)))
    return assemble_output(res.results)


# version 09
# speedup vs baseline: 1.0668x; 1.0189x over previous
import math
import numpy as np
import concourse.bass as bass
import concourse.mybir as mybir
from concourse.alu_op_type import AluOpType as OP
from concourse.bass_utils import run_bass_kernel_spmd

F32 = mybir.dt.float32
BF16 = mybir.dt.bfloat16
ACT = mybir.ActivationFunctionType
AX = mybir.AxisListType

E = 2048
T = 2048
OT = 1024
NCH = 16
PAIRS = 16
LAMBDA_INIT = 0.8 - 0.6 * math.exp(-0.3 * 12)
SCALING = 64.0 ** -0.5
I = 1408
ICH = 11
NE = 8

OWN_EVEN = [0, 3, 4, 7, 8, 11, 12, 15]
OWN_ODD = [1, 2, 5, 6, 9, 10, 13, 14]


def perm_for(core):
    own = OWN_EVEN if core % 2 == 0 else OWN_ODD
    other = [c for c in range(16) if c not in own]
    return own + other, own, other


def s_chunks(slot):
    own = list(range(0, 4 * slot + 4))
    other = list(range(8, 8 + 4 * slot + 4))
    return own, other


class Builder:
    def __init__(self, debug=(), stop_after=None):
        self.stop_after = stop_after
        self.debug = set(debug)
        nc = bass.Bass("TRN2", target_bir_lowering=False, detect_race_conditions=False)
        self.nc = nc
        dr = lambda n, s, d, k="ExternalInput": nc.dram_tensor(n, s, d, kind=k).ap()
        self.xfm = dr("xfm", [E, T], F32)
        self.wq = dr("wq", [E, E], F32)
        self.wk = dr("wk", [E, E], F32)
        self.wv = dr("wv", [E, E], F32)
        self.wo = dr("wo", [E, E], F32)
        self.ws1 = dr("ws1", [E, I], F32)
        self.ws2 = dr("ws2", [I, E], F32)
        self.w1 = dr("w1", [NE, E, I], F32)
        self.w2 = dr("w2", [NE, I, E], F32)
        self.wrt = dr("wrt", [128, 128], F32)
        self.lq = dr("lq", [1, 128], F32)
        self.lk = dr("lk", [1, 128], F32)
        self.cones128 = dr("cones128", [128, 1], F32)
        self.cones1x128 = dr("cones1x128", [1, 128], F32)
        self.cones1x128f = dr("cones1x128f", [1, 128], F32)
        self.crms1x128 = dr("crms1x128", [1, 128], F32)
        self.ctri = dr("ctri", [128, 128], F32)
        self.conehot = dr("conehot", [8, 8 * 128], F32)
        self.cident = dr("cident", [128, 128], F32)
        self.cmrep = dr("cmrep", [2, 128, 512], F32)
        self.out = dr("out", [E, OT], F32, "ExternalOutput")
        self.x2scr = dr("x2scr", [E, OT], F32, "Internal")

        self.sbX = nc.alloc_sbuf_tensor("sbX", [128, NCH * T], BF16)
        self.sbV = nc.alloc_sbuf_tensor("sbV", [128, NCH * T], BF16)
        self.sbW = nc.alloc_sbuf_tensor("sbW", [128, NCH * T], BF16)
        self.sbS = nc.alloc_sbuf_tensor("sbS", [128, 6272], BF16)
        self.sbF = nc.alloc_sbuf_tensor("sbF", [128, 64], F32)
        self.ones128 = nc.alloc_sbuf_tensor("ones128", [128, 1], BF16)
        self.ones1x128 = nc.alloc_sbuf_tensor("ones1x128", [1, 128], BF16)
        self.ones1x128f = nc.alloc_sbuf_tensor("ones1x128f", [1, 128], F32)
        self.rms1x128 = nc.alloc_sbuf_tensor("rms1x128", [1, 128], BF16)

        self.psA = nc.alloc_psum_tensor("psA", [128, 1024], F32)
        self.psB = nc.alloc_psum_tensor("psB", [128, 1024], F32)
        self.psU1 = nc.alloc_psum_tensor("psU1", [128, 512], F32)
        self.psU2 = nc.alloc_psum_tensor("psU2", [128, 512], F32)
        self.psL1 = nc.alloc_psum_tensor("psL1", [128, 512], F32)
        self.psL2 = nc.alloc_psum_tensor("psL2", [128, 512], F32)

        self.sem = {}
        self.cnt = {}
        for name in ("g", "s", "c", "t", "v", "a"):
            self.sem[name] = nc.alloc_semaphore("sem_" + name)
            self.cnt[name] = 0
        self.taps = {}

    def inc(self, instr, sem, n=1):
        instr.then_inc(self.sem[sem], n)
        self.cnt[sem] += n
        return self.cnt[sem]

    def snap(self):
        return dict(self.cnt)

    def check(self, c0, **expected):
        for k, d in expected.items():
            got = self.cnt[k] - c0[k]
            assert got == d, f"emission count mismatch sem {k}: got {got} expected {d}"

    def barrier(self, fence=True):
        b = self
        if fence:
            with self.nc.Block() as blk:
                @blk.sync
                def _(s):
                    s.wait_ge(b.sem["s"], b.cnt["s"])
                    s.wait_ge(b.sem["c"], b.cnt["c"])
                @blk.gpsimd
                def _(g):
                    g.wait_ge(b.sem["g"], b.cnt["g"])
        self.nc.all_engine_barrier()

    def want(self, name):
        return name in self.debug

    def tap(self, name, shape, dtype=F32):
        ap = self.nc.dram_tensor("tap_" + name, shape, dtype, kind="ExternalOutput").ap()
        self.taps[name] = ap
        return ap

    def store_tap(self, name, sb_ap, shape, dtype=F32):
        b = self
        nc = self.nc
        ap = self.tap(name, shape, dtype)
        with nc.Block() as blk:
            @blk.sync
            def _(s):
                n = b.inc(s.dma_start(ap, sb_ap), "s", 16)
                s.wait_ge(b.sem["s"], n)
        self.barrier()

    def build(self):
        steps = [
            ("consts", self.phase_consts, None),
            ("ln1", self.phase_ln1, ("xhat", lambda: self.store_tap("xhat", self.sbX[:], [128, NCH * T], BF16))),
            ("vproj", self.phase_vproj, ("V", lambda: self.store_tap("V", self.sbV[:], [128, NCH * T], BF16))),
            ("kproj", self.phase_kproj, ("K", lambda: self.store_tap("K", self.sbW[:], [128, NCH * T], BF16))),
            ("qproj", self.phase_qproj, ("Q", lambda: self.store_tap("Q", self.sbX[:], [128, NCH * T], BF16))),
            ("bounds", self.phase_bounds, ("bias", lambda: self.store_tap("bias", self.sbF[:, 0:64], [128, 64], F32))),
            ("attn", self.phase_attn, ("O", lambda: self.store_tap("O", self.sbX[:], [128, NCH * T], BF16))),
            ("outproj", self.phase_outproj, ("x2", lambda: self.store_tap("x2", self.x2scr, [E, OT], F32))),
            ("ln2", self.phase_ln2, ("xhat2", lambda: (self.store_tap("xhat2",
                self.sbW[:].rearrange("p (c t) -> p c t", c=NCH)[:, :, 0:1024], [128, NCH, OT], BF16),
                self.store_tap("xlo2",
                self.sbW[:].rearrange("p (c t) -> p c t", c=NCH)[:, :, 1024:2048], [128, NCH, OT], BF16)))),
            ("router", self.phase_router, ("comb", lambda: (self.store_tap("comb", self.sbS[0:8, 2432:2432 + 1024], [8, 1024], BF16),
                                                             self.store_tap("zrouter", self.sbS[:, 4608:4736], [128, 128], BF16)))),
            ("ffn", self.phase_ffn, None),
            ("final", self.phase_final, None),
        ]
        for name, fn, tapinfo in steps:
            fn()
            if tapinfo is not None and self.want(tapinfo[0]):
                tapinfo[1]()
            if self.stop_after == name:
                break
        return self.nc

    def phase_consts(self):
        b = self
        nc = self.nc
        with nc.Block() as blk:
            @blk.gpsimd
            def _(g):
                b.g_x = []
                for xs in range(4):
                    b.inc(g.dma_start(b.sbX[:].rearrange("p (c t) -> p c t", c=NCH)[:, xs * 4:(xs + 1) * 4, :],
                                      b.xfm.rearrange("(c p) t -> p c t", p=128)[:, xs * 4:(xs + 1) * 4, :]),
                          "g", 16)
                    b.g_x.append(b.cnt["g"])
                b.inc(g.dma_start(b.ones128[:], b.cones128), "g", 16)
                b.inc(g.dma_start(b.ones1x128[:], b.cones1x128), "g", 16)
                b.inc(g.dma_start(b.ones1x128f[:], b.cones1x128f), "g", 16)
                b.inc(g.dma_start(b.rms1x128[:], b.crms1x128), "g", 16)
                b.g_ln1 = b.cnt["g"]
                b.g_wv = []
                for dvg in range(4):
                    b.inc(g.dma_start(b.sbW[:].rearrange("p (c n) -> p c n", c=NCH)[:, :, dvg * 512:(dvg + 1) * 512],
                                      b.wv.rearrange("(c p) n -> p c n", p=128)[:, :, dvg * 512:(dvg + 1) * 512]),
                          "g", 16)
                    b.g_wv.append(b.cnt["g"])

            @blk.sync
            def _(s):
                lqk = b.sbS[0:1, 0:512].bitcast(F32)
                b.inc(s.dma_start(lqk[:, 0:128], b.lq), "s", 16)
                b.inc(s.dma_start(lqk[:, 128:256], b.lk), "s", 16)

            @blk.vector
            def _(v):
                v.wait_ge(b.sem["s"], b.cnt["s"])
                v.memset(b.sbF[0:1, 5:6], 1e-5)
                lqk = b.sbS[0:1, 0:512].bitcast(F32)
                v.tensor_tensor(lqk[:, 0:128], lqk[:, 0:128], lqk[:, 128:256], OP.mult)
                v.reduce_sum(b.sbF[0:1, 0:1], lqk[:, 0:64], axis=AX.X)
                b.inc(v.reduce_sum(b.sbF[0:1, 1:2], lqk[:, 64:128], axis=AX.X), "v")

            @blk.vector
            def _(v):
                b.inc(v.tensor_scalar(b.sbS[0:1, 600:602], b.sbF[0:1, 0:2], 1.0, None, OP.mult), "v")

            @blk.tensor
            def _(t):
                t.wait_ge(b.sem["v"], b.cnt["v"])
                t.wait_ge(b.sem["g"], b.cnt["g"])
                b.inc(t.matmul(b.psU1[:, 0:2], b.ones1x128[:], b.sbS[0:1, 600:602], start=True, stop=True), "t")

            @blk.vector
            def _(v):
                v.wait_ge(b.sem["t"], b.cnt["t"])
                b.inc(v.tensor_copy(b.sbF[:, 2:4], b.psU1[:, 0:2]), "v")

            @blk.scalar
            def _(sc):
                sc.wait_ge(b.sem["v"], b.cnt["v"])
                b.inc(sc.activation(b.sbF[:, 2:4], b.sbF[:, 2:4], ACT.Exp), "a")

            @blk.vector
            def _(v):
                v.wait_ge(b.sem["a"], b.cnt["a"])
                b.inc(v.tensor_scalar(b.sbF[:, 4:5], b.sbF[:, 2:3], b.sbF[:, 3:4],
                                      float(LAMBDA_INIT), OP.subtract, OP.add), "v")
        self.barrier(fence=False)

    def phase_ln1(self):
        b = self
        nc = self.nc
        X = b.sbX[:].rearrange("p (c t) -> p c t", c=NCH)
        sx = [b.psA[0:1, 0:512], b.psA[0:1, 512:1024], b.psB[0:1, 0:512], b.psB[0:1, 512:1024]]
        sxx = [b.psU1[0:1, :], b.psU2[0:1, :], b.psL1[0:1, :], b.psL2[0:1, :]]
        TMP = [b.sbS[:, 0:2048], b.sbS[:, 2048:4096]]
        c0 = self.snap()
        with nc.Block() as blk:
            @blk.vector
            def _(v):
                for c in range(NCH):
                    if c % 4 == 0:
                        v.wait_ge(b.sem["g"], b.g_x[c // 4])
                    if c >= 2:
                        v.wait_ge(b.sem["t"], c0["t"] + (c - 2) + 1)
                    b.inc(v.tensor_tensor(TMP[c % 2], X[:, c, :], X[:, c, :], OP.mult), "v")

            @blk.tensor
            def _(t):
                t.wait_ge(b.sem["g"], b.g_ln1)
                for c in range(NCH):
                    for j in range(4):
                        t.matmul(sx[j], b.ones128[:], X[:, c, 512 * j:512 * (j + 1)],
                                 start=(c == 0), stop=(c == NCH - 1))
                    t.wait_ge(b.sem["v"], c0["v"] + c + 1)
                    for j in range(4):
                        ins = t.matmul(sxx[j], b.ones128[:], TMP[c % 2][:, 512 * j:512 * (j + 1)],
                                       start=(c == 0), stop=(c == NCH - 1))
                    b.inc(ins, "t")

            @blk.vector
            def _(v):
                v.wait_ge(b.sem["t"], c0["t"] + NCH)
                m = b.sbS[0:1, 4096:6144]
                var = b.sbS[0:1, 0:2048]
                for j in range(4):
                    v.tensor_scalar(m[:, 512 * j:512 * (j + 1)], sx[j], 1.0 / E, None, OP.mult)
                    v.tensor_scalar(var[:, 512 * j:512 * (j + 1)], sxx[j], 1.0 / E, None, OP.mult)
                msq = b.sbS[0:1, 2048:4096]
                v.tensor_tensor(msq, m, m, OP.mult)
                b.inc(v.tensor_tensor(var, var, msq, OP.subtract), "v")

            @blk.scalar
            def _(sc):
                var = b.sbS[0:1, 0:2048]
                sc.wait_ge(b.sem["v"], c0["v"] + NCH + 1)
                sc.activation(var, var, ACT.Ln, bias=b.sbF[0:1, 5:6], scale=1.0)
                b.inc(sc.activation(var, var, ACT.Exp, scale=-0.5), "a")

            @blk.tensor
            def _(t):
                t.wait_ge(b.sem["a"], c0["a"] + 1)
                m = b.sbS[0:1, 4096:6144]
                r = b.sbS[0:1, 0:2048]
                mout = [b.psA[:, 0:512], b.psA[:, 512:1024], b.psB[:, 0:512], b.psB[:, 512:1024]]
                rout = [b.psU1[:], b.psU2[:], b.psL1[:], b.psL2[:]]
                for j in range(4):
                    t.matmul(mout[j], b.ones1x128[:], m[:, 512 * j:512 * (j + 1)], start=True, stop=True)
                for j in range(4):
                    ins = t.matmul(rout[j], b.ones1x128[:], r[:, 512 * j:512 * (j + 1)], start=True, stop=True)
                b.inc(ins, "t")

            @blk.vector
            def _(v):
                v.wait_ge(b.sem["t"], c0["t"] + NCH + 1)
                mb = b.sbS[:, 4096:6144]
                rb = b.sbS[:, 2048:4096]
                v.tensor_copy(mb[:, 0:512], b.psA[:, 0:512])
                v.tensor_copy(mb[:, 512:1024], b.psA[:, 512:1024])
                v.tensor_copy(mb[:, 1024:1536], b.psB[:, 0:512])
                v.tensor_copy(mb[:, 1536:2048], b.psB[:, 512:1024])
                v.tensor_copy(rb[:, 0:512], b.psU1[:])
                v.tensor_copy(rb[:, 512:1024], b.psU2[:])
                v.tensor_copy(rb[:, 1024:1536], b.psL1[:])
                v.tensor_copy(rb[:, 1536:2048], b.psL2[:])
        self.check(c0, v=NCH + 1, t=NCH + 1, a=1)
        if self.want("xhat"):
            self.barrier()

    def phase_vproj(self):
        b = self
        nc = self.nc
        X = b.sbX[:].rearrange("p (c t) -> p c t", c=NCH)
        W = b.sbW[:].rearrange("p (c n) -> p c n", c=NCH)
        V = b.sbV[:].rearrange("p (c n) -> p c n", c=NCH)
        ps = [b.psU1, b.psU2, b.psL1, b.psL2]
        c0 = self.snap()
        with nc.Block() as blk:
            @blk.vector
            def _(v):
                mb = b.sbS[:, 4096:6144]
                rb = b.sbS[:, 2048:4096]
                b.v_ln1apply = []
                for c in range(NCH):
                    v.tensor_tensor(X[:, c, :], X[:, c, :], mb, OP.subtract)
                    b.inc(v.tensor_tensor(X[:, c, :], X[:, c, :], rb, OP.mult), "v")
                    b.v_ln1apply.append(b.cnt["v"])

            @blk.tensor
            def _(t):
                for n in range(64):
                    dvg, kc = divmod(n, 16)
                    if kc == 0:
                        t.wait_ge(b.sem["g"], b.g_wv[dvg])
                    if n >= 4:
                        t.wait_ge(b.sem["v"], c0["v"] + NCH + (n - 4) + 1)
                    for f in range(NCH):
                        if n == 0:
                            t.wait_ge(b.sem["v"], b.v_ln1apply[f])
                        ins = t.matmul(ps[n % 4][:], X[:, f, kc * 128:(kc + 1) * 128],
                                       W[:, f, dvg * 512:(dvg + 1) * 512],
                                       start=(f == 0), stop=(f == NCH - 1))
                    b.inc(ins, "t")

            @blk.vector
            def _(v):
                for n in range(64):
                    dvg, kc = divmod(n, 16)
                    v.wait_ge(b.sem["t"], c0["t"] + n + 1)
                    b.inc(v.tensor_copy(V[:, kc, dvg * 512:(dvg + 1) * 512], ps[n % 4][:]), "v")
        self.check(c0, t=64, v=NCH + 64)
        self.barrier()

    def _proj_stream(self, wdram, nout, ngrp, out_ap_fn, scale=None):
        b = self
        nc = self.nc
        X = b.sbX[:].rearrange("p (c t) -> p c t", c=NCH)
        ring = [b.sbS[:, 0:2048].rearrange("p (c n) -> p c n", c=NCH),
                b.sbS[:, 2048:4096].rearrange("p (c n) -> p c n", c=NCH)]
        ps = [b.psU1, b.psU2, b.psL1, b.psL2]
        c0 = self.snap()
        with nc.Block() as blk:
            @blk.gpsimd
            def _(g):
                for d in range(nout):
                    if d >= 2:
                        g.wait_ge(b.sem["t"], c0["t"] + (d - 1) * ngrp)
                    b.inc(g.dma_start(ring[d % 2][:], wdram(d)), "g", 16)

            @blk.tensor
            def _(t):
                for d in range(nout):
                    t.wait_ge(b.sem["g"], c0["g"] + (d + 1) * 16)
                    for tg in range(ngrp):
                        n = d * ngrp + tg
                        if n >= 4:
                            t.wait_ge(b.sem["v"], c0["v"] + (n - 4) + 1)
                        for f in range(NCH):
                            ins = t.matmul(ps[n % 4][:], ring[d % 2][:, f, :],
                                           X[:, f, tg * 512:(tg + 1) * 512],
                                           start=(f == 0), stop=(f == NCH - 1))
                        b.inc(ins, "t")

            @blk.vector
            def _(v):
                for d in range(nout):
                    for tg in range(ngrp):
                        n = d * ngrp + tg
                        v.wait_ge(b.sem["t"], c0["t"] + n + 1)
                        if scale is None:
                            ins = v.tensor_copy(out_ap_fn(d, tg), ps[n % 4][:])
                        else:
                            ins = v.tensor_scalar(out_ap_fn(d, tg), ps[n % 4][:], scale, None, OP.mult)
                        b.inc(ins, "v")
        self.check(c0, t=nout * ngrp, v=nout * ngrp, g=16 * nout)
        self.barrier()

    def phase_kproj(self):
        b = self
        K = b.sbW[:].rearrange("p (c t) -> p c t", c=NCH)
        self._proj_stream(
            lambda d: b.wk.rearrange("(c p) n -> p c n", p=128)[:, :, d * 128:(d + 1) * 128],
            16, 4, lambda d, tg: K[:, d, tg * 512:(tg + 1) * 512])

    def phase_qproj(self):
        b = self
        X = b.sbX[:].rearrange("p (c t) -> p c t", c=NCH)
        self._proj_stream(
            lambda d: b.wq.rearrange("(c p) n -> p c n", p=128)[:, :, d * 128:(d + 1) * 128],
            16, 2, lambda d, tg: X[:, d, 1024 + tg * 512:1024 + (tg + 1) * 512],
            scale=float(SCALING))

    def phase_bounds(self):
        b = self
        nc = self.nc
        X = b.sbX[:].rearrange("p (c t) -> p c t", c=NCH)
        K = b.sbW[:].rearrange("p (c t) -> p c t", c=NCH)
        Q1 = b.sbS[:, 0:1024]
        K1 = b.sbS[:, 1024:3072]
        SF = b.sbS[0:1, 3072:5120].bitcast(F32)
        PS = [(b.psA[0:1, 0:512], b.psA[0:1, 512:1024]),
              (b.psB[0:1, 0:512], b.psB[0:1, 512:1024])]
        c0 = self.snap()
        with nc.Block() as blk:
            @blk.gpsimd
            def _(g):
                b.inc(g.dma_start(b.sbS[:, 5120:5248], b.ctri), "g", 16)
                b.inc(g.dma_start(b.sbS[:, 5248:6272].rearrange("p (s n) -> p s n", s=2),
                                  b.cmrep.rearrange("s p n -> p s n")), "g", 16)

            @blk.vector
            def _(v):
                def red(p):
                    v.wait_ge(b.sem["t"], c0["t"] + p + 1)
                    v.reduce_max(SF[:, 48 + p:49 + p], PS[p % 2][0], axis=AX.X)
                    b.inc(v.reduce_max(SF[:, 0 + p:1 + p], PS[p % 2][1], axis=AX.X), "v")
                for pair in range(PAIRS):
                    if pair >= 1:
                        v.wait_ge(b.sem["t"], c0["t"] + pair)
                    v.tensor_tensor(Q1, X[:, pair, 1024:2048], X[:, pair, 1024:2048], OP.mult)
                    v.tensor_tensor(K1[:, 0:1024], K[:, pair, 0:1024], K[:, pair, 0:1024], OP.mult)
                    v.tensor_tensor(K1[:, 1024:2048], K[:, pair, 1024:2048], K[:, pair, 1024:2048], OP.mult)
                    v.tensor_tensor(K1[:, 0:1024], K1[:, 0:1024], K1[:, 1024:2048], OP.max)
                    v.tensor_tensor(K1[:, 0:512], K1[:, 0:512], K1[:, 512:1024], OP.max)
                    b.inc(v.tensor_tensor(Q1[:, 0:512], Q1[:, 0:512], Q1[:, 512:1024], OP.max), "v")
                    if pair >= 1:
                        red(pair - 1)
                red(PAIRS - 1)

            @blk.vector
            def _(v):
                v.wait_ge(b.sem["v"], c0["v"] + 2 * PAIRS)
                b.inc(v.tensor_tensor(b.sbF[0:1, 32:48], SF[:, 0:16], SF[:, 48:64], OP.mult), "v")

            @blk.tensor
            def _(t):
                for pair in range(PAIRS):
                    t.wait_ge(b.sem["v"], c0["v"] + (1 if pair == 0 else 2 * pair))
                    t.matmul(PS[pair % 2][0], b.ones128[:], Q1[:, 0:512], start=True, stop=True)
                    b.inc(t.matmul(PS[pair % 2][1], b.ones128[:], K1[:, 0:512], start=True, stop=True), "t")
        self.check(c0, v=2 * PAIRS + 1, t=PAIRS, g=32)
        self.barrier()
        c1 = self.snap()
        with nc.Block() as blk:
            @blk.scalar
            def _(sc):
                sc.activation(b.sbF[0:1, 32:48], b.sbF[0:1, 32:48], ACT.Ln)
                b.inc(sc.activation(b.sbF[0:1, 32:48], b.sbF[0:1, 32:48], ACT.Exp, scale=0.5), "a")

            @blk.vector
            def _(v):
                v.wait_ge(b.sem["a"], c1["a"] + 1)
                b.inc(v.tensor_scalar(b.sbS[0:1, 4480:4496], b.sbF[0:1, 32:48], -1.0, None, OP.mult), "v")

            @blk.tensor
            def _(t):
                t.wait_ge(b.sem["v"], c1["v"] + 1)
                b.inc(t.matmul(b.psL1[:, 0:16], b.ones1x128[:], b.sbS[0:1, 4480:4496], start=True, stop=True), "t")

            @blk.vector
            def _(v):
                v.wait_ge(b.sem["t"], c1["t"] + 1)
                b.inc(v.tensor_copy(b.sbF[:, 8:24], b.psL1[:, 0:16]), "v")
        self.barrier()

    def phase_attn(self):
        b = self
        with self.nc.Block() as blk:
            for pair in range(PAIRS):
                for slot in range(2):
                    self.attn_iter(blk, pair, slot)
        self.barrier()

    def attn_iter(self, blk, pair, slot):
        b = self
        nc = self.nc
        dbgP = (pair == 0 and slot == 0 and self.want("P00"))
        own, other = s_chunks(slot)
        chunks = own + other
        n_ch = len(chunks)
        q0 = slot * 512
        X = b.sbX[:].rearrange("p (c t) -> p c t", c=NCH)
        Khi = b.sbW[64:128, :].rearrange("p (c t) -> p c t", c=NCH)
        Klo = b.sbW[0:64, :].rearrange("p (c t) -> p c t", c=NCH)
        V = b.sbV[:].rearrange("p (c n) -> p c n", c=NCH)
        Qlo = b.sbX[0:64, :].rearrange("p (c t) -> p c t", c=NCH)
        Qhi = b.sbX[64:128, :].rearrange("p (c t) -> p c t", c=NCH)
        P12 = [b.sbS[:, 0:1024], b.sbS[:, 1024:2048], b.sbS[:, 3072:4096]]
        RB1 = b.sbS[:, 2048:2560]
        RB2 = b.sbS[:, 2560:3072]
        OSQ = b.sbS[:, 0:512]
        REC = b.sbS[0:1, 4096:5120]
        RSTD = b.sbS[0:1, 2048:2560]
        TRI = b.sbS[:, 5120:5248]
        BIAS = b.sbF[:, 8:24]
        lam_ap = b.sbF[0:1, 4:5]
        SB = [b.psA, b.psB]
        Oap = X[:, pair, q0:q0 + 512]

        seq = [("S", 0), ("S", 1)]
        for ci in range(n_ch):
            if ci + 2 < n_ch:
                seq.append(("S", ci + 2))
            seq.append(("L", ci))
        t_pos = {}
        for idx, item in enumerate(seq):
            t_pos[item] = idx + 1
        n_t_loop = len(seq)
        c0 = self.snap()
        if dbgP:
            tapP = self.tap("P00", [8, 128, 1024], BF16)
            tapL = self.tap("L00", [1, 1024], F32)
            tapRB = self.tap("RB00", [128, 1024], BF16)

        if True:
            if dbgP:
                @blk.sync
                def _(s):
                    for ci in range(n_ch):
                        s.wait_ge(b.sem["v"], c0["v"] + ci + 1)
                        b.inc(s.dma_start(tapP[ci], P12[ci % 2]), "s", 16)

            @blk.tensor
            def _(t):
                for item in seq:
                    kind, ci = item
                    kc = chunks[ci]
                    if kind == "S" and dbgP:
                        if ci >= 2:
                            t.wait_ge(b.sem["s"], c0["s"] + (ci - 2 + 1) * 16)
                    if kind == "S":
                        S = SB[ci % 2]
                        if ci >= 2:
                            t.wait_ge(b.sem["a"], c0["a"] + (ci - 2) + 1)
                        t.matmul(S[:, 0:512], Klo[:, pair, kc * 128:(kc + 1) * 128],
                                 Qlo[:, pair, 1024 + q0:1024 + q0 + 512],
                                 start=True, stop=True, tile_position=(0, 0))
                        ins = t.matmul(S[:, 512:1024], Khi[:, pair, kc * 128:(kc + 1) * 128],
                                       Qhi[:, pair, 1024 + q0:1024 + q0 + 512],
                                       start=True, stop=True, tile_position=(64, 0))
                        b.inc(ins, "t")
                    else:
                        t.wait_ge(b.sem["v"], c0["v"] + ci + 1)
                        P = P12[ci % 3]
                        first = ci == 0
                        last = ci == n_ch - 1
                        t.matmul(b.psL1[0:1, :], b.ones128[:], P[:, 0:512], start=first, stop=last)
                        t.matmul(b.psL2[0:1, :], b.ones128[:], P[:, 512:1024], start=first, stop=last)
                        t.matmul(b.psU1[:], V[:, kc, pair * 128:(pair + 1) * 128], P[:, 0:512],
                                 start=first, stop=last)
                        ins = t.matmul(b.psU2[:], V[:, kc, pair * 128:(pair + 1) * 128], P[:, 512:1024],
                                       start=first, stop=last)
                        b.inc(ins, "t")
                if dbgP:
                    t.wait_ge(b.sem["s"], c0["s"] + n_ch * 16)
                t.wait_ge(b.sem["a"], c0["a"] + n_ch + 1)
                t.matmul(b.psL1[:], b.ones1x128[:], REC[:, 0:512], start=True, stop=True)
                b.inc(t.matmul(b.psL2[:], b.ones1x128[:], REC[:, 512:1024], start=True, stop=True), "t")
                t.wait_ge(b.sem["v"], c0["v"] + n_ch + 3)
                b.inc(t.matmul(b.psU1[0:1, :], b.ones128[:], OSQ, start=True, stop=True), "t")
                t.wait_ge(b.sem["a"], c0["a"] + n_ch + 2)
                b.inc(t.matmul(b.psU2[:], b.rms1x128[:], RSTD[:, 0:512], start=True, stop=True), "t")

            @blk.scalar
            def _(sc):
                for ci in range(n_ch):
                    sc.wait_ge(b.sem["t"], c0["t"] + t_pos[("S", ci)])
                    if ci >= 3:
                        sc.wait_ge(b.sem["t"], c0["t"] + t_pos[("L", ci - 3)])
                    b.inc(sc.activation(P12[ci % 3], SB[ci % 2][:], ACT.Exp,
                                        bias=BIAS[:, pair:pair + 1], scale=1.0), "a")
                sc.wait_ge(b.sem["t"], c0["t"] + n_t_loop)
                sc.activation(b.psA[0:1, 0:512], b.psL1[0:1, :], ACT.Ln)
                sc.activation(b.psA[0:1, 512:1024], b.psL2[0:1, :], ACT.Ln)
                b.inc(sc.activation(REC, b.psA[0:1, 0:1024], ACT.Exp, scale=-1.0), "a")
                sc.wait_ge(b.sem["t"], c0["t"] + n_t_loop + 2)
                sc.activation(RSTD[:, 0:512], b.psU1[0:1, :], ACT.Ln, bias=b.sbF[0:1, 5:6], scale=1.0 / 128.0)
                b.inc(sc.activation(RSTD[:, 0:512], RSTD[:, 0:512], ACT.Exp, scale=-0.5), "a")

            @blk.vector
            def _(v):
                for ci in range(n_ch):
                    kc = chunks[ci]
                    P = P12[ci % 3]
                    v.wait_ge(b.sem["a"], c0["a"] + ci + 1)
                    if ci < len(own):
                        jq = kc - 4 * slot
                        if 0 <= jq <= 3:
                            for half in range(2):
                                base = half * 512
                                if jq > 0:
                                    v.memset(P[:, base:base + jq * 128], 0.0)
                                v.tensor_tensor(P[:, base + jq * 128:base + (jq + 1) * 128],
                                                P[:, base + jq * 128:base + (jq + 1) * 128],
                                                TRI, OP.mult)
                    else:
                        p = kc - 8
                        if not (slot == 1 and p < 4):
                            for half in range(2):
                                base = half * 512
                                v.scalar_tensor_tensor(P[:, base:base + 512], b.sbS[:, 5248 + slot * 512:5248 + (slot + 1) * 512],
                                                       float(p) + 0.5, P[:, base:base + 512],
                                                       OP.is_gt, OP.mult)
                    b.inc(v.tensor_copy(b.sbF[0:1, 47:48], b.sbF[0:1, 47:48]), "v")
                v.wait_ge(b.sem["t"], c0["t"] + n_t_loop + 1)
                v.tensor_copy(RB1, b.psL1[:])
                b.inc(v.tensor_scalar(RB2, b.psL2[:], b.sbF[:, 4:5], 1.0, OP.mult, OP.mult), "v")
                T1 = b.sbS[:, 0:512]
                T2 = b.sbS[:, 512:1024]
                v.tensor_tensor(T1, b.psU1[:], RB1, OP.mult)
                v.tensor_tensor(T2, b.psU2[:], RB2, OP.mult)
                b.inc(v.tensor_tensor(Oap, T1, T2, OP.subtract), "v")
                b.inc(v.tensor_tensor(OSQ, Oap, Oap, OP.mult), "v")
                v.wait_ge(b.sem["t"], c0["t"] + n_t_loop + 3)
                b.inc(v.tensor_tensor(Oap, Oap, b.psU2[:], OP.mult), "v")
        self.check(c0, t=n_t_loop + 3, a=n_ch + 2, v=n_ch + 4)

    def phase_outproj(self):
        b = self
        nc = self.nc
        X = b.sbX[:].rearrange("p (c t) -> p c t", c=NCH)
        ring = [b.sbS[:, 0:2048].rearrange("p (c n) -> p c n", c=NCH),
                b.sbS[:, 2048:4096].rearrange("p (c n) -> p c n", c=NCH)]
        sbv_f32 = b.sbV[:].bitcast(F32)
        xres = [sbv_f32[:, 0:1024], sbv_f32[:, 1024:2048]]
        x2strip = [sbv_f32[:, 2048:3072], sbv_f32[:, 3072:4096]]
        x2b = [b.sbV[:, 8192:9216], b.sbV[:, 9216:10240]]
        x2sq = [b.sbV[:, 10240:11264], b.sbV[:, 11264:12288]]
        sx = [b.psU1[0:1, :], b.psU2[0:1, :]]
        sxx = [b.psL1[0:1, :], b.psL2[0:1, :]]
        c0 = self.snap()
        with nc.Block() as blk:
            def y_done(j):
                return 2 if j == 0 else (4 if j == 1 else 3 * j)

            def st_done(j):
                return 3 * j + 7 if j <= 13 else (47 if j == 14 else 48)

            @blk.gpsimd
            def _(g):
                for fo in range(16):
                    if fo >= 2:
                        g.wait_ge(b.sem["t"], c0["t"] + y_done(fo - 2))
                    b.inc(g.dma_start(ring[fo % 2][:],
                                      b.wo.rearrange("(c p) n -> p c n", p=128)[:, :, fo * 128:(fo + 1) * 128]),
                          "g", 16)

            @blk.sync
            def _(s):
                for fo in range(16):
                    if fo >= 2:
                        s.wait_ge(b.sem["v"], c0["v"] + (fo - 2) * 3 + 1)
                    b.inc(s.dma_start(xres[fo % 2], b.xfm[fo * 128:(fo + 1) * 128, 0:1024]), "s", 16)

            def stats_group(t, j):
                t.wait_ge(b.sem["v"], c0["v"] + j * 3 + 3)
                t.matmul(sx[0], b.ones128[:], x2b[j % 2][:, 0:512], start=(j == 0), stop=(j == 15))
                t.matmul(sx[1], b.ones128[:], x2b[j % 2][:, 512:1024], start=(j == 0), stop=(j == 15))
                t.matmul(sxx[0], b.ones128[:], x2sq[j % 2][:, 0:512], start=(j == 0), stop=(j == 15))
                ins = t.matmul(sxx[1], b.ones128[:], x2sq[j % 2][:, 512:1024], start=(j == 0), stop=(j == 15))
                b.inc(ins, "t")

            @blk.tensor
            def _(t):
                for fo in range(16):
                    t.wait_ge(b.sem["g"], c0["g"] + (fo + 1) * 16)
                    if fo >= 2:
                        t.wait_ge(b.sem["v"], c0["v"] + (fo - 2) * 3 + 1)
                    PS = [b.psA, b.psB][fo % 2]
                    for th in range(2):
                        for pair in range(PAIRS):
                            ins = t.matmul(PS[:, th * 512:(th + 1) * 512], ring[fo % 2][:, pair, :],
                                           X[:, pair, th * 512:(th + 1) * 512],
                                           start=(pair == 0), stop=(pair == PAIRS - 1))
                        b.inc(ins, "t")
                    if fo >= 2:
                        stats_group(t, fo - 2)
                stats_group(t, 14)
                stats_group(t, 15)

            @blk.vector
            def _(v):
                for fo in range(16):
                    PS = [b.psA, b.psB][fo % 2]
                    v.wait_ge(b.sem["t"], c0["t"] + y_done(fo))
                    v.wait_ge(b.sem["s"], c0["s"] + (fo + 1) * 16)
                    if fo >= 2:
                        v.wait_ge(b.sem["c"], c0["c"] + (fo - 2 + 1) * 16)
                        v.wait_ge(b.sem["t"], c0["t"] + st_done(fo - 2))
                    v.tensor_tensor(x2strip[fo % 2], PS[:], xres[fo % 2], OP.add)
                    b.inc(v.tensor_copy(x2b[fo % 2], x2strip[fo % 2]), "v")
                    b.inc(v.tensor_copy(b.sbF[0:1, 47:48], b.sbF[0:1, 47:48]), "v")
                    b.inc(v.tensor_tensor(x2sq[fo % 2], x2b[fo % 2], x2b[fo % 2], OP.mult), "v")

            @blk.scalar
            def _(sc):
                for fo in range(16):
                    sc.wait_ge(b.sem["v"], c0["v"] + fo * 3 + 2)
                    b.inc(sc.dma_start(b.x2scr[fo * 128:(fo + 1) * 128, :], x2strip[fo % 2]), "c", 16)
                sc.wait_ge(b.sem["c"], c0["c"] + 256)
        self.check(c0, t=48, v=48, c=256, g=256, s=256)
        self.barrier()

    def phase_ln2(self):
        b = self
        nc = self.nc
        XH2 = b.sbW[:].rearrange("p (c t) -> p c t", c=NCH)[:, :, 0:1024]
        XL2 = b.sbW[:].rearrange("p (c t) -> p c t", c=NCH)[:, :, 1024:2048]
        sx = [b.psU1[0:1, :], b.psU2[0:1, :]]
        sxx = [b.psL1[0:1, :], b.psL2[0:1, :]]
        m = b.sbS[0:1, 4096:5120]
        var = b.sbS[0:1, 5120:6144]
        sbv_f32 = b.sbV[:].bitcast(F32)
        strips = [sbv_f32[:, 0:1024], sbv_f32[:, 1024:2048], sbv_f32[:, 2048:3072]]
        mbf = sbv_f32[:, 3072:4096]
        rbf = sbv_f32[:, 4096:5120]
        hifs = [b.sbS[:, 0:2048].bitcast(F32), b.sbS[:, 2048:4096].bitcast(F32)]
        c0 = self.snap()
        with nc.Block() as blk:
            @blk.vector
            def _(v):
                for j in range(2):
                    v.tensor_scalar(m[:, 512 * j:512 * (j + 1)], sx[j], 1.0 / E, None, OP.mult)
                    v.tensor_scalar(var[:, 512 * j:512 * (j + 1)], sxx[j], 1.0 / E, None, OP.mult)
                msq = b.sbS[0:1, 2048:3072]
                v.tensor_tensor(msq, m, m, OP.mult)
                b.inc(v.tensor_tensor(var, var, msq, OP.subtract), "v")

            @blk.scalar
            def _(sc):
                sc.wait_ge(b.sem["v"], c0["v"] + 1)
                sc.activation(var, var, ACT.Ln, bias=b.sbF[0:1, 5:6], scale=1.0)
                b.inc(sc.activation(var, var, ACT.Exp, scale=-0.5), "a")

            @blk.tensor
            def _(t):
                t.wait_ge(b.sem["a"], c0["a"] + 1)
                t.matmul(b.psA[:, 0:512], b.ones1x128[:], m[:, 0:512], start=True, stop=True)
                t.matmul(b.psA[:, 512:1024], b.ones1x128[:], m[:, 512:1024], start=True, stop=True)
                t.matmul(b.psB[:, 0:512], b.ones1x128[:], var[:, 0:512], start=True, stop=True)
                b.inc(t.matmul(b.psB[:, 512:1024], b.ones1x128[:], var[:, 512:1024], start=True, stop=True), "t")

            @blk.vector
            def _(v):
                v.wait_ge(b.sem["t"], c0["t"] + 1)
                v.tensor_copy(mbf, b.psA[:, 0:1024])
                b.inc(v.tensor_copy(rbf, b.psB[:, 0:1024]), "v")

            @blk.sync
            def _(s):
                for fo in range(16):
                    if fo >= 3:
                        s.wait_ge(b.sem["v"], c0["v"] + 2 * fo - 1)
                        s.wait_ge(b.sem["a"], c0["a"] + (fo - 3) + 2)
                    b.inc(s.dma_start(strips[fo % 3], b.x2scr[fo * 128:(fo + 1) * 128, :]), "s", 16)

            @blk.scalar
            def _(sc):
                for fo in range(16):
                    sc.wait_ge(b.sem["v"], c0["v"] + (3 if fo == 0 else 2 * fo + 2))
                    sc.activation(XH2[:, fo, :], strips[fo % 3], ACT.Copy)
                    b.inc(sc.activation(hifs[fo % 2], XH2[:, fo, :], ACT.Copy), "a")

            @blk.vector
            def _(v):
                for fo in range(16):
                    v.wait_ge(b.sem["s"], c0["s"] + (fo + 1) * 16)
                    st = strips[fo % 3]
                    v.tensor_tensor(st, st, mbf, OP.subtract)
                    b.inc(v.tensor_tensor(st, st, rbf, OP.mult), "v")
                    if fo >= 1:
                        v.wait_ge(b.sem["a"], c0["a"] + (fo - 1) + 2)
                        b.inc(v.tensor_tensor(XL2[:, fo - 1, :], strips[(fo - 1) % 3],
                                              hifs[(fo - 1) % 2], OP.subtract), "v")
                v.wait_ge(b.sem["a"], c0["a"] + 15 + 2)
                b.inc(v.tensor_tensor(XL2[:, 15, :], strips[15 % 3], hifs[15 % 2], OP.subtract), "v")
        self.check(c0, v=2 + 32, t=1, a=1 + 16, s=256)
        self.barrier()

    def phase_router(self):
        b = self
        nc = self.nc
        XH2 = b.sbW[:].rearrange("p (c t) -> p c t", c=NCH)[:, :, 0:1024]
        XL2 = b.sbW[:].rearrange("p (c t) -> p c t", c=NCH)[:, :, 1024:2048]
        WRT = b.sbS[:, 2048:2048 + 16 * 8].rearrange("p (c n) -> p c n", c=NCH)
        WRTL = b.sbS[:, 2176:2176 + 16 * 8].rearrange("p (c n) -> p c n", c=NCH)
        WRTF = b.sbS[:, 5120:5376].bitcast(F32)
        WRTHF = b.sbS[:, 5376:5632].bitcast(F32)
        IDN = b.sbS[:, 4480:4608]
        CFM_OFF = 2432
        ZA = b.sbS[:, 4608:4736].bitcast(F32)
        GA = b.sbS[:, 4736:4864].bitcast(F32)
        Z2A = b.sbS[:, 4864:4992].bitcast(F32)
        CMA = b.sbS[:, 4992:5120].bitcast(F32)
        NMX = b.sbF[:, 8:16]
        V1 = b.sbF[:, 16:24]
        V2 = b.sbF[:, 24:32]
        S12 = b.sbF[:, 32:40]
        ZC = b.sbF[:, 44:45]
        OC = b.sbF[:, 45:46]
        ZTM = b.sbS[:, 2304:2368]
        crep = b.sbV[:, 12288:12288 + 8 * 1024].rearrange("p (e t) -> p e t", e=8)
        c0 = self.snap()

        def zs(tc):
            return ZA[:, tc * 8:(tc + 1) * 8]

        with nc.Block() as blk:
            @blk.gpsimd
            def _(g):
                b.inc(g.dma_start(WRTF[:, 0:128], b.wrt), "g", 16)
                b.inc(g.dma_start(b.sbS[0:8, 3456:4480], b.conehot), "g", 16)
                b.inc(g.dma_start(IDN, b.cident), "g", 16)

            @blk.tensor
            def _(t):
                t.wait_ge(b.sem["g"], c0["g"] + 48)
                t.wait_ge(b.sem["v"], c0["v"] + 1)
                for tc in range(8):
                    if tc >= 2:
                        t.wait_ge(b.sem["v"], c0["v"] + 1 + (tc - 2) + 1)
                    PL = [b.psL1, b.psL2][tc % 2]
                    for f in range(NCH):
                        t.matmul(PL[:, 0:8], XH2[:, f, tc * 128:(tc + 1) * 128],
                                 WRT[:, f, :], start=(f == 0), stop=False)
                        t.matmul(PL[:, 0:8], XL2[:, f, tc * 128:(tc + 1) * 128],
                                 WRT[:, f, :], start=False, stop=False)
                        ins = t.matmul(PL[:, 0:8], XH2[:, f, tc * 128:(tc + 1) * 128],
                                       WRTL[:, f, :], start=False,
                                       stop=(f == NCH - 1))
                    b.inc(ins, "t")
                t.wait_ge(b.sem["v"], c0["v"] + 11)
                for tc in range(8):
                    if tc >= 2:
                        t.wait_ge(b.sem["v"], c0["v"] + 11 + (tc - 2) + 1)
                    ins = t.matmul([b.psL1, b.psL2][tc % 2][0:8, 0:128],
                                   ZTM[:, tc * 8:(tc + 1) * 8], IDN, start=True, stop=True)
                    b.inc(ins, "t")
                t.wait_ge(b.sem["v"], c0["v"] + 19)
                for e in range(8):
                    if e >= 2:
                        t.wait_ge(b.sem["v"], c0["v"] + 19 + (e - 2) + 1)
                    PR = [b.psA, b.psB][e % 2]
                    t.matmul(PR[:, 0:512], b.sbS[0:8, 3456 + e * 128:3456 + (e + 1) * 128],
                             b.sbS[0:8, CFM_OFF:CFM_OFF + 512], start=True, stop=True)
                    ins = t.matmul(PR[:, 512:1024], b.sbS[0:8, 3456 + e * 128:3456 + (e + 1) * 128],
                                   b.sbS[0:8, CFM_OFF + 512:CFM_OFF + 1024], start=True, stop=True)
                    b.inc(ins, "t")

            @blk.scalar
            def _(sc):
                sc.wait_ge(b.sem["v"], c0["v"] + 10)
                for tc in range(8):
                    ins = sc.activation(zs(tc), zs(tc), ACT.Exp, bias=NMX[:, tc:tc + 1], scale=1.0)
                b.inc(ins, "a")

            @blk.vector
            def _(v):
                v.wait_ge(b.sem["g"], c0["g"] + 16)
                v.tensor_copy(b.sbS[:, 2048:2176], WRTF)
                v.memset(ZC, 0.0)
                v.tensor_copy(WRTHF, b.sbS[:, 2048:2176])
                v.memset(OC, 1.0)
                b.inc(v.tensor_tensor(b.sbS[:, 2176:2304], WRTF, WRTHF, OP.subtract), "v")
                for tc in range(8):
                    v.wait_ge(b.sem["t"], c0["t"] + tc + 1)
                    PL = [b.psL1, b.psL2][tc % 2]
                    b.inc(v.tensor_copy(zs(tc), PL[:, 0:8]), "v")
                for tc in range(8):
                    ins = v.reduce_max(NMX[:, tc:tc + 1], zs(tc), axis=AX.X, negate=True)
                b.inc(ins, "v")
                v.wait_ge(b.sem["a"], c0["a"] + 1)
                for tc in range(8):
                    v.reduce_max(V1[:, tc:tc + 1], zs(tc), axis=AX.X)
                for tc in range(8):
                    v.tensor_scalar(GA[:, tc * 8:(tc + 1) * 8], zs(tc), V1[:, tc:tc + 1], ZC, OP.subtract, OP.add)
                for tc in range(8):
                    v.tensor_scalar(GA[:, tc * 8:(tc + 1) * 8], GA[:, tc * 8:(tc + 1) * 8], 0.0, None, OP.is_ge)
                for tc in range(8):
                    v.scalar_tensor_tensor(Z2A[:, tc * 8:(tc + 1) * 8], GA[:, tc * 8:(tc + 1) * 8], -10.0,
                                           zs(tc), OP.mult, OP.add)
                for tc in range(8):
                    v.reduce_max(V2[:, tc:tc + 1], Z2A[:, tc * 8:(tc + 1) * 8], axis=AX.X)
                for tc in range(8):
                    v.tensor_scalar(S12[:, tc:tc + 1], V1[:, tc:tc + 1], V2[:, tc:tc + 1], ZC, OP.add, OP.add)
                for tc in range(8):
                    v.tensor_scalar(GA[:, tc * 8:(tc + 1) * 8], zs(tc), V2[:, tc:tc + 1], ZC, OP.subtract, OP.add)
                for tc in range(8):
                    v.tensor_scalar(GA[:, tc * 8:(tc + 1) * 8], GA[:, tc * 8:(tc + 1) * 8], 0.0, None, OP.is_ge)
                v.reciprocal(S12, S12)
                for tc in range(8):
                    v.tensor_tensor(CMA[:, tc * 8:(tc + 1) * 8], zs(tc), GA[:, tc * 8:(tc + 1) * 8], OP.mult)
                for tc in range(8):
                    v.tensor_scalar(CMA[:, tc * 8:(tc + 1) * 8], CMA[:, tc * 8:(tc + 1) * 8],
                                    S12[:, tc:tc + 1], OC, OP.mult, OP.mult)
                b.inc(v.tensor_copy(ZTM, CMA), "v")
                for tc in range(8):
                    v.wait_ge(b.sem["t"], c0["t"] + 8 + tc + 1)
                    b.inc(v.tensor_copy(b.sbS[0:8, CFM_OFF + tc * 128:CFM_OFF + (tc + 1) * 128],
                                        [b.psL1, b.psL2][tc % 2][0:8, 0:128]), "v")
                for e in range(8):
                    PR = [b.psA, b.psB][e % 2]
                    v.wait_ge(b.sem["t"], c0["t"] + 16 + e + 1)
                    v.tensor_copy(crep[:, e, 0:512], PR[:, 0:512])
                    b.inc(v.tensor_copy(crep[:, e, 512:1024], PR[:, 512:1024]), "v")
        self.check(c0, t=24, a=1, v=27, g=48)
        self.barrier()

    def phase_ffn(self):
        b = self
        nc = self.nc
        XH2 = b.sbW[:].rearrange("p (c t) -> p c t", c=NCH)[:, :, 0:1024]
        y2 = b.sbX[:].bitcast(F32).rearrange("p (c t) -> p c t", c=NCH)[:, :, 0:1024]
        h = b.sbV[:, 21504:32768].rearrange("p (c t) -> p c t", c=ICH)
        crep = b.sbV[:, 12288:12288 + 8 * 1024].rearrange("p (e t) -> p e t", e=8)
        w1r = [b.sbV[:, 0:2048].rearrange("p (c n) -> p c n", c=NCH),
               b.sbV[:, 2048:4096].rearrange("p (c n) -> p c n", c=NCH)]
        w2r = [b.sbV[:, 4096:4096 + ICH * 128].rearrange("p (c n) -> p c n", c=ICH),
               b.sbV[:, 5504:5504 + ICH * 128].rearrange("p (c n) -> p c n", c=ICH)]
        hps = [b.psU1, b.psU2, b.psL1, b.psL2]
        yps = [b.psA, b.psB]
        ee = list(range(8)) + [8]

        def w1_dram(e, i):
            src = b.ws1 if e == 8 else b.w1[e]
            return src.rearrange("(c p) n -> p c n", p=128)[:, :, i * 128:(i + 1) * 128]

        def w2_dram(e, fo):
            src = b.ws2 if e == 8 else b.w2[e]
            return src.rearrange("(c p) n -> p c n", p=128)[:, :, fo * 128:(fo + 1) * 128]

        TPE = 2 * ICH + 16
        def vec_before(ei):
            return sum((ICH if ee[k] < 8 else 0) + 16 for k in range(ei))
        c0 = self.snap()

        with nc.Block() as blk:
            @blk.gpsimd
            def _(g):
                nw1 = 0
                nw2 = 0
                for ei in range(9):
                    for i in range(ICH):
                        if nw1 >= 2:
                            pei, pi = divmod(nw1 - 2, ICH)
                            g.wait_ge(b.sem["t"], c0["t"] + pei * TPE + (pi + 1) * 2)
                        b.inc(g.dma_start(w1r[nw1 % 2][:], w1_dram(ee[ei], i)), "g", 16)
                        nw1 += 1
                    for fo in range(16):
                        if nw2 >= 2:
                            pei, pfo = divmod(nw2 - 2, 16)
                            g.wait_ge(b.sem["t"], c0["t"] + pei * TPE + 2 * ICH + pfo + 1)
                        b.inc(g.dma_start(w2r[nw2 % 2][:], w2_dram(ee[ei], fo)), "g", 16)
                        nw2 += 1

            @blk.tensor
            def _(t):
                for ei in range(9):
                    e = ee[ei]
                    for i in range(ICH):
                        t.wait_ge(b.sem["g"], c0["g"] + (ei * (ICH + 16) + i + 1) * 16)
                        for th in range(2):
                            n = (ei * ICH + i) * 2 + th
                            if n >= 4:
                                t.wait_ge(b.sem["a"], c0["a"] + (n - 4) + 1)
                            for f in range(NCH):
                                ins = t.matmul(hps[n % 4][:], w1r[(ei * ICH + i) % 2][:, f, :],
                                               XH2[:, f, th * 512:(th + 1) * 512],
                                               start=(f == 0), stop=(f == NCH - 1))
                            b.inc(ins, "t")
                    for fo in range(16):
                        t.wait_ge(b.sem["g"], c0["g"] + (ei * (ICH + 16) + ICH + fo + 1) * 16)
                        t.wait_ge(b.sem["a"], c0["a"] + (ei + 1) * 2 * ICH)
                        if e < 8:
                            t.wait_ge(b.sem["v"], c0["v"] + vec_before(ei) + ICH)
                        ny = ei * 16 + fo
                        if ny >= 2:
                            pei, pfo = divmod(ny - 2, 16)
                            t.wait_ge(b.sem["v"], c0["v"] + vec_before(pei) +
                                      (ICH if ee[pei] < 8 else 0) + pfo + 1)
                        PS = yps[ny % 2]
                        for th in range(2):
                            for i in range(ICH):
                                ins = t.matmul(PS[:, th * 512:(th + 1) * 512],
                                               w2r[ny % 2][:, i, :],
                                               h[:, i, th * 512:(th + 1) * 512],
                                               start=(i == 0), stop=(i == ICH - 1))
                        b.inc(ins, "t")

            @blk.scalar
            def _(sc):
                for ei in range(9):
                    for i in range(ICH):
                        for th in range(2):
                            n = (ei * ICH + i) * 2 + th
                            sc.wait_ge(b.sem["t"], c0["t"] + ei * TPE + i * 2 + th + 1)
                            b.inc(sc.activation(h[:, i, th * 512:(th + 1) * 512],
                                                hps[n % 4][:], ACT.Silu), "a")

            @blk.vector
            def _(v):
                for ei in range(9):
                    e = ee[ei]
                    if e < 8:
                        for i in range(ICH):
                            v.wait_ge(b.sem["a"], c0["a"] + ei * 2 * ICH + (i + 1) * 2)
                            b.inc(v.tensor_tensor(h[:, i, :], h[:, i, :], crep[:, e, :], OP.mult), "v")
                    for fo in range(16):
                        v.wait_ge(b.sem["t"], c0["t"] + ei * TPE + 2 * ICH + fo + 1)
                        PS = yps[(ei * 16 + fo) % 2]
                        if ei == 0:
                            ins = v.tensor_copy(y2[:, fo, :], PS[:])
                        else:
                            ins = v.tensor_tensor(y2[:, fo, :], y2[:, fo, :], PS[:], OP.add)
                        b.inc(ins, "v")
        self.check(c0, t=9 * TPE, a=9 * 2 * ICH, v=vec_before(9), g=16 * 9 * (ICH + 16))
        self.barrier()

    def phase_final(self):
        b = self
        nc = self.nc
        y2 = b.sbX[:].bitcast(F32).rearrange("p (c t) -> p c t", c=NCH)[:, :, 0:1024]
        sbv_f32 = b.sbV[:].bitcast(F32)
        xres = [sbv_f32[:, 0:1024], sbv_f32[:, 1024:2048]]
        c0 = self.snap()
        with nc.Block() as blk:
            @blk.sync
            def _(s):
                for fo in range(16):
                    if fo >= 2:
                        s.wait_ge(b.sem["v"], c0["v"] + (fo - 2) + 1)
                        s.wait_ge(b.sem["c"], c0["c"] + (fo - 2 + 1) * 16)
                    b.inc(s.dma_start(xres[fo % 2], b.x2scr[fo * 128:(fo + 1) * 128, :]), "s", 16)

            @blk.vector
            def _(v):
                for fo in range(16):
                    v.wait_ge(b.sem["s"], c0["s"] + (fo + 1) * 16)
                    b.inc(v.tensor_tensor(xres[fo % 2], y2[:, fo, :], xres[fo % 2], OP.add), "v")

            @blk.scalar
            def _(sc):
                for fo in range(16):
                    sc.wait_ge(b.sem["v"], c0["v"] + fo + 1)
                    b.inc(sc.dma_start(b.out[fo * 128:(fo + 1) * 128, :], xres[fo % 2]), "c", 16)
                sc.wait_ge(b.sem["c"], c0["c"] + 256)
        self.check(c0, s=256, v=16, c=256)


def make_host_inputs(inputs, core):
    x = np.asarray(inputs["L0_attn_x"], np.float32)
    perm, own, other = perm_for(core)
    xt = np.ascontiguousarray(x[core // 2].T).reshape(E, 16, 128)
    xfm = np.ascontiguousarray(xt[:, perm, :].reshape(E, T)).astype(np.float32)
    mvals = [sum(1 for c in other if c <= own[qi]) for qi in range(8)]
    mrep = np.zeros((2, 128, 512), np.float32)
    for s in range(2):
        for qsub in range(4):
            mrep[s, :, qsub * 128:(qsub + 1) * 128] = mvals[4 * s + qsub]
    tri = (np.arange(128)[:, None] <= np.arange(128)[None, :]).astype(np.float32)
    onehot = np.zeros((8, 8 * 128), np.float32)
    for e in range(8):
        onehot[e, e * 128:(e + 1) * 128] = 1.0
    return dict(
        xfm=xfm,
        wq=np.ascontiguousarray(inputs["L0_attn_Wq"], np.float32),
        wk=np.ascontiguousarray(inputs["L0_attn_Wk"], np.float32),
        wv=np.ascontiguousarray(inputs["L0_attn_Wv"], np.float32),
        wo=np.ascontiguousarray(inputs["L0_attn_Wo"], np.float32),
        ws1=np.ascontiguousarray(inputs["L0_ffn_w_shared1"], np.float32),
        ws2=np.ascontiguousarray(inputs["L0_ffn_w_shared2"], np.float32),
        w1=np.ascontiguousarray(inputs["L0_ffn_w1"], np.float32),
        w2=np.ascontiguousarray(inputs["L0_ffn_w2"], np.float32),
        wrt=np.ascontiguousarray(
            np.asarray(inputs["L0_ffn_w_router"], np.float32).reshape(16, 128, 8)
            .transpose(1, 0, 2).reshape(128, 128)),
        lq=np.concatenate([np.asarray(inputs["L0_attn_lq1"], np.float32),
                           np.asarray(inputs["L0_attn_lq2"], np.float32)])[None, :],
        lk=np.concatenate([np.asarray(inputs["L0_attn_lk1"], np.float32),
                           np.asarray(inputs["L0_attn_lk2"], np.float32)])[None, :],
        cones128=np.ones((128, 1), np.float32),
        cones1x128=np.ones((1, 128), np.float32),
        cones1x128f=np.ones((1, 128), np.float32),
        crms1x128=np.full((1, 128), 1.0 - LAMBDA_INIT, np.float32),
        ctri=tri,
        conehot=onehot,
        cident=np.eye(128, dtype=np.float32),
        cmrep=mrep,
    )


def assemble_output(results):
    full = np.zeros((4, 2048, 2048), np.float32)
    for core in range(8):
        perm, own, other = perm_for(core)
        o = np.asarray(results[core]["out"])
        for j, c in enumerate(own):
            full[core // 2, c * 128:(c + 1) * 128, :] = o[:, j * 128:(j + 1) * 128].T
    return full


def kernel(**inputs):
    bld = Builder()
    nc = bld.build()
    in_maps = [make_host_inputs(inputs, core) for core in range(8)]
    res = run_bass_kernel_spmd(nc, in_maps, core_ids=list(range(8)))
    return assemble_output(res.results)
